# version 01
# baseline (speedup 1.0000x reference)
import contextlib
import math

import numpy as np
import ml_dtypes

import concourse.bass as bass
import concourse.mybir as mybir
from concourse.bass_utils import run_bass_kernel_spmd

F32 = mybir.dt.float32
BF16 = mybir.dt.bfloat16
FP8 = mybir.dt.float8e4
I32 = mybir.dt.int32
AF = mybir.ActivationFunctionType
ALU = mybir.AluOpType
AXX = mybir.AxisListType.X

D = 2048
S = 2048
TOK = 1024
NCH = 16
HEADS = 8
LAMBDA_INIT = 0.8 - 0.6 * math.exp(-0.3 * 12)
EPS_LN = 1e-5
EPS_RMS = 1e-8
SCALE_A = 64.0 ** -0.5
SCALE_B = 128.0 ** -0.5
N_SUB = 4
TAPS = ()

_ENGS = ("te", "dv", "ac", "gp", "sy")


class _B:

    def __init__(self, nc):
        self.nc = nc
        self.q = {e: [] for e in _ENGS}
        self.cnt = {}
        self.sems = {}
        self.stack = contextlib.ExitStack()
        self._lastwait = {e: {} for e in _ENGS}
        self.guard = {}

    def sem(self, name):
        self.sems[name] = self.stack.enter_context(self.nc.semaphore(name))
        self.cnt[name] = 0

    def wait(self, eng, sem, val=None):
        v = self.cnt[sem] if val is None else val
        if v <= 0:
            return
        if self._lastwait[eng].get(sem, -1) >= v:
            return
        self._lastwait[eng][sem] = v
        h = self.sems[sem]
        self.q[eng].append(lambda e, h=h, v=v: e.wait_ge(h, v))

    def op(self, eng, fn, inc=None, n=1):
        if inc is None:
            self.q[eng].append(fn)
        else:
            h = self.sems[inc]
            self.q[eng].append(lambda e, fn=fn, h=h, n=n: fn(e).then_inc(h, n))
            self.cnt[inc] += n

    def dma(self, eng, out, in_, inc):
        self.op(eng, lambda e, o=out, i=in_: e.dma_start(out=o, in_=i),
                inc=inc, n=16)

    def now(self, sem):
        return self.cnt[sem]

    def need(self, eng, key):
        g = self.guard.get(key)
        if g:
            self.wait(eng, g[0], g[1])

    def setg(self, key, sem):
        self.guard[key] = (sem, self.cnt[sem])

    def dvf(self, fn):
        self.op("dv", fn, inc="dv")
        self.wait("dv", "dv")


def build_nc(n_sub=N_SUB):
    nc = bass.Bass("TRN2", target_bir_lowering=False)
    b = _B(nc)
    st = b.stack
    sb = lambda name, shape, dt: st.enter_context(
        nc.sbuf_tensor(name, list(shape), dt))

    def din(name, shape, dt):
        return nc.dram_tensor(name, list(shape), dt, kind="ExternalInput")

    x_in = din("x_in", [D, TOK], F32)
    wq = din("wq", [D, TOK], F32)
    wk = din("wk", [D, TOK], F32)
    wv = din("wv", [D, TOK], F32)
    wo = din("wo", [TOK, D], F32)
    wbk = din("wbk", [D, TOK], F32)
    wbq = din("wbq", [D, TOK], F32)
    wbv = din("wbv", [D, TOK], F32)
    wbo = din("wbo", [TOK, D], F32)
    w8t = din("w8t", [D, D], FP8)
    cos_in = din("cos_in", [128, S], BF16)
    sin_in = din("sin_in", [128, S], BF16)
    rot_in = din("rot_in", [128, 128], BF16)
    tri_in = din("tri_in", [128, 128], BF16)
    eye_in = din("eye_in", [128, 128], BF16)
    lam_in = din("lam_in", [1, 256], F32)
    lnw_in = din("lnw_in", [128, 1], F32)
    lnb_in = din("lnb_in", [128, 1], F32)
    wsc_in = din("wsc_in", [1, 1], F32)
    isc_in = din("isc_in", [1, 1], F32)
    offs_in = din("offs_in", [1, 4], I32)

    out_fm = nc.dram_tensor("out_fm", [D, TOK], F32, kind="ExternalOutput")

    xbufA = nc.dram_tensor("xbufA", [D, TOK], F32)
    xbufB = nc.dram_tensor("xbufB", [D, TOK], F32)
    agi1 = {a: nc.dram_tensor(f"ag1{a}_in", [D, TOK], BF16) for a in "ab"}
    ago1 = {a: nc.dram_tensor(f"ag1{a}_out", [8 * D, TOK], BF16) for a in "ab"}
    agi2 = {a: nc.dram_tensor(f"ag2{a}_in", [2 * D, TOK], BF16) for a in "ab"}
    ago2 = {a: nc.dram_tensor(f"ag2{a}_out", [16 * D, TOK], BF16) for a in "ab"}
    xnloc = {a: nc.dram_tensor(f"xnloc{a}", [2 * D, TOK], BF16) for a in "ab"}
    parloc = {a: nc.dram_tensor(f"parloc{a}", [2 * D, TOK], BF16) for a in "ab"}

    xn_full = sb("xn_full", [128, NCH * S], BF16)
    scr32 = sb("scr32", [128, NCH * TOK], BF16)
    ab16 = sb("ab16", [128, 8192], BF16)
    k4 = sb("k4", [128, 8192], BF16)
    v4 = sb("v4", [128, 8192], BF16)
    cs16 = sb("cs16", [128, 4096], BF16)
    wpool = sb("wpool", [128, 4096], BF16)
    w8p = sb("w8p", [128, 4096], FP8)
    xch = [sb(f"xch{i}", [128, TOK], F32) for i in (0, 1)]
    sq1 = sb("sq1", [128, TOK], BF16)
    ptile = [sb(f"ptile{i}", [128, 512], BF16) for i in (0, 1)]
    qraw = sb("qraw", [128, 512], BF16)
    vtmp = sb("vtmp", [128, 512], BF16)
    f8t = sb("f8t", [128, 512], FP8)
    vtmp2 = sb("vtmp2", [128, 512], F32)
    zbt = sb("zbt", [128, 1024], BF16)
    w1t = sb("w1t", [128, 512], F32)
    rows = sb("rows", [1, 2560], F32)
    rowsb = sb("rowsb", [1, 2048], BF16)
    maskt = sb("maskt", [128, 128], BF16)
    rotmb = sb("rotmb", [128, 128], BF16)
    eyeb = sb("eyeb", [128, 128], BF16)
    ones_c = sb("ones_c", [128, 1], BF16)
    ones_r = sb("ones_r", [1, 128], BF16)
    lamv = sb("lamv", [1, 256], F32)
    lnw = sb("lnw", [128, 1], F32)
    lnb = sb("lnb", [128, 1], F32)
    lnw_eff = sb("lnw_eff", [128, 1], F32)
    lnb_eff = sb("lnb_eff", [128, 1], F32)
    sca = sb("sca", [1, 8], F32)
    onef = sb("onef", [1, 128], F32)
    alpha_c = sb("alpha_c", [128, 1], F32)
    lamn_c = sb("lamn_c", [128, 1], F32)
    offs_sb = sb("offs_sb", [1, 4], I32)

    mb_t = ab16[:, 0:1024]
    rb_t = ab16[:, 1024:2048]
    q4 = ab16
    par_sl = [ab16[:, 0:2048], ab16[:, 2048:4096]]
    cosb = cs16[:, 0:2048]
    sinb = cs16[:, 2048:4096]
    po_sl = [cs16[:, 0:2048], cs16[:, 2048:4096]]

    psF = [st.enter_context(nc.psum_tensor(f"ps{i}", [128, 512], F32))
           for i in range(6)]
    psT = [st.enter_context(nc.psum_tensor(f"tb{i}", [128, 1024], BF16))
           for i in range(2)]

    for s_name in ("sy", "gp", "te", "dv", "ac",
                   "cca1", "cca2", "ccb1", "ccb2"):
        b.sem(s_name)

    tap_t = {}

    def tap(point, buf, cols, waits=("dv", "ac", "te")):
        if point not in TAPS:
            return
        t = nc.dram_tensor(f"tap_{point}", [128, cols], BF16,
                           kind="ExternalOutput")
        tap_t[point] = t
        for w in waits:
            b.wait("sy", w, b.now(w))
        b.dma("sy", t.ap(), buf[:, 0:cols], "sy")

    def tapf(point, buf, cols, waits=("dv", "ac", "te")):
        if point not in TAPS:
            return
        t = nc.dram_tensor(f"tap_{point}", [128, cols], F32,
                           kind="ExternalOutput")
        tap_t[point] = t
        for w in waits:
            b.wait("sy", w, b.now(w))
        b.dma("sy", t.ap(), buf[:, 0:cols], "sy")

    r_xnb = st.enter_context(nc.gpsimd.register("r_xnb"))
    r_parb = st.enter_context(nc.gpsimd.register("r_parb"))
    r_tmp = st.enter_context(nc.gpsimd.register("r_tmp"))

    for src, dst in ((cos_in, cosb), (sin_in, sinb), (rot_in, rotmb[:, :]),
                     (tri_in, maskt[:, :]), (eye_in, eyeb[:, :]),
                     (lam_in, lamv[:, :]), (lnw_in, lnw[:, :]),
                     (lnb_in, lnb[:, :]), (offs_in, offs_sb[:, :])):
        b.dma("sy", dst, src.ap(), "sy")
    b.dma("sy", sca[0:1, 0:1], wsc_in.ap(), "sy")
    b.dma("sy", sca[0:1, 1:2], isc_in.ap(), "sy")

    b.wait("dv", "sy", b.now("sy"))
    b.op("dv", lambda e: e.memset(ones_c[:, :], 1.0))
    b.op("dv", lambda e: e.memset(sca[0:1, 6:7], EPS_LN))
    b.op("dv", lambda e: e.memset(sca[0:1, 7:8], EPS_RMS))
    b.op("dv", lambda e: e.memset(ones_r[:, :], 1.0))
    b.op("dv", lambda e: e.memset(onef[:, :], 1.0))
    b.op("dv", lambda e: e.tensor_mul(rows[0:1, 0:64], lamv[0:1, 0:64],
                                      lamv[0:1, 64:128]))
    b.dvf(lambda e: e.tensor_mul(rows[0:1, 64:128], lamv[0:1, 128:192],
                                 lamv[0:1, 192:256]))
    b.op("dv", lambda e: e.reduce_sum(rows[0:1, 128:129], rows[0:1, 0:64], axis=AXX))
    b.op("dv", lambda e: e.reduce_sum(rows[0:1, 129:130], rows[0:1, 64:128],
                                      axis=AXX), inc="dv")
    b.wait("ac", "dv", b.now("dv"))
    b.op("ac", lambda e: e.activation(rows[0:1, 130:132], rows[0:1, 128:130],
                                      AF.Exp), inc="ac")
    b.wait("dv", "ac", b.now("ac"))
    b.dvf(lambda e: e.tensor_sub(rows[0:1, 132:133], rows[0:1, 130:131],
                                 rows[0:1, 131:132]))
    b.dvf(lambda e: e.tensor_scalar_add(sca[0:1, 4:5], rows[0:1, 132:133],
                                        LAMBDA_INIT))
    b.dvf(lambda e: e.tensor_scalar_mul(sca[0:1, 5:6], sca[0:1, 4:5], -1.0))
    b.dvf(lambda e: e.reciprocal(sca[0:1, 2:3], sca[0:1, 1:2]))
    b.op("dv", lambda e: e.tensor_mul(sca[0:1, 3:4], sca[0:1, 0:1], sca[0:1, 1:2]),
         inc="dv")
    b.wait("ac", "dv", b.now("dv"))
    b.op("ac", lambda e: e.activation(lnw_eff[:, :], lnw[:, :], AF.Copy,
                                      scale=1.0 - LAMBDA_INIT))
    b.op("ac", lambda e: e.activation(lnb_eff[:, :], lnb[:, :], AF.Copy,
                                      scale=1.0 - LAMBDA_INIT), inc="ac")
    b.wait("te", "dv", b.now("dv"))
    b.op("te", lambda e: e.matmul(psF[4][:, 0:1], onef[0:1, :], sca[0:1, 3:4],
                                  start=True, stop=True))
    b.op("te", lambda e: e.matmul(psF[4][:, 1:2], onef[0:1, :], sca[0:1, 5:6],
                                  start=True, stop=True), inc="te")
    b.wait("ac", "te", b.now("te"))
    b.op("ac", lambda e: e.activation(alpha_c[:, :], psF[4][:, 0:1], AF.Copy))
    b.op("ac", lambda e: e.activation(lamn_c[:, :], psF[4][:, 1:2], AF.Copy),
         inc="ac")
    b.setg("ps4", "ac")

    if "setup" in TAPS:
        tp = nc.dram_tensor("tap_sca", [1, 8], F32, kind="ExternalOutput")
        tap_t["sca"] = tp
        tp2 = nc.dram_tensor("tap_cols", [2, 128], F32, kind="ExternalOutput")
        tap_t["cols"] = tp2
        tp3 = nc.dram_tensor("tap_rows0", [1, 256], F32, kind="ExternalOutput")
        tap_t["rows0"] = tp3
        b.wait("sy", "ac", b.now("ac"))
        b.wait("sy", "dv", b.now("dv"))
        b.dma("sy", tp.ap(), sca[0:1, :], "sy")
        b.dma("sy", tp2.ap()[0:1, :].rearrange("a p -> p a"), alpha_c[:, :], "sy")
        b.dma("sy", tp2.ap()[1:2, :].rearrange("a p -> p a"), lamn_c[:, :], "sy")
        b.dma("sy", tp3.ap(), rows[0:1, 0:256], "sy")
    b.wait("gp", "sy", b.now("sy"))
    b.op("gp", lambda e: e.reg_load(r_xnb, offs_sb[0:1, 0:1]))
    b.op("gp", lambda e: e.reg_load(r_parb, offs_sb[0:1, 1:2]))

    def stats_chunk(i, src_bf):
        b.need("ac", "sq1")
        b.op("ac", lambda e, s=src_bf: e.activation(sq1[:, :], s, AF.Square),
             inc="ac")
        b.wait("te", "ac", b.now("ac"))
        for tc in range(2):
            sl = slice(tc * 512, (tc + 1) * 512)
            if i == 0:
                b.need("te", f"ps{2 + tc}")
                b.need("te", f"ps{4 + tc}")
            b.op("te", lambda e, s=src_bf, sl=sl, tc=tc, i=i:
                 e.matmul(psF[2 + tc][0:1, :], ones_c[:, 0:1], s[:, sl],
                          start=(i == 0), stop=(i == NCH - 1)))
            b.op("te", lambda e, sl=sl, tc=tc, i=i:
                 e.matmul(psF[4 + tc][0:1, :], ones_c[:, 0:1], sq1[:, sl],
                          start=(i == 0), stop=(i == NCH - 1)),
                 inc=("te" if (i == NCH - 1 and tc == 1) else None))
        b.setg("sq1", "te")

    def ln_finish(ffn):
        b.wait("ac", "te", b.now("te"))
        for tc in range(2):
            b.op("ac", lambda e, tc=tc: e.activation(
                rows[0:1, tc * 512:(tc + 1) * 512], psF[2 + tc][0:1, :],
                AF.Copy, scale=1.0 / D))
            b.op("ac", lambda e, tc=tc: e.activation(
                rows[0:1, 1024 + tc * 512:1024 + (tc + 1) * 512],
                psF[4 + tc][0:1, :], AF.Copy, scale=1.0 / D), inc="ac")
        for i in (2, 3, 4, 5):
            b.setg(f"ps{i}", "ac")
        b.wait("dv", "ac", b.now("ac"))
        for tc in range(2):
            m_h = rows[0:1, tc * 512:(tc + 1) * 512]
            e_h = rows[0:1, 1024 + tc * 512:1024 + (tc + 1) * 512]
            t_h = rows[0:1, 2048:2560]
            b.dvf(lambda e, m=m_h, t=t_h: e.tensor_mul(t, m, m))
            b.op("dv", lambda e, eh=e_h, t=t_h: e.tensor_sub(eh, eh, t),
                 inc="dv")
            b.wait("dv", "dv")
        b.wait("ac", "dv", b.now("dv"))
        var_r = rows[0:1, 1024:2048]
        b.op("ac", lambda e: e.activation(var_r, var_r, AF.Ln, bias=sca[0:1, 6:7]))
        b.op("ac", lambda e: e.activation(var_r, var_r, AF.Exp, scale=-0.5),
             inc="ac")
        b.wait("dv", "ac", b.now("ac"))
        if ffn:
            b.dvf(lambda e: e.tensor_scalar(var_r, var_r, sca[0:1, 2:3],
                                            None, ALU.mult))
        b.op("dv", lambda e: e.tensor_copy(rowsb[0:1, 0:1024], rows[0:1, 0:1024]))
        b.op("dv", lambda e: e.tensor_copy(rowsb[0:1, 1024:2048], var_r),
             inc="dv")
        b.wait("te", "dv", b.now("dv"))
        for tc in range(2):
            b.need("te", f"ps{2 + tc}")
            b.need("te", f"ps{4 + tc}")
            b.op("te", lambda e, tc=tc: e.matmul(
                psF[2 + tc][:, :], ones_r[0:1, :],
                rowsb[0:1, tc * 512:(tc + 1) * 512], start=True, stop=True))
            b.op("te", lambda e, tc=tc: e.matmul(
                psF[4 + tc][:, :], ones_r[0:1, :],
                rowsb[0:1, 1024 + tc * 512:1024 + (tc + 1) * 512],
                start=True, stop=True), inc="te")
        b.wait("ac", "te", b.now("te"))
        b.need("ac", "ab16")
        for tc in range(2):
            b.op("ac", lambda e, tc=tc: e.activation(
                mb_t[:, tc * 512:(tc + 1) * 512], psF[2 + tc][:, :], AF.Copy))
            b.op("ac", lambda e, tc=tc: e.activation(
                rb_t[:, tc * 512:(tc + 1) * 512], psF[4 + tc][:, :], AF.Copy),
                inc="ac")
        for i in (2, 3, 4, 5):
            b.setg(f"ps{i}", "ac")
        b.wait("dv", "ac", b.now("ac"))
        for i in range(NCH):
            for tc in range(2):
                s_ = scr32[:, i * TOK + tc * 512:i * TOK + (tc + 1) * 512]
                b.dvf(lambda e, s=s_, tc=tc:
                      e.tensor_sub(s, s, mb_t[:, tc * 512:(tc + 1) * 512]))
                b.op("dv", lambda e, s=s_, tc=tc:
                     e.tensor_mul(s, s, rb_t[:, tc * 512:(tc + 1) * 512]),
                     inc=("dv" if (i == NCH - 1 and tc == 1) else None))
        b.setg("ab16", "dv")

    def first_stats():
        for i in range(NCH):
            xc = xch[i % 2]
            b.need("sy", f"xch{i % 2}")
            b.dma("sy", xc[:, :], x_in.ap()[i * 128:(i + 1) * 128, :], "sy")
            b.wait("dv", "sy", b.now("sy"))
            dst = scr32[:, i * TOK:(i + 1) * TOK]
            b.op("dv", lambda e, s=xc, d=dst: e.tensor_copy(d, s[:, :]), inc="dv")
            b.setg(f"xch{i % 2}", "dv")
            b.wait("ac", "dv", b.now("dv"))
            stats_chunk(i, dst)

    def xn_exchange(which):
        agi, ago = agi1[which], ago1[which]
        cc = "cca1" if which == "a" else "ccb1"
        dstd = agi.ap().rearrange("(i p) c -> p i c", p=128)
        srcd = scr32.ap().rearrange("p (i c) -> p i c", c=TOK)
        b.wait("sy", "dv", b.now("dv"))
        b.dma("sy", dstd, srcd, "sy")
        b.wait("gp", "sy", b.now("sy"))
        b.op("gp", lambda e, i=agi, o=ago: e.collective_compute(
            "AllGather", ALU.bypass, replica_groups=[list(range(8))],
            ins=[i.ap().opt()], outs=[o.ap().opt()]), inc=cc)
        b.wait("gp", cc, 1)
        xl = xnloc[which]
        srcflat = bass.AP(ago, r_xnb, [[1, 2 * D * TOK]])
        dstflat = bass.AP(xl, 0, [[1, 2 * D * TOK]])
        b.op("gp", lambda e, d=dstflat, s=srcflat:
             e.dma_start(out=d, in_=s), inc="gp", n=16)
        b.wait("sy", "gp", b.now("gp"))
        b.need("sy", "xn_full")
        for r in (0, 1):
            dstap = bass.AP(xn_full, r * TOK,
                            [[NCH * S, 128], [S, NCH], [1, TOK]])
            srcap = bass.AP(xl, r * D * TOK,
                            [[TOK, 128], [128 * TOK, NCH], [1, TOK]])
            b.dma("sy", dstap, srcap, "sy")
        b.wait("te", "sy", b.now("sy"))

    def wblock(src_dram, j, slot, rows_n=NCH):
        dst = wpool.ap()[:, slot * 2048: slot * 2048 + rows_n * 128].rearrange(
            "p (i c) -> p i c", c=128)
        src = src_dram.ap()[:, j * 128:(j + 1) * 128].rearrange(
            "(i p) c -> p i c", p=128)
        b.need("gp", f"wslot{slot}")
        b.dma("gp", dst, src, "gp")

    def wtile(slot, i):
        return wpool[:, slot * 2048 + i * 128: slot * 2048 + (i + 1) * 128]

    def attn_sub(which, xsrc, xdst, fuse_stats, last):
        is_a = which == "a"
        wlq, wlk, wrv, wout = (wq, wk, wv, wo) if is_a else (wbq, wbk, wbv, wbo)
        scale = SCALE_A if is_a else SCALE_B
        cc2 = "cca2" if is_a else "ccb2"
        nhalf = 2 if is_a else 1
        kdim = 64 if is_a else 128

        xn_exchange(which)
        if which == "a":
            b.wait("sy", "gp", b.now("gp"))
            tap("xnfull", xn_full, NCH * S, waits=())

        for g in range(2):
            for kind in ("k", "q", "v"):
                wsrc = {"k": wlk, "q": wlq, "v": wrv}[kind]
                dbuf = {"k": k4, "q": q4, "v": None}[kind]
                for jj in range(4):
                    j = 4 * g + jj
                    slot = jj % 2
                    wblock(wsrc, j, slot)
                    b.wait("te", "gp", b.now("gp"))
                    for tc in range(4):
                        pj = psF[tc % 2]
                        b.need("te", f"ps{tc % 2}")
                        for i in range(NCH):
                            rh = xn_full[:, i * S + tc * 512:
                                         i * S + (tc + 1) * 512]
                            b.op("te", lambda e, p=pj, l=wtile(slot, i), r=rh,
                                 i=i: e.matmul(p[:, :], l, r, start=(i == 0),
                                               stop=(i == NCH - 1)),
                                 inc=("te" if i == NCH - 1 else None))
                        if kind == "v":
                            b.wait("ac", "te", b.now("te"))
                            b.need("ac", "vtmp")
                            b.op("ac", lambda e, p=pj: e.activation(
                                vtmp[:, :], p[:, :], AF.Copy), inc="ac")
                            b.setg(f"ps{tc % 2}", "ac")
                            b.wait("te", "ac", b.now("ac"))
                            for m in range(4):
                                tb = psT[m % 2]
                                b.need("te", f"tb{m % 2}")
                                b.op("te", lambda e, t=tb, m=m: e.transpose(
                                    t[:, 0:128], vtmp[:, m * 128:(m + 1) * 128],
                                    eyeb[:, :]), inc="te")
                                b.wait("ac", "te", b.now("te"))
                                kc = tc * 4 + m
                                dsl = v4[:, kc * 512 + jj * 128:
                                         kc * 512 + (jj + 1) * 128]
                                b.op("ac", lambda e, t=tb, d=dsl: e.activation(
                                    d, t[:, 0:128], AF.Copy), inc="ac")
                                b.setg(f"tb{m % 2}", "ac")
                            b.setg("vtmp", "te")
                        elif is_a:
                            b.wait("ac", "te", b.now("te"))
                            b.need("ac", "qraw")
                            b.op("ac", lambda e, p=pj: e.activation(
                                qraw[:, :], p[:, :], AF.Copy), inc="ac")
                            b.setg(f"ps{tc % 2}", "ac")
                            b.wait("te", "ac", b.now("ac"))
                            prot = psF[2 + tc % 2]
                            b.need("te", f"ps{2 + tc % 2}")
                            b.op("te", lambda e, p=prot: e.matmul(
                                p[:, :], rotmb[:, :], qraw[:, :],
                                start=True, stop=True), inc="te")
                            dsl = dbuf[:, jj * S + tc * 512:
                                       jj * S + (tc + 1) * 512]
                            csl = cosb[:, tc * 512:(tc + 1) * 512]
                            ssl = sinb[:, tc * 512:(tc + 1) * 512]
                            b.wait("dv", "te", b.now("te"))
                            b.op("dv", lambda e, d=dsl, c=csl:
                                 e.tensor_mul(d, qraw[:, :], c))
                            b.dvf(lambda e, p=prot, s=ssl:
                                  e.tensor_mul(w1t[:, :], p[:, :], s))
                            b.op("dv", lambda e, d=dsl:
                                 e.tensor_add(d, d, w1t[:, :]), inc="dv")
                            b.setg("qraw", "dv")
                            b.setg(f"ps{2 + tc % 2}", "dv")
                        else:
                            dsl = dbuf[:, jj * S + tc * 512:
                                       jj * S + (tc + 1) * 512]
                            b.wait("ac", "te", b.now("te"))
                            b.op("ac", lambda e, p=pj, d=dsl: e.activation(
                                d, p[:, :], AF.Copy), inc="ac")
                            b.setg(f"ps{tc % 2}", "ac")
                    b.setg(f"wslot{slot}", "te")
            if g == 1:
                b.setg("xn_full", "te")
            if which == "a" and g == 0:
                tap("q4", q4, 8192)
                tap("k4", k4, 8192)
                tap("v4", v4, 8192)

            b.wait("te", "ac", b.now("ac"))
            b.wait("te", "dv", b.now("dv"))
            for jj in range(4):
                head = 4 * g + jj
                for qc in range(4):
                    nkc = 4 * (qc + 1)
                    for half in range(nhalf):
                        base = 64 * half if is_a else 0
                        pvb = psF[4 + half]
                        zbank = psF[2 + half]
                        b.need("te", f"ps{4 + half}")
                        b.need("te", f"ps{2 + half}")
                        for kc in range(nkc):
                            pt = ptile[kc % 2]
                            sbk = psF[kc % 2]
                            lt = k4[:, jj * S + kc * 128:jj * S + (kc + 1) * 128]
                            rh = q4[:, jj * S + qc * 512:jj * S + (qc + 1) * 512]
                            b.need("te", f"ps{kc % 2}")
                            b.op("te", lambda e, p=sbk, l=lt, r=rh, b0=base,
                                 kd=kdim: e.matmul(p[:, :], l[b0:b0 + kd, :],
                                                   r[b0:b0 + kd, :],
                                                   start=True, stop=True),
                                 inc="te")
                            b.wait("ac", "te", b.now("te"))
                            b.need("ac", f"pt{kc % 2}")
                            b.op("ac", lambda e, p=sbk, t=pt: e.activation(
                                t[:, :], p[:, :], AF.Exp, scale=scale),
                                inc="ac")
                            b.setg(f"ps{kc % 2}", "ac")
                            m = kc - 4 * qc
                            if m >= 0:
                                b.wait("dv", "ac", b.now("ac"))
                                if m > 0:
                                    b.op("dv", lambda e, t=pt, m=m: e.memset(
                                        t[:, 0:m * 128], 0.0))
                                b.op("dv", lambda e, t=pt, m=m: e.tensor_mul(
                                    t[:, m * 128:(m + 1) * 128],
                                    t[:, m * 128:(m + 1) * 128], maskt[:, :]),
                                    inc="dv")
                                b.wait("te", "dv", b.now("dv"))
                            else:
                                b.wait("te", "ac", b.now("ac"))
                            vl = v4[:, kc * 512 + jj * 128:
                                    kc * 512 + (jj + 1) * 128]
                            if ("ptap" in TAPS and g == 0 and jj == 0
                                    and qc <= 1):
                                nm = f"p_{which}_q{qc}_h{half}_k{kc}"
                                tp = nc.dram_tensor(f"tap_{nm}", [128, 512],
                                                    BF16, kind="ExternalOutput")
                                tap_t[nm] = tp
                                b.wait("sy", "ac", b.now("ac"))
                                b.wait("sy", "dv", b.now("dv"))
                                b.dma("sy", tp.ap(), pt[:, :], "sy")
                                b.wait("te", "sy", b.now("sy"))
                            b.op("te", lambda e, p=pvb, v=vl, t=pt, kc=kc:
                                 e.matmul(p[:, :], v, t[:, :],
                                          start=(kc == 0), stop=(kc == nkc - 1)))
                            b.op("te", lambda e, p=zbank, t=pt, kc=kc:
                                 e.matmul(p[0:1, :], ones_c[:, 0:1], t[:, :],
                                          start=(kc == 0), stop=(kc == nkc - 1)),
                                 inc="te")
                            b.setg(f"pt{kc % 2}", "te")
                    b.wait("dv", "te", b.now("te"))
                    if "ptap" in TAPS and g == 0 and jj == 0 and qc <= 1:
                        nm = f"z_{which}_q{qc}"
                        tp = nc.dram_tensor(f"tap_{nm}", [1, 1024], F32,
                                            kind="ExternalOutput")
                        tap_t[nm] = tp
                        b.wait("ac", "te", b.now("te"))
                        b.op("ac", lambda e: e.activation(
                            rows[0:1, 1536:2048], psF[2][0:1, :], AF.Copy))
                        b.op("ac", lambda e: e.activation(
                            rows[0:1, 2048:2560], psF[3][0:1, :], AF.Copy),
                            inc="ac")
                        b.wait("sy", "ac", b.now("ac"))
                        b.dma("sy", tp.ap(), rows[0:1, 1536:2560], "sy")
                        b.wait("dv", "sy", b.now("sy"))
                    zsl = scr32[:, head * S + qc * 512:head * S + (qc + 1) * 512]
                    if is_a:
                        b.dvf(lambda e: e.reciprocal(rows[0:1, 0:512],
                                                     psF[2][0:1, :]))
                        b.dvf(lambda e: e.tensor_mul(
                            rows[0:1, 512:1024], psF[3][0:1, :],
                            rows[0:1, 0:512]))
                        b.op("dv", lambda e: e.tensor_copy(
                            rowsb[0:1, 0:512], rows[0:1, 512:1024]), inc="dv")
                        b.setg("ps2", "dv")
                        b.setg("ps3", "dv")
                        b.wait("te", "dv", b.now("dv"))
                        b.need("te", "ps1")
                        b.op("te", lambda e: e.matmul(
                            psF[1][:, :], ones_r[0:1, :], rowsb[0:1, 0:512],
                            start=True, stop=True), inc="te")
                        b.wait("ac", "te", b.now("te"))
                        b.op("ac", lambda e: e.activation(
                            zbt[:, 0:512], psF[1][:, :], AF.Copy), inc="ac")
                        b.setg("ps1", "ac")
                        b.wait("dv", "ac", b.now("ac"))
                        b.dvf(lambda e: e.tensor_mul(
                            w1t[:, :], psF[4][:, :], zbt[:, 0:512]))
                        b.op("dv", lambda e, z=zsl: e.scalar_tensor_tensor(
                            z, psF[5][:, :], lamn_c[:, 0:1], w1t[:, :],
                            ALU.mult, ALU.add), inc="dv")
                        if "ptap" in TAPS and g == 0 and jj == 0 and qc <= 1:
                            for nm, src_ in ((f"pv1_q{qc}", psF[4]),
                                             (f"pv2_q{qc}", psF[5])):
                                tp = nc.dram_tensor(f"tap_{nm}", [128, 512],
                                                    F32, kind="ExternalOutput")
                                tap_t[nm] = tp
                                b.wait("ac", "dv", b.now("dv"))
                                b.op("ac", lambda e, s=src_: e.activation(
                                    vtmp2[:, :], s[:, :], AF.Copy), inc="ac")
                                b.wait("sy", "ac", b.now("ac"))
                                b.dma("sy", tp.ap(), vtmp2[:, :], "sy")
                                b.wait("ac", "sy", b.now("sy"))
                            nm = f"w_q{qc}"
                            tp = nc.dram_tensor(f"tap_{nm}", [128, 512], BF16,
                                                kind="ExternalOutput")
                            tap_t[nm] = tp
                            b.wait("sy", "dv", b.now("dv"))
                            b.dma("sy", tp.ap(), zsl, "sy")
                            b.wait("dv", "sy", b.now("sy"))
                            nm = f"r12_q{qc}"
                            tp = nc.dram_tensor(f"tap_{nm}", [1, 512], F32,
                                                kind="ExternalOutput")
                            tap_t[nm] = tp
                            b.dma("sy", tp.ap(), rows[0:1, 512:1024], "sy")
                        b.setg("ps4", "dv")
                        b.setg("ps5", "dv")
                    else:
                        b.dvf(lambda e: e.reciprocal(rows[0:1, 0:512],
                                                     psF[2][0:1, :]))
                        b.op("dv", lambda e: e.tensor_copy(
                            rowsb[0:1, 0:512], rows[0:1, 0:512]), inc="dv")
                        b.setg("ps2", "dv")
                        b.wait("te", "dv", b.now("dv"))
                        b.need("te", "ps1")
                        b.op("te", lambda e: e.matmul(
                            psF[1][:, :], ones_r[0:1, :], rowsb[0:1, 0:512],
                            start=True, stop=True), inc="te")
                        b.wait("ac", "te", b.now("te"))
                        b.op("ac", lambda e: e.activation(
                            zbt[:, 0:512], psF[1][:, :], AF.Copy), inc="ac")
                        b.setg("ps1", "ac")
                        b.wait("dv", "ac", b.now("ac"))
                        b.op("dv", lambda e, z=zsl: e.tensor_mul(
                            z, psF[4][:, :], zbt[:, 0:512]), inc="dv")
                        b.setg("ps4", "dv")
                if is_a:
                    for seg in range(2):
                        hsl = scr32[:, head * S + seg * TOK:
                                    head * S + (seg + 1) * TOK]
                        b.need("ac", "sq1")
                        b.wait("ac", "dv", b.now("dv"))
                        b.op("ac", lambda e, h=hsl: e.activation(
                            sq1[:, :], h, AF.Square), inc="ac")
                        b.wait("te", "ac", b.now("ac"))
                        for sc in range(2):
                            pb = psF[2 + sc]
                            b.need("te", f"ps{2 + sc}")
                            b.op("te", lambda e, p=pb, sc=sc: e.matmul(
                                p[0:1, :], ones_c[:, 0:1],
                                sq1[:, sc * 512:(sc + 1) * 512],
                                start=True, stop=True), inc="te")
                        b.setg("sq1", "te")
                        b.wait("ac", "te", b.now("te"))
                        for sc in range(2):
                            b.op("ac", lambda e, sc=sc, seg=seg: e.activation(
                                rows[0:1, seg * 1024 + sc * 512:
                                     seg * 1024 + (sc + 1) * 512],
                                psF[2 + sc][0:1, :], AF.Copy, scale=1.0 / 128),
                                inc="ac")
                        b.setg("ps2", "ac")
                        b.setg("ps3", "ac")
                    b.op("ac", lambda e: e.activation(
                        rows[0:1, 0:2048], rows[0:1, 0:2048], AF.Ln,
                        bias=sca[0:1, 7:8]))
                    b.op("ac", lambda e: e.activation(
                        rows[0:1, 0:2048], rows[0:1, 0:2048], AF.Exp,
                        scale=-0.5), inc="ac")
                    b.wait("dv", "ac", b.now("ac"))
                    b.op("dv", lambda e: e.tensor_copy(rowsb[0:1, 0:2048],
                                                       rows[0:1, 0:2048]),
                         inc="dv")
                    b.wait("te", "dv", b.now("dv"))
                    for sc in range(4):
                        pb = psF[2 + sc % 2]
                        b.need("te", f"ps{2 + sc % 2}")
                        b.op("te", lambda e, p=pb, sc=sc: e.matmul(
                            p[:, :], ones_r[0:1, :],
                            rowsb[0:1, sc * 512:(sc + 1) * 512],
                            start=True, stop=True), inc="te")
                        b.wait("ac", "te", b.now("te"))
                        b.op("ac", lambda e, p=pb, sc=sc: e.activation(
                            zbt[:, (sc % 2) * 512:(sc % 2) * 512 + 512],
                            p[:, :], AF.Copy), inc="ac")
                        b.setg(f"ps{2 + sc % 2}", "ac")
                        b.wait("dv", "ac", b.now("ac"))
                        hsl = scr32[:, head * S + sc * 512:
                                    head * S + (sc + 1) * 512]
                        b.op("dv", lambda e, h=hsl, sc=sc:
                             e.tensor_mul(h, h, zbt[:, (sc % 2) * 512:
                                                     (sc % 2) * 512 + 512]),
                             inc="dv")
                        b.wait("dv", "dv")
                    b.wait("dv", "dv")
                    for q2 in range(4):
                        hsl = scr32[:, head * S + q2 * 512:
                                    head * S + (q2 + 1) * 512]
                        b.op("dv", lambda e, h=hsl: e.tensor_scalar(
                            h, h, lnw_eff[:, 0:1], lnb_eff[:, 0:1],
                            ALU.mult, ALU.add), inc="dv")

        if which == "a":
            tap("zbuf", scr32, NCH * TOK)
        agi, ago = agi2[which], ago2[which]
        b.wait("te", "ac", b.now("ac"))
        b.wait("te", "dv", b.now("dv"))
        b.wait("ac", "te", b.now("te"))
        for oc in range(NCH):
            slot = oc % 2
            wblock(wout, oc, slot, rows_n=8)
            b.wait("te", "gp", b.now("gp"))
            psl = par_sl[oc % 2]
            b.need("ac", f"par{oc % 2}")
            for tc in range(4):
                pj = psF[tc % 2]
                b.need("te", f"ps{tc % 2}")
                for i in range(8):
                    rh = scr32[:, i * S + tc * 512:i * S + (tc + 1) * 512]
                    b.op("te", lambda e, p=pj, l=wtile(slot, i), r=rh, i=i:
                         e.matmul(p[:, :], l, r, start=(i == 0),
                                  stop=(i == 7)),
                         inc=("te" if i == 7 else None))
                b.wait("ac", "te", b.now("te"))
                b.op("ac", lambda e, p=pj, d=psl[:, tc * 512:(tc + 1) * 512]:
                     e.activation(d, p[:, :], AF.Copy), inc="ac")
                b.setg(f"ps{tc % 2}", "ac")
            b.setg(f"wslot{slot}", "te")
            b.wait("sy", "ac", b.now("ac"))
            for hh in (0, 1):
                b.dma("sy",
                      agi.ap()[hh * D + oc * 128:hh * D + (oc + 1) * 128, :],
                      psl[:, hh * TOK:(hh + 1) * TOK], "sy")
            b.setg(f"par{oc % 2}", "sy")
        b.wait("gp", "sy", b.now("sy"))
        b.op("gp", lambda e, i=agi, o=ago: e.collective_compute(
            "AllGather", ALU.bypass, replica_groups=[list(range(8))],
            ins=[i.ap().opt()], outs=[o.ap().opt()]), inc=cc2)
        b.wait("gp", cc2, 1)

        if "par" in TAPS and which == "a":
            tpa = nc.dram_tensor("tap_agi2", [2 * D, TOK], BF16,
                                 kind="ExternalOutput")
            tap_t["agi2"] = tpa
            b.dma("gp", tpa.ap(), agi.ap(), "gp")
        stream_update(which, xsrc, xdst, fuse_stats, last, ago)

    def stream_update(which, xsrc, xdst, fuse_stats, last, ago):
        pl = parloc[which]
        for r in (0, 1):
            dstrot = bass.AP(pl, r * D * TOK, [[1, D * TOK]])
            if r == 0:
                srcrot = bass.AP(ago, r_parb, [[1, D * TOK]])
            else:
                b.op("gp", lambda e: e.reg_mov(r_tmp, 2 * D * TOK))
                b.op("gp", lambda e: e.reg_add(r_tmp, r_tmp, r_parb))
                srcrot = bass.AP(ago, r_tmp, [[1, D * TOK]])
            b.op("gp", lambda e, d=dstrot, s=srcrot: e.dma_start(out=d, in_=s),
                 inc="gp", n=16)
        if "par" in TAPS and which == "a":
            tpo = nc.dram_tensor("tap_ago2", [4 * D, TOK], BF16,
                                 kind="ExternalOutput")
            tap_t["ago2"] = tpo
            b.dma("gp", tpo.ap(), ago.ap()[0:4 * D, :], "gp")
            tpp = nc.dram_tensor("tap_parloc", [2 * D, TOK], BF16,
                                 kind="ExternalOutput")
            tap_t["parloc"] = tpp
            b.dma("gp", tpp.ap(), pl.ap(), "gp")
        for i in range(NCH):
            pob = po_sl[i % 2]
            b.need("gp", f"po{i % 2}")
            srcap = bass.AP(pl, i * 128 * TOK,
                            [[TOK, 128], [D * TOK, 2], [1, TOK]])
            dstap = pob.rearrange("p (r c) -> p r c", c=TOK)
            b.op("gp", lambda e, d=dstap, s=srcap: e.dma_start(out=d, in_=s),
                 inc="gp", n=16)
            xc = xch[i % 2]
            b.need("sy", f"xch{i % 2}")
            b.dma("sy", xc[:, :], xsrc.ap()[i * 128:(i + 1) * 128, :], "sy")
            b.wait("dv", "gp", b.now("gp"))
            b.wait("dv", "sy", b.now("sy"))
            for tc in range(2):
                sl = slice(tc * 512, (tc + 1) * 512)
                sl2 = slice(1024 + tc * 512, 1024 + (tc + 1) * 512)
                b.dvf(lambda e, p=pob, sl=sl, sl2=sl2:
                      e.tensor_add(w1t[:, :], p[:, sl], p[:, sl2]))
                b.op("dv", lambda e, x=xc, sl=sl:
                     e.tensor_add(x[:, sl], x[:, sl], w1t[:, :]), inc="dv")
                b.wait("dv", "dv")
            b.setg(f"po{i % 2}", "dv")
            if fuse_stats:
                dstn = scr32[:, i * TOK:(i + 1) * TOK]
                b.op("dv", lambda e, s=xc, d=dstn: e.tensor_copy(d, s[:, :]),
                     inc="dv")
                b.wait("ac", "dv", b.now("dv"))
                stats_chunk(i, dstn)
            b.wait("sy", "dv", b.now("dv"))
            b.dma("sy", xdst.ap()[i * 128:(i + 1) * 128, :], xc[:, :], "sy")
            b.setg(f"xch{i % 2}", "sy")

    def ffn_sub(xsrc, xdst, fuse_stats, last):
        b.wait("dv", "ac", b.now("ac"))
        for i in range(NCH):
            for tc in range(2):
                s_ = scr32[:, i * TOK + tc * 512:i * TOK + (tc + 1) * 512]
                b.dvf(lambda e, s=s_: e.tensor_scalar(
                    f8t[:, :], s, 240.0, -240.0, ALU.min, ALU.max))
                b.op("dv", lambda e, s=s_: e.tensor_copy(s, f8t[:, :]),
                     inc="dv")
                b.wait("dv", "dv")
        for oc in range(NCH):
            slot = oc % 2
            dst8 = w8p.ap()[:, slot * 2048:(slot + 1) * 2048].rearrange(
                "p (i c) -> p i c", c=128)
            src8 = w8t.ap()[:, oc * 128:(oc + 1) * 128].rearrange(
                "(i p) c -> p i c", p=128)
            b.need("sy", f"w8s{slot}")
            b.dma("sy", dst8, src8, "sy")
            b.wait("dv", "sy", b.now("sy"))
            b.need("dv", f"wslot{slot}")
            b.op("dv", lambda e, s=slot: e.tensor_copy(
                wpool[:, s * 2048:(s + 1) * 2048],
                w8p[:, s * 2048:(s + 1) * 2048]), inc="dv")
            b.setg(f"w8s{slot}", "dv")
            b.wait("te", "dv", b.now("dv"))
            xc = xch[oc % 2]
            b.need("sy", f"xch{oc % 2}")
            b.dma("sy", xc[:, :], xsrc.ap()[oc * 128:(oc + 1) * 128, :], "sy")
            for tc in range(2):
                pj = psF[tc % 2]
                b.need("te", f"ps{tc % 2}")
                for i in range(NCH):
                    rh = scr32[:, i * TOK + tc * 512:i * TOK + (tc + 1) * 512]
                    b.op("te", lambda e, p=pj, l=wtile(slot, i), r=rh, i=i:
                         e.matmul(p[:, :], l, r, start=(i == 0),
                                  stop=(i == NCH - 1)),
                         inc=("te" if i == NCH - 1 else None))
            b.setg(f"wslot{slot}", "te")
            b.wait("dv", "te", b.now("te"))
            b.wait("dv", "sy", b.now("sy"))
            for tc in range(2):
                pj = psF[tc % 2]
                sl = slice(tc * 512, (tc + 1) * 512)
                b.op("dv", lambda e, p=pj, x=xc, sl=sl:
                     e.scalar_tensor_tensor(x[:, sl], p[:, :], alpha_c[:, 0:1],
                                            x[:, sl], ALU.mult, ALU.add),
                     inc="dv")
                b.setg(f"ps{tc % 2}", "dv")
            b.wait("sy", "dv", b.now("dv"))
            b.dma("sy", xdst.ap()[oc * 128:(oc + 1) * 128, :], xc[:, :], "sy")
            b.setg(f"xch{oc % 2}", "sy")
        if fuse_stats:
            for i in range(NCH):
                xc = xch[i % 2]
                b.need("sy", f"xch{i % 2}")
                b.dma("sy", xc[:, :], xdst.ap()[i * 128:(i + 1) * 128, :], "sy")
                b.wait("dv", "sy", b.now("sy"))
                dstn = scr32[:, i * TOK:(i + 1) * TOK]
                b.op("dv", lambda e, s=xc, d=dstn: e.tensor_copy(d, s[:, :]),
                     inc="dv")
                b.setg(f"xch{i % 2}", "dv")
                b.wait("ac", "dv", b.now("dv"))
                stats_chunk(i, dstn)

    first_stats()
    ln_finish(ffn=False)
    tap("xnw1", scr32, NCH * TOK)
    attn_sub("a", x_in, xbufA, fuse_stats=(n_sub >= 2), last=(n_sub == 1))
    if n_sub >= 2:
        ln_finish(ffn=True)
        ffn_sub(xbufA, xbufB, fuse_stats=(n_sub >= 3), last=(n_sub == 2))
    if n_sub >= 3:
        ln_finish(ffn=False)
        attn_sub("b", xbufB, xbufA, fuse_stats=(n_sub >= 4), last=(n_sub == 3))
    if n_sub >= 4:
        ln_finish(ffn=True)
        ffn_sub(xbufA, out_fm, fuse_stats=False, last=True)

    fin = {1: xbufA, 2: xbufB, 3: xbufA}.get(n_sub)
    if fin is not None and n_sub < 4:
        b.wait("gp", "sy", b.now("sy"))
        b.dma("gp", out_fm.ap(), fin.ap(), "gp")

    b.wait("sy", "sy", b.now("sy"))
    b.wait("sy", "gp", b.now("gp"))

    with nc.Block() as block:
        for eng_name, attr in (("te", "tensor"), ("dv", "vector"),
                               ("ac", "scalar"), ("gp", "gpsimd"),
                               ("sy", "sync")):
            ops = b.q[eng_name]

            def runner(e, ops=ops):
                for f in ops:
                    f(e)
            getattr(block, attr)(runner)

    b.stack.close()
    return nc


_NC_CACHE = {}


def _get_nc():
    key = N_SUB
    if key not in _NC_CACHE:
        _NC_CACHE[key] = build_nc(key)
    return _NC_CACHE[key]


def _rope_tables():
    dd = 64
    inv = (1.0 / np.float32(10000.0) **
           (np.arange(0, dd, 2, dtype=np.float32) / np.float32(dd)))
    t = np.arange(S, dtype=np.float32)
    freqs = (t[:, None] * inv[None, :]).astype(np.float32)
    freqs = np.repeat(freqs, 2, axis=1)
    cos = np.cos(freqs).astype(np.float32)
    sin = np.sin(freqs).astype(np.float32)
    cosf = np.concatenate([cos.T, cos.T], axis=0)
    sinf = np.concatenate([sin.T, sin.T], axis=0)
    return (np.ascontiguousarray(cosf).astype(ml_dtypes.bfloat16),
            np.ascontiguousarray(sinf).astype(ml_dtypes.bfloat16))


def _host_prep(inputs):
    x = np.asarray(inputs["L0_attn_a_x"])
    Wq = np.asarray(inputs["L0_attn_a_Wq"])
    Wk = np.asarray(inputs["L0_attn_a_Wk"])
    Wv = np.asarray(inputs["L0_attn_a_Wv"])
    Wo = np.asarray(inputs["L0_attn_a_Wo"])
    WK = np.asarray(inputs["L0_attn_b_W_K"])
    WQ = np.asarray(inputs["L0_attn_b_W_Q"])
    WV = np.asarray(inputs["L0_attn_b_W_V"])
    WO = np.asarray(inputs["L0_attn_b_W_O"])
    w8 = np.asarray(inputs["L0_ffn_weight"])
    wsc = np.float32(inputs["L0_ffn_weight_scale"])
    isc = np.float32(inputs["L0_ffn_input_scale"])
    lam = np.concatenate([np.asarray(inputs["L0_attn_a_lambda_q1"]),
                          np.asarray(inputs["L0_attn_a_lambda_k1"]),
                          np.asarray(inputs["L0_attn_a_lambda_q2"]),
                          np.asarray(inputs["L0_attn_a_lambda_k2"])]
                         ).astype(np.float32).reshape(1, 256)
    lnw = np.asarray(inputs["L0_attn_a_ln_w"]).astype(np.float32).reshape(128, 1)
    lnb = np.asarray(inputs["L0_attn_a_ln_b"]).astype(np.float32).reshape(128, 1)

    cosb, sinb = _rope_tables()
    rot = np.zeros((128, 128), dtype=np.float32)
    for p in range(0, 128, 2):
        rot[p + 1, p] = -1.0
        rot[p, p + 1] = 1.0
    rot = np.ascontiguousarray(rot).astype(ml_dtypes.bfloat16)
    tri = np.ascontiguousarray(
        np.triu(np.ones((128, 128), dtype=np.float32))).astype(
        ml_dtypes.bfloat16)
    eye = np.ascontiguousarray(np.eye(128, dtype=np.float32)).astype(
        ml_dtypes.bfloat16)

    w8t = np.ascontiguousarray(w8.T).view(ml_dtypes.float8_e4m3)

    in_maps = []
    for c in range(8):
        bb, h = c >> 1, c & 1
        tok = slice(1024 * h, 1024 * h + 1024)
        qk = slice(1024 * h, 1024 * h + 1024)
        heads = slice(8 * h, 8 * h + 8)
        x_fm = np.ascontiguousarray(x[bb, tok, :].T)
        wbk_c = np.ascontiguousarray(
            WK[heads].transpose(2, 0, 1).reshape(D, 1024))
        wbq_c = np.ascontiguousarray(
            WQ[heads].transpose(2, 0, 1).reshape(D, 1024))
        wbv_c = np.ascontiguousarray(
            WV[heads].transpose(2, 0, 1).reshape(D, 1024))
        wbo_c = np.ascontiguousarray(WO[:, qk].T)
        offs = np.array([[(c & ~1) * D * TOK,
                          (c & ~1) * 2 * D * TOK + h * D * TOK, 0, 0]],
                        dtype=np.int32)
        in_maps.append({
            "x_in": x_fm,
            "wq": np.ascontiguousarray(Wq[:, qk]),
            "wk": np.ascontiguousarray(Wk[:, qk]),
            "wv": np.ascontiguousarray(Wv[:, qk]),
            "wo": np.ascontiguousarray(Wo[qk, :]),
            "wbk": wbk_c, "wbq": wbq_c, "wbv": wbv_c, "wbo": wbo_c,
            "w8t": w8t,
            "cos_in": cosb, "sin_in": sinb, "rot_in": rot,
            "tri_in": tri,
            "eye_in": eye,
            "lam_in": lam, "lnw_in": lnw, "lnb_in": lnb,
            "wsc_in": wsc.reshape(1, 1),
            "isc_in": isc.reshape(1, 1),
            "offs_in": offs,
        })
    return in_maps


def kernel(**inputs):
    in_maps = _host_prep(inputs)
    nc = _get_nc()
    res = run_bass_kernel_spmd(nc, in_maps, core_ids=list(range(8)))
    out = np.empty((4, S, D), dtype=np.float32)
    for c in range(8):
        bb, h = c >> 1, c & 1
        out[bb, 1024 * h:1024 * h + 1024, :] = res.results[c]["out_fm"].T
    if TAPS:
        kernel.tap_results = [
            {k: v for k, v in r.items() if k.startswith("tap_")}
            for r in res.results]
    return out


# version 02
# speedup vs baseline: 1.0796x; 1.0796x over previous
import contextlib
import math

import numpy as np
import ml_dtypes

import concourse.bass as bass
import concourse.mybir as mybir
from concourse.bass_utils import run_bass_kernel_spmd

F32 = mybir.dt.float32
BF16 = mybir.dt.bfloat16
FP8 = mybir.dt.float8e4
I32 = mybir.dt.int32
AF = mybir.ActivationFunctionType
ALU = mybir.AluOpType
AXX = mybir.AxisListType.X

D = 2048
S = 2048
TOK = 1024
NCH = 16
HEADS = 8
LAMBDA_INIT = 0.8 - 0.6 * math.exp(-0.3 * 12)
EPS_LN = 1e-5
EPS_RMS = 1e-8
SCALE_A = 64.0 ** -0.5
SCALE_B = 128.0 ** -0.5
N_SUB = 4
TAPS = ()

_ENGS = ("te", "dv", "ac", "gp", "sy")


class _B:

    def __init__(self, nc):
        self.nc = nc
        self.q = {e: [] for e in _ENGS}
        self.cnt = {}
        self.sems = {}
        self.stack = contextlib.ExitStack()
        self._lastwait = {e: {} for e in _ENGS}
        self.guard = {}

    def sem(self, name):
        self.sems[name] = self.stack.enter_context(self.nc.semaphore(name))
        self.cnt[name] = 0

    def wait(self, eng, sem, val=None):
        v = self.cnt[sem] if val is None else val
        if v <= 0:
            return
        if self._lastwait[eng].get(sem, -1) >= v:
            return
        self._lastwait[eng][sem] = v
        h = self.sems[sem]
        self.q[eng].append(lambda e, h=h, v=v: e.wait_ge(h, v))

    def op(self, eng, fn, inc=None, n=1):
        if inc is None:
            self.q[eng].append(fn)
        else:
            h = self.sems[inc]
            self.q[eng].append(lambda e, fn=fn, h=h, n=n: fn(e).then_inc(h, n))
            self.cnt[inc] += n

    def dma(self, eng, out, in_, inc):
        self.op(eng, lambda e, o=out, i=in_: e.dma_start(out=o, in_=i),
                inc=inc, n=16)

    def now(self, sem):
        return self.cnt[sem]

    def need(self, eng, key):
        g = self.guard.get(key)
        if g:
            self.wait(eng, g[0], g[1])

    def setg(self, key, sem):
        self.guard[key] = (sem, self.cnt[sem])

    def dvf(self, fn):
        self.op("dv", fn, inc="dv")
        self.wait("dv", "dv")


def build_nc(n_sub=N_SUB):
    nc = bass.Bass("TRN2", target_bir_lowering=False)
    b = _B(nc)
    st = b.stack
    sb = lambda name, shape, dt: st.enter_context(
        nc.sbuf_tensor(name, list(shape), dt))

    def din(name, shape, dt):
        return nc.dram_tensor(name, list(shape), dt, kind="ExternalInput")

    x_in = din("x_in", [D, TOK], F32)
    wq = din("wq", [D, TOK], F32)
    wk = din("wk", [D, TOK], F32)
    wv = din("wv", [D, TOK], F32)
    wo = din("wo", [TOK, D], F32)
    wbk = din("wbk", [D, TOK], F32)
    wbq = din("wbq", [D, TOK], F32)
    wbv = din("wbv", [D, TOK], F32)
    wbo = din("wbo", [TOK, D], F32)
    w8t = din("w8t", [D, D], FP8)
    cos_in = din("cos_in", [128, S], BF16)
    sin_in = din("sin_in", [128, S], BF16)
    rot_in = din("rot_in", [128, 128], BF16)
    tri_in = din("tri_in", [128, 128], BF16)
    eye_in = din("eye_in", [128, 128], BF16)
    lam_in = din("lam_in", [1, 256], F32)
    lnw_in = din("lnw_in", [128, 1], F32)
    lnb_in = din("lnb_in", [128, 1], F32)
    wsc_in = din("wsc_in", [1, 1], F32)
    isc_in = din("isc_in", [1, 1], F32)
    offs_in = din("offs_in", [1, 4], I32)

    out_fm = nc.dram_tensor("out_fm", [D, TOK], F32, kind="ExternalOutput")

    xbufA = nc.dram_tensor("xbufA", [D, TOK], F32)
    xbufB = nc.dram_tensor("xbufB", [D, TOK], F32)
    agi1 = {a: nc.dram_tensor(f"ag1{a}_in", [D, TOK], BF16) for a in "ab"}
    ago1 = {a: nc.dram_tensor(f"ag1{a}_out", [8 * D, TOK], BF16,
                              addr_space="Shared") for a in "ab"}
    agi2 = {a: nc.dram_tensor(f"ag2{a}_in", [2 * D, TOK], BF16) for a in "ab"}
    ago2 = {a: nc.dram_tensor(f"ag2{a}_out", [16 * D, TOK], BF16,
                              addr_space="Shared") for a in "ab"}
    xnloc = {a: nc.dram_tensor(f"xnloc{a}", [2 * D, TOK], BF16) for a in "ab"}
    parloc = {a: nc.dram_tensor(f"parloc{a}", [2 * D, TOK], BF16) for a in "ab"}

    xn_full = sb("xn_full", [128, NCH * S], BF16)
    scr32 = sb("scr32", [128, NCH * TOK], BF16)
    ab16 = sb("ab16", [128, 8192], BF16)
    k4 = sb("k4", [128, 8192], BF16)
    v4 = sb("v4", [128, 8192], BF16)
    cs16 = sb("cs16", [128, 4096], BF16)
    wpool = sb("wpool", [128, 4096], BF16)
    w8p = sb("w8p", [128, 4096], FP8)
    xch = [sb(f"xch{i}", [128, TOK], F32) for i in (0, 1)]
    sq1 = sb("sq1", [128, TOK], BF16)
    ptile = [sb(f"ptile{i}", [128, 512], BF16) for i in (0, 1)]
    qraw = sb("qraw", [128, 512], BF16)
    vtmp = sb("vtmp", [128, 512], BF16)
    f8t = sb("f8t", [128, 512], FP8)
    vtmp2 = sb("vtmp2", [128, 512], F32)
    zbt = sb("zbt", [128, 1024], BF16)
    w1t = sb("w1t", [128, 512], F32)
    rows = sb("rows", [1, 2560], F32)
    rowsb = sb("rowsb", [1, 2048], BF16)
    maskt = sb("maskt", [128, 128], BF16)
    rotmb = sb("rotmb", [128, 128], BF16)
    eyeb = sb("eyeb", [128, 128], BF16)
    ones_c = sb("ones_c", [128, 1], BF16)
    ones_r = sb("ones_r", [1, 128], BF16)
    lamv = sb("lamv", [1, 256], F32)
    lnw = sb("lnw", [128, 1], F32)
    lnb = sb("lnb", [128, 1], F32)
    lnw_eff = sb("lnw_eff", [128, 1], F32)
    lnb_eff = sb("lnb_eff", [128, 1], F32)
    sca = sb("sca", [1, 8], F32)
    onef = sb("onef", [1, 128], F32)
    alpha_c = sb("alpha_c", [128, 1], F32)
    lamn_c = sb("lamn_c", [128, 1], F32)
    offs_sb = sb("offs_sb", [1, 4], I32)

    mb_t = ab16[:, 0:1024]
    rb_t = ab16[:, 1024:2048]
    q4 = ab16
    par_sl = [ab16[:, 0:2048], ab16[:, 2048:4096]]
    cosb = cs16[:, 0:2048]
    sinb = cs16[:, 2048:4096]
    po_sl = [cs16[:, 0:2048], cs16[:, 2048:4096]]

    psF = [st.enter_context(nc.psum_tensor(f"ps{i}", [128, 512], F32))
           for i in range(6)]
    psT = [st.enter_context(nc.psum_tensor(f"tb{i}", [128, 1024], BF16))
           for i in range(2)]

    for s_name in ("sy", "gp", "te", "dv", "ac",
                   "cca1", "cca2", "ccb1", "ccb2"):
        b.sem(s_name)

    tap_t = {}

    def tap(point, buf, cols, waits=("dv", "ac", "te")):
        if point not in TAPS:
            return
        t = nc.dram_tensor(f"tap_{point}", [128, cols], BF16,
                           kind="ExternalOutput")
        tap_t[point] = t
        for w in waits:
            b.wait("sy", w, b.now(w))
        b.dma("sy", t.ap(), buf[:, 0:cols], "sy")

    def tapf(point, buf, cols, waits=("dv", "ac", "te")):
        if point not in TAPS:
            return
        t = nc.dram_tensor(f"tap_{point}", [128, cols], F32,
                           kind="ExternalOutput")
        tap_t[point] = t
        for w in waits:
            b.wait("sy", w, b.now(w))
        b.dma("sy", t.ap(), buf[:, 0:cols], "sy")

    r_xnb = st.enter_context(nc.gpsimd.register("r_xnb"))
    r_parb = st.enter_context(nc.gpsimd.register("r_parb"))
    r_tmp = st.enter_context(nc.gpsimd.register("r_tmp"))

    for src, dst in ((cos_in, cosb), (sin_in, sinb), (rot_in, rotmb[:, :]),
                     (tri_in, maskt[:, :]), (eye_in, eyeb[:, :]),
                     (lam_in, lamv[:, :]), (lnw_in, lnw[:, :]),
                     (lnb_in, lnb[:, :]), (offs_in, offs_sb[:, :])):
        b.dma("sy", dst, src.ap(), "sy")
    b.dma("sy", sca[0:1, 0:1], wsc_in.ap(), "sy")
    b.dma("sy", sca[0:1, 1:2], isc_in.ap(), "sy")

    b.wait("dv", "sy", b.now("sy"))
    b.op("dv", lambda e: e.memset(ones_c[:, :], 1.0))
    b.op("dv", lambda e: e.memset(sca[0:1, 6:7], EPS_LN))
    b.op("dv", lambda e: e.memset(sca[0:1, 7:8], EPS_RMS))
    b.op("dv", lambda e: e.memset(ones_r[:, :], 1.0))
    b.op("dv", lambda e: e.memset(onef[:, :], 1.0))
    b.op("dv", lambda e: e.tensor_mul(rows[0:1, 0:64], lamv[0:1, 0:64],
                                      lamv[0:1, 64:128]))
    b.dvf(lambda e: e.tensor_mul(rows[0:1, 64:128], lamv[0:1, 128:192],
                                 lamv[0:1, 192:256]))
    b.op("dv", lambda e: e.reduce_sum(rows[0:1, 128:129], rows[0:1, 0:64], axis=AXX))
    b.op("dv", lambda e: e.reduce_sum(rows[0:1, 129:130], rows[0:1, 64:128],
                                      axis=AXX), inc="dv")
    b.wait("ac", "dv", b.now("dv"))
    b.op("ac", lambda e: e.activation(rows[0:1, 130:132], rows[0:1, 128:130],
                                      AF.Exp), inc="ac")
    b.wait("dv", "ac", b.now("ac"))
    b.dvf(lambda e: e.tensor_sub(rows[0:1, 132:133], rows[0:1, 130:131],
                                 rows[0:1, 131:132]))
    b.dvf(lambda e: e.tensor_scalar_add(sca[0:1, 4:5], rows[0:1, 132:133],
                                        LAMBDA_INIT))
    b.dvf(lambda e: e.tensor_scalar_mul(sca[0:1, 5:6], sca[0:1, 4:5], -1.0))
    b.dvf(lambda e: e.reciprocal(sca[0:1, 2:3], sca[0:1, 1:2]))
    b.op("dv", lambda e: e.tensor_mul(sca[0:1, 3:4], sca[0:1, 0:1], sca[0:1, 1:2]),
         inc="dv")
    b.wait("ac", "dv", b.now("dv"))
    b.op("ac", lambda e: e.activation(lnw_eff[:, :], lnw[:, :], AF.Copy,
                                      scale=1.0 - LAMBDA_INIT))
    b.op("ac", lambda e: e.activation(lnb_eff[:, :], lnb[:, :], AF.Copy,
                                      scale=1.0 - LAMBDA_INIT), inc="ac")
    b.wait("te", "dv", b.now("dv"))
    b.op("te", lambda e: e.matmul(psF[4][:, 0:1], onef[0:1, :], sca[0:1, 3:4],
                                  start=True, stop=True))
    b.op("te", lambda e: e.matmul(psF[4][:, 1:2], onef[0:1, :], sca[0:1, 5:6],
                                  start=True, stop=True), inc="te")
    b.wait("ac", "te", b.now("te"))
    b.op("ac", lambda e: e.activation(alpha_c[:, :], psF[4][:, 0:1], AF.Copy))
    b.op("ac", lambda e: e.activation(lamn_c[:, :], psF[4][:, 1:2], AF.Copy),
         inc="ac")
    b.setg("ps4", "ac")

    if "setup" in TAPS:
        tp = nc.dram_tensor("tap_sca", [1, 8], F32, kind="ExternalOutput")
        tap_t["sca"] = tp
        tp2 = nc.dram_tensor("tap_cols", [2, 128], F32, kind="ExternalOutput")
        tap_t["cols"] = tp2
        tp3 = nc.dram_tensor("tap_rows0", [1, 256], F32, kind="ExternalOutput")
        tap_t["rows0"] = tp3
        b.wait("sy", "ac", b.now("ac"))
        b.wait("sy", "dv", b.now("dv"))
        b.dma("sy", tp.ap(), sca[0:1, :], "sy")
        b.dma("sy", tp2.ap()[0:1, :].rearrange("a p -> p a"), alpha_c[:, :], "sy")
        b.dma("sy", tp2.ap()[1:2, :].rearrange("a p -> p a"), lamn_c[:, :], "sy")
        b.dma("sy", tp3.ap(), rows[0:1, 0:256], "sy")
    b.wait("gp", "sy", b.now("sy"))
    b.op("gp", lambda e: e.reg_load(r_xnb, offs_sb[0:1, 0:1]))
    b.op("gp", lambda e: e.reg_load(r_parb, offs_sb[0:1, 1:2]))

    def stats_chunk(i, src_bf):
        b.need("ac", "sq1")
        b.op("ac", lambda e, s=src_bf: e.activation(sq1[:, :], s, AF.Square),
             inc="ac")
        b.wait("te", "ac", b.now("ac"))
        for tc in range(2):
            sl = slice(tc * 512, (tc + 1) * 512)
            if i == 0:
                b.need("te", f"ps{2 + tc}")
                b.need("te", f"ps{4 + tc}")
            b.op("te", lambda e, s=src_bf, sl=sl, tc=tc, i=i:
                 e.matmul(psF[2 + tc][0:1, :], ones_c[:, 0:1], s[:, sl],
                          start=(i == 0), stop=(i == NCH - 1)))
            b.op("te", lambda e, sl=sl, tc=tc, i=i:
                 e.matmul(psF[4 + tc][0:1, :], ones_c[:, 0:1], sq1[:, sl],
                          start=(i == 0), stop=(i == NCH - 1)),
                 inc=("te" if (i == NCH - 1 and tc == 1) else None))
        b.setg("sq1", "te")

    def ln_finish(ffn):
        b.wait("ac", "te", b.now("te"))
        for tc in range(2):
            b.op("ac", lambda e, tc=tc: e.activation(
                rows[0:1, tc * 512:(tc + 1) * 512], psF[2 + tc][0:1, :],
                AF.Copy, scale=1.0 / D))
            b.op("ac", lambda e, tc=tc: e.activation(
                rows[0:1, 1024 + tc * 512:1024 + (tc + 1) * 512],
                psF[4 + tc][0:1, :], AF.Copy, scale=1.0 / D), inc="ac")
        for i in (2, 3, 4, 5):
            b.setg(f"ps{i}", "ac")
        b.wait("dv", "ac", b.now("ac"))
        for tc in range(2):
            m_h = rows[0:1, tc * 512:(tc + 1) * 512]
            e_h = rows[0:1, 1024 + tc * 512:1024 + (tc + 1) * 512]
            t_h = rows[0:1, 2048:2560]
            b.dvf(lambda e, m=m_h, t=t_h: e.tensor_mul(t, m, m))
            b.op("dv", lambda e, eh=e_h, t=t_h: e.tensor_sub(eh, eh, t),
                 inc="dv")
            b.wait("dv", "dv")
        b.wait("ac", "dv", b.now("dv"))
        var_r = rows[0:1, 1024:2048]
        b.op("ac", lambda e: e.activation(var_r, var_r, AF.Ln, bias=sca[0:1, 6:7]))
        b.op("ac", lambda e: e.activation(var_r, var_r, AF.Exp, scale=-0.5),
             inc="ac")
        b.wait("dv", "ac", b.now("ac"))
        if ffn:
            b.dvf(lambda e: e.tensor_scalar(var_r, var_r, sca[0:1, 2:3],
                                            None, ALU.mult))
        b.op("dv", lambda e: e.tensor_copy(rowsb[0:1, 0:1024], rows[0:1, 0:1024]))
        b.op("dv", lambda e: e.tensor_copy(rowsb[0:1, 1024:2048], var_r),
             inc="dv")
        b.wait("te", "dv", b.now("dv"))
        for tc in range(2):
            b.need("te", f"ps{2 + tc}")
            b.need("te", f"ps{4 + tc}")
            b.op("te", lambda e, tc=tc: e.matmul(
                psF[2 + tc][:, :], ones_r[0:1, :],
                rowsb[0:1, tc * 512:(tc + 1) * 512], start=True, stop=True))
            b.op("te", lambda e, tc=tc: e.matmul(
                psF[4 + tc][:, :], ones_r[0:1, :],
                rowsb[0:1, 1024 + tc * 512:1024 + (tc + 1) * 512],
                start=True, stop=True), inc="te")
        b.wait("ac", "te", b.now("te"))
        b.need("ac", "ab16")
        for tc in range(2):
            b.op("ac", lambda e, tc=tc: e.activation(
                mb_t[:, tc * 512:(tc + 1) * 512], psF[2 + tc][:, :], AF.Copy))
            b.op("ac", lambda e, tc=tc: e.activation(
                rb_t[:, tc * 512:(tc + 1) * 512], psF[4 + tc][:, :], AF.Copy),
                inc="ac")
        for i in (2, 3, 4, 5):
            b.setg(f"ps{i}", "ac")
        b.wait("dv", "ac", b.now("ac"))
        for i in range(NCH):
            for tc in range(2):
                s_ = scr32[:, i * TOK + tc * 512:i * TOK + (tc + 1) * 512]
                b.dvf(lambda e, s=s_, tc=tc:
                      e.tensor_sub(s, s, mb_t[:, tc * 512:(tc + 1) * 512]))
                b.op("dv", lambda e, s=s_, tc=tc:
                     e.tensor_mul(s, s, rb_t[:, tc * 512:(tc + 1) * 512]),
                     inc=("dv" if (i == NCH - 1 and tc == 1) else None))
        b.setg("ab16", "dv")

    def first_stats():
        for i in range(NCH):
            xc = xch[i % 2]
            b.need("sy", f"xch{i % 2}")
            b.dma("sy", xc[:, :], x_in.ap()[i * 128:(i + 1) * 128, :], "sy")
            b.wait("dv", "sy", b.now("sy"))
            dst = scr32[:, i * TOK:(i + 1) * TOK]
            b.op("dv", lambda e, s=xc, d=dst: e.tensor_copy(d, s[:, :]), inc="dv")
            b.setg(f"xch{i % 2}", "dv")
            b.wait("ac", "dv", b.now("dv"))
            stats_chunk(i, dst)

    def xn_exchange(which):
        agi, ago = agi1[which], ago1[which]
        cc = "cca1" if which == "a" else "ccb1"
        dstd = agi.ap().rearrange("(i p) c -> p i c", p=128)
        srcd = scr32.ap().rearrange("p (i c) -> p i c", c=TOK)
        b.wait("sy", "dv", b.now("dv"))
        b.dma("sy", dstd, srcd, "sy")
        b.wait("gp", "sy", b.now("sy"))
        b.op("gp", lambda e, i=agi, o=ago: e.collective_compute(
            "AllGather", ALU.bypass, replica_groups=[list(range(8))],
            ins=[i.ap().opt()], outs=[o.ap().opt()]), inc=cc)
        b.wait("gp", cc, 1)
        xl = xnloc[which]
        srcflat = bass.AP(ago, r_xnb, [[1, 2 * D * TOK]])
        dstflat = bass.AP(xl, 0, [[1, 2 * D * TOK]])
        b.op("gp", lambda e, d=dstflat, s=srcflat:
             e.dma_start(out=d, in_=s), inc="gp", n=16)
        b.wait("sy", "gp", b.now("gp"))
        b.need("sy", "xn_full")
        for r in (0, 1):
            dstap = bass.AP(xn_full, r * TOK,
                            [[NCH * S, 128], [S, NCH], [1, TOK]])
            srcap = bass.AP(xl, r * D * TOK,
                            [[TOK, 128], [128 * TOK, NCH], [1, TOK]])
            b.dma("sy", dstap, srcap, "sy")
        b.wait("te", "sy", b.now("sy"))

    def wblock(src_dram, j, slot, rows_n=NCH):
        dst = wpool.ap()[:, slot * 2048: slot * 2048 + rows_n * 128].rearrange(
            "p (i c) -> p i c", c=128)
        src = src_dram.ap()[:, j * 128:(j + 1) * 128].rearrange(
            "(i p) c -> p i c", p=128)
        b.need("gp", f"wslot{slot}")
        b.dma("gp", dst, src, "gp")

    def wtile(slot, i):
        return wpool[:, slot * 2048 + i * 128: slot * 2048 + (i + 1) * 128]

    def attn_sub(which, xsrc, xdst, fuse_stats, last):
        is_a = which == "a"
        wlq, wlk, wrv, wout = (wq, wk, wv, wo) if is_a else (wbq, wbk, wbv, wbo)
        scale = SCALE_A if is_a else SCALE_B
        cc2 = "cca2" if is_a else "ccb2"
        nhalf = 2 if is_a else 1
        kdim = 64 if is_a else 128

        xn_exchange(which)
        if which == "a":
            b.wait("sy", "gp", b.now("gp"))
            tap("xnfull", xn_full, NCH * S, waits=())

        for g in range(2):
            for kind in ("k", "q", "v"):
                wsrc = {"k": wlk, "q": wlq, "v": wrv}[kind]
                dbuf = {"k": k4, "q": q4, "v": None}[kind]
                for jj in range(4):
                    j = 4 * g + jj
                    slot = jj % 2
                    wblock(wsrc, j, slot)
                    b.wait("te", "gp", b.now("gp"))
                    for tc in range(4):
                        pj = psF[tc % 2]
                        b.need("te", f"ps{tc % 2}")
                        for i in range(NCH):
                            rh = xn_full[:, i * S + tc * 512:
                                         i * S + (tc + 1) * 512]
                            b.op("te", lambda e, p=pj, l=wtile(slot, i), r=rh,
                                 i=i: e.matmul(p[:, :], l, r, start=(i == 0),
                                               stop=(i == NCH - 1)),
                                 inc=("te" if i == NCH - 1 else None))
                        if kind == "v":
                            b.wait("ac", "te", b.now("te"))
                            b.need("ac", "vtmp")
                            b.op("ac", lambda e, p=pj: e.activation(
                                vtmp[:, :], p[:, :], AF.Copy), inc="ac")
                            b.setg(f"ps{tc % 2}", "ac")
                            b.wait("te", "ac", b.now("ac"))
                            for m in range(4):
                                tb = psT[m % 2]
                                b.need("te", f"tb{m % 2}")
                                b.op("te", lambda e, t=tb, m=m: e.transpose(
                                    t[:, 0:128], vtmp[:, m * 128:(m + 1) * 128],
                                    eyeb[:, :]), inc="te")
                                b.wait("ac", "te", b.now("te"))
                                kc = tc * 4 + m
                                dsl = v4[:, kc * 512 + jj * 128:
                                         kc * 512 + (jj + 1) * 128]
                                b.op("ac", lambda e, t=tb, d=dsl: e.activation(
                                    d, t[:, 0:128], AF.Copy), inc="ac")
                                b.setg(f"tb{m % 2}", "ac")
                            b.setg("vtmp", "te")
                        elif is_a:
                            b.wait("ac", "te", b.now("te"))
                            b.need("ac", "qraw")
                            b.op("ac", lambda e, p=pj: e.activation(
                                qraw[:, :], p[:, :], AF.Copy), inc="ac")
                            b.setg(f"ps{tc % 2}", "ac")
                            b.wait("te", "ac", b.now("ac"))
                            prot = psF[2 + tc % 2]
                            b.need("te", f"ps{2 + tc % 2}")
                            b.op("te", lambda e, p=prot: e.matmul(
                                p[:, :], rotmb[:, :], qraw[:, :],
                                start=True, stop=True), inc="te")
                            dsl = dbuf[:, jj * S + tc * 512:
                                       jj * S + (tc + 1) * 512]
                            csl = cosb[:, tc * 512:(tc + 1) * 512]
                            ssl = sinb[:, tc * 512:(tc + 1) * 512]
                            b.wait("dv", "te", b.now("te"))
                            b.op("dv", lambda e, d=dsl, c=csl:
                                 e.tensor_mul(d, qraw[:, :], c))
                            b.dvf(lambda e, p=prot, s=ssl:
                                  e.tensor_mul(w1t[:, :], p[:, :], s))
                            b.op("dv", lambda e, d=dsl:
                                 e.tensor_add(d, d, w1t[:, :]), inc="dv")
                            b.setg("qraw", "dv")
                            b.setg(f"ps{2 + tc % 2}", "dv")
                        else:
                            dsl = dbuf[:, jj * S + tc * 512:
                                       jj * S + (tc + 1) * 512]
                            b.wait("ac", "te", b.now("te"))
                            b.op("ac", lambda e, p=pj, d=dsl: e.activation(
                                d, p[:, :], AF.Copy), inc="ac")
                            b.setg(f"ps{tc % 2}", "ac")
                    b.setg(f"wslot{slot}", "te")
            if g == 1:
                b.setg("xn_full", "te")
            if which == "a" and g == 0:
                tap("q4", q4, 8192)
                tap("k4", k4, 8192)
                tap("v4", v4, 8192)

            b.wait("te", "ac", b.now("ac"))
            b.wait("te", "dv", b.now("dv"))
            for jj in range(4):
                head = 4 * g + jj
                for qc in range(4):
                    nkc = 4 * (qc + 1)
                    for half in range(nhalf):
                        base = 64 * half if is_a else 0
                        pvb = psF[4 + half]
                        zbank = psF[2 + half]
                        b.need("te", f"ps{4 + half}")
                        b.need("te", f"ps{2 + half}")
                        for kc in range(nkc):
                            pt = ptile[kc % 2]
                            sbk = psF[kc % 2]
                            lt = k4[:, jj * S + kc * 128:jj * S + (kc + 1) * 128]
                            rh = q4[:, jj * S + qc * 512:jj * S + (qc + 1) * 512]
                            b.need("te", f"ps{kc % 2}")
                            b.op("te", lambda e, p=sbk, l=lt, r=rh, b0=base,
                                 kd=kdim: e.matmul(p[:, :], l[b0:b0 + kd, :],
                                                   r[b0:b0 + kd, :],
                                                   start=True, stop=True),
                                 inc="te")
                            b.wait("ac", "te", b.now("te"))
                            b.need("ac", f"pt{kc % 2}")
                            b.op("ac", lambda e, p=sbk, t=pt: e.activation(
                                t[:, :], p[:, :], AF.Exp, scale=scale),
                                inc="ac")
                            b.setg(f"ps{kc % 2}", "ac")
                            m = kc - 4 * qc
                            if m >= 0:
                                b.wait("dv", "ac", b.now("ac"))
                                if m > 0:
                                    b.op("dv", lambda e, t=pt, m=m: e.memset(
                                        t[:, 0:m * 128], 0.0))
                                b.op("dv", lambda e, t=pt, m=m: e.tensor_mul(
                                    t[:, m * 128:(m + 1) * 128],
                                    t[:, m * 128:(m + 1) * 128], maskt[:, :]),
                                    inc="dv")
                                b.wait("te", "dv", b.now("dv"))
                            else:
                                b.wait("te", "ac", b.now("ac"))
                            vl = v4[:, kc * 512 + jj * 128:
                                    kc * 512 + (jj + 1) * 128]
                            if ("ptap" in TAPS and g == 0 and jj == 0
                                    and qc <= 1):
                                nm = f"p_{which}_q{qc}_h{half}_k{kc}"
                                tp = nc.dram_tensor(f"tap_{nm}", [128, 512],
                                                    BF16, kind="ExternalOutput")
                                tap_t[nm] = tp
                                b.wait("sy", "ac", b.now("ac"))
                                b.wait("sy", "dv", b.now("dv"))
                                b.dma("sy", tp.ap(), pt[:, :], "sy")
                                b.wait("te", "sy", b.now("sy"))
                            b.op("te", lambda e, p=pvb, v=vl, t=pt, kc=kc:
                                 e.matmul(p[:, :], v, t[:, :],
                                          start=(kc == 0), stop=(kc == nkc - 1)))
                            b.op("te", lambda e, p=zbank, t=pt, kc=kc:
                                 e.matmul(p[0:1, :], ones_c[:, 0:1], t[:, :],
                                          start=(kc == 0), stop=(kc == nkc - 1)),
                                 inc="te")
                            b.setg(f"pt{kc % 2}", "te")
                    b.wait("dv", "te", b.now("te"))
                    if "ptap" in TAPS and g == 0 and jj == 0 and qc <= 1:
                        nm = f"z_{which}_q{qc}"
                        tp = nc.dram_tensor(f"tap_{nm}", [1, 1024], F32,
                                            kind="ExternalOutput")
                        tap_t[nm] = tp
                        b.wait("ac", "te", b.now("te"))
                        b.op("ac", lambda e: e.activation(
                            rows[0:1, 1536:2048], psF[2][0:1, :], AF.Copy))
                        b.op("ac", lambda e: e.activation(
                            rows[0:1, 2048:2560], psF[3][0:1, :], AF.Copy),
                            inc="ac")
                        b.wait("sy", "ac", b.now("ac"))
                        b.dma("sy", tp.ap(), rows[0:1, 1536:2560], "sy")
                        b.wait("dv", "sy", b.now("sy"))
                    zsl = scr32[:, head * S + qc * 512:head * S + (qc + 1) * 512]
                    if is_a:
                        b.dvf(lambda e: e.reciprocal(rows[0:1, 0:512],
                                                     psF[2][0:1, :]))
                        b.dvf(lambda e: e.tensor_mul(
                            rows[0:1, 512:1024], psF[3][0:1, :],
                            rows[0:1, 0:512]))
                        b.op("dv", lambda e: e.tensor_copy(
                            rowsb[0:1, 0:512], rows[0:1, 512:1024]), inc="dv")
                        b.setg("ps2", "dv")
                        b.setg("ps3", "dv")
                        b.wait("te", "dv", b.now("dv"))
                        b.need("te", "ps1")
                        b.op("te", lambda e: e.matmul(
                            psF[1][:, :], ones_r[0:1, :], rowsb[0:1, 0:512],
                            start=True, stop=True), inc="te")
                        b.wait("ac", "te", b.now("te"))
                        b.op("ac", lambda e: e.activation(
                            zbt[:, 0:512], psF[1][:, :], AF.Copy), inc="ac")
                        b.setg("ps1", "ac")
                        b.wait("dv", "ac", b.now("ac"))
                        b.dvf(lambda e: e.tensor_mul(
                            w1t[:, :], psF[4][:, :], zbt[:, 0:512]))
                        b.op("dv", lambda e, z=zsl: e.scalar_tensor_tensor(
                            z, psF[5][:, :], lamn_c[:, 0:1], w1t[:, :],
                            ALU.mult, ALU.add), inc="dv")
                        if "ptap" in TAPS and g == 0 and jj == 0 and qc <= 1:
                            for nm, src_ in ((f"pv1_q{qc}", psF[4]),
                                             (f"pv2_q{qc}", psF[5])):
                                tp = nc.dram_tensor(f"tap_{nm}", [128, 512],
                                                    F32, kind="ExternalOutput")
                                tap_t[nm] = tp
                                b.wait("ac", "dv", b.now("dv"))
                                b.op("ac", lambda e, s=src_: e.activation(
                                    vtmp2[:, :], s[:, :], AF.Copy), inc="ac")
                                b.wait("sy", "ac", b.now("ac"))
                                b.dma("sy", tp.ap(), vtmp2[:, :], "sy")
                                b.wait("ac", "sy", b.now("sy"))
                            nm = f"w_q{qc}"
                            tp = nc.dram_tensor(f"tap_{nm}", [128, 512], BF16,
                                                kind="ExternalOutput")
                            tap_t[nm] = tp
                            b.wait("sy", "dv", b.now("dv"))
                            b.dma("sy", tp.ap(), zsl, "sy")
                            b.wait("dv", "sy", b.now("sy"))
                            nm = f"r12_q{qc}"
                            tp = nc.dram_tensor(f"tap_{nm}", [1, 512], F32,
                                                kind="ExternalOutput")
                            tap_t[nm] = tp
                            b.dma("sy", tp.ap(), rows[0:1, 512:1024], "sy")
                        b.setg("ps4", "dv")
                        b.setg("ps5", "dv")
                    else:
                        b.dvf(lambda e: e.reciprocal(rows[0:1, 0:512],
                                                     psF[2][0:1, :]))
                        b.op("dv", lambda e: e.tensor_copy(
                            rowsb[0:1, 0:512], rows[0:1, 0:512]), inc="dv")
                        b.setg("ps2", "dv")
                        b.wait("te", "dv", b.now("dv"))
                        b.need("te", "ps1")
                        b.op("te", lambda e: e.matmul(
                            psF[1][:, :], ones_r[0:1, :], rowsb[0:1, 0:512],
                            start=True, stop=True), inc="te")
                        b.wait("ac", "te", b.now("te"))
                        b.op("ac", lambda e: e.activation(
                            zbt[:, 0:512], psF[1][:, :], AF.Copy), inc="ac")
                        b.setg("ps1", "ac")
                        b.wait("dv", "ac", b.now("ac"))
                        b.op("dv", lambda e, z=zsl: e.tensor_mul(
                            z, psF[4][:, :], zbt[:, 0:512]), inc="dv")
                        b.setg("ps4", "dv")
                if is_a:
                    for seg in range(2):
                        hsl = scr32[:, head * S + seg * TOK:
                                    head * S + (seg + 1) * TOK]
                        b.need("ac", "sq1")
                        b.wait("ac", "dv", b.now("dv"))
                        b.op("ac", lambda e, h=hsl: e.activation(
                            sq1[:, :], h, AF.Square), inc="ac")
                        b.wait("te", "ac", b.now("ac"))
                        for sc in range(2):
                            pb = psF[2 + sc]
                            b.need("te", f"ps{2 + sc}")
                            b.op("te", lambda e, p=pb, sc=sc: e.matmul(
                                p[0:1, :], ones_c[:, 0:1],
                                sq1[:, sc * 512:(sc + 1) * 512],
                                start=True, stop=True), inc="te")
                        b.setg("sq1", "te")
                        b.wait("ac", "te", b.now("te"))
                        for sc in range(2):
                            b.op("ac", lambda e, sc=sc, seg=seg: e.activation(
                                rows[0:1, seg * 1024 + sc * 512:
                                     seg * 1024 + (sc + 1) * 512],
                                psF[2 + sc][0:1, :], AF.Copy, scale=1.0 / 128),
                                inc="ac")
                        b.setg("ps2", "ac")
                        b.setg("ps3", "ac")
                    b.op("ac", lambda e: e.activation(
                        rows[0:1, 0:2048], rows[0:1, 0:2048], AF.Ln,
                        bias=sca[0:1, 7:8]))
                    b.op("ac", lambda e: e.activation(
                        rows[0:1, 0:2048], rows[0:1, 0:2048], AF.Exp,
                        scale=-0.5), inc="ac")
                    b.wait("dv", "ac", b.now("ac"))
                    b.op("dv", lambda e: e.tensor_copy(rowsb[0:1, 0:2048],
                                                       rows[0:1, 0:2048]),
                         inc="dv")
                    b.wait("te", "dv", b.now("dv"))
                    for sc in range(4):
                        pb = psF[2 + sc % 2]
                        b.need("te", f"ps{2 + sc % 2}")
                        b.op("te", lambda e, p=pb, sc=sc: e.matmul(
                            p[:, :], ones_r[0:1, :],
                            rowsb[0:1, sc * 512:(sc + 1) * 512],
                            start=True, stop=True), inc="te")
                        b.wait("ac", "te", b.now("te"))
                        b.op("ac", lambda e, p=pb, sc=sc: e.activation(
                            zbt[:, (sc % 2) * 512:(sc % 2) * 512 + 512],
                            p[:, :], AF.Copy), inc="ac")
                        b.setg(f"ps{2 + sc % 2}", "ac")
                        b.wait("dv", "ac", b.now("ac"))
                        hsl = scr32[:, head * S + sc * 512:
                                    head * S + (sc + 1) * 512]
                        b.op("dv", lambda e, h=hsl, sc=sc:
                             e.tensor_mul(h, h, zbt[:, (sc % 2) * 512:
                                                     (sc % 2) * 512 + 512]),
                             inc="dv")
                        b.wait("dv", "dv")
                    b.wait("dv", "dv")
                    for q2 in range(4):
                        hsl = scr32[:, head * S + q2 * 512:
                                    head * S + (q2 + 1) * 512]
                        b.op("dv", lambda e, h=hsl: e.tensor_scalar(
                            h, h, lnw_eff[:, 0:1], lnb_eff[:, 0:1],
                            ALU.mult, ALU.add), inc="dv")

        if which == "a":
            tap("zbuf", scr32, NCH * TOK)
        agi, ago = agi2[which], ago2[which]
        b.wait("te", "ac", b.now("ac"))
        b.wait("te", "dv", b.now("dv"))
        b.wait("ac", "te", b.now("te"))
        for oc in range(NCH):
            slot = oc % 2
            wblock(wout, oc, slot, rows_n=8)
            b.wait("te", "gp", b.now("gp"))
            psl = par_sl[oc % 2]
            b.need("ac", f"par{oc % 2}")
            for tc in range(4):
                pj = psF[tc % 2]
                b.need("te", f"ps{tc % 2}")
                for i in range(8):
                    rh = scr32[:, i * S + tc * 512:i * S + (tc + 1) * 512]
                    b.op("te", lambda e, p=pj, l=wtile(slot, i), r=rh, i=i:
                         e.matmul(p[:, :], l, r, start=(i == 0),
                                  stop=(i == 7)),
                         inc=("te" if i == 7 else None))
                b.wait("ac", "te", b.now("te"))
                b.op("ac", lambda e, p=pj, d=psl[:, tc * 512:(tc + 1) * 512]:
                     e.activation(d, p[:, :], AF.Copy), inc="ac")
                b.setg(f"ps{tc % 2}", "ac")
            b.setg(f"wslot{slot}", "te")
            b.wait("sy", "ac", b.now("ac"))
            for hh in (0, 1):
                b.dma("sy",
                      agi.ap()[hh * D + oc * 128:hh * D + (oc + 1) * 128, :],
                      psl[:, hh * TOK:(hh + 1) * TOK], "sy")
            b.setg(f"par{oc % 2}", "sy")
        b.wait("gp", "sy", b.now("sy"))
        b.op("gp", lambda e, i=agi, o=ago: e.collective_compute(
            "AllGather", ALU.bypass, replica_groups=[list(range(8))],
            ins=[i.ap().opt()], outs=[o.ap().opt()]), inc=cc2)
        b.wait("gp", cc2, 1)

        if "par" in TAPS and which == "a":
            tpa = nc.dram_tensor("tap_agi2", [2 * D, TOK], BF16,
                                 kind="ExternalOutput")
            tap_t["agi2"] = tpa
            b.dma("gp", tpa.ap(), agi.ap(), "gp")
        stream_update(which, xsrc, xdst, fuse_stats, last, ago)

    def stream_update(which, xsrc, xdst, fuse_stats, last, ago):
        pl = parloc[which]
        for r in (0, 1):
            dstrot = bass.AP(pl, r * D * TOK, [[1, D * TOK]])
            if r == 0:
                srcrot = bass.AP(ago, r_parb, [[1, D * TOK]])
            else:
                b.op("gp", lambda e: e.reg_mov(r_tmp, 2 * D * TOK))
                b.op("gp", lambda e: e.reg_add(r_tmp, r_tmp, r_parb))
                srcrot = bass.AP(ago, r_tmp, [[1, D * TOK]])
            b.op("gp", lambda e, d=dstrot, s=srcrot: e.dma_start(out=d, in_=s),
                 inc="gp", n=16)
        if "par" in TAPS and which == "a":
            tpo = nc.dram_tensor("tap_ago2", [4 * D, TOK], BF16,
                                 kind="ExternalOutput")
            tap_t["ago2"] = tpo
            b.dma("gp", tpo.ap(), ago.ap()[0:4 * D, :], "gp")
            tpp = nc.dram_tensor("tap_parloc", [2 * D, TOK], BF16,
                                 kind="ExternalOutput")
            tap_t["parloc"] = tpp
            b.dma("gp", tpp.ap(), pl.ap(), "gp")
        for i in range(NCH):
            pob = po_sl[i % 2]
            b.need("gp", f"po{i % 2}")
            srcap = bass.AP(pl, i * 128 * TOK,
                            [[TOK, 128], [D * TOK, 2], [1, TOK]])
            dstap = pob.rearrange("p (r c) -> p r c", c=TOK)
            b.op("gp", lambda e, d=dstap, s=srcap: e.dma_start(out=d, in_=s),
                 inc="gp", n=16)
            xc = xch[i % 2]
            b.need("sy", f"xch{i % 2}")
            b.dma("sy", xc[:, :], xsrc.ap()[i * 128:(i + 1) * 128, :], "sy")
            b.wait("dv", "gp", b.now("gp"))
            b.wait("dv", "sy", b.now("sy"))
            for tc in range(2):
                sl = slice(tc * 512, (tc + 1) * 512)
                sl2 = slice(1024 + tc * 512, 1024 + (tc + 1) * 512)
                b.dvf(lambda e, p=pob, sl=sl, sl2=sl2:
                      e.tensor_add(w1t[:, :], p[:, sl], p[:, sl2]))
                b.op("dv", lambda e, x=xc, sl=sl:
                     e.tensor_add(x[:, sl], x[:, sl], w1t[:, :]), inc="dv")
                b.wait("dv", "dv")
            b.setg(f"po{i % 2}", "dv")
            if fuse_stats:
                dstn = scr32[:, i * TOK:(i + 1) * TOK]
                b.op("dv", lambda e, s=xc, d=dstn: e.tensor_copy(d, s[:, :]),
                     inc="dv")
                b.wait("ac", "dv", b.now("dv"))
                stats_chunk(i, dstn)
            b.wait("sy", "dv", b.now("dv"))
            b.dma("sy", xdst.ap()[i * 128:(i + 1) * 128, :], xc[:, :], "sy")
            b.setg(f"xch{i % 2}", "sy")

    def ffn_sub(xsrc, xdst, fuse_stats, last):
        b.wait("dv", "ac", b.now("ac"))
        for i in range(NCH):
            for tc in range(2):
                s_ = scr32[:, i * TOK + tc * 512:i * TOK + (tc + 1) * 512]
                b.dvf(lambda e, s=s_: e.tensor_scalar(
                    f8t[:, :], s, 240.0, -240.0, ALU.min, ALU.max))
                b.op("dv", lambda e, s=s_: e.tensor_copy(s, f8t[:, :]),
                     inc="dv")
                b.wait("dv", "dv")
        for oc in range(NCH):
            slot = oc % 2
            dst8 = w8p.ap()[:, slot * 2048:(slot + 1) * 2048].rearrange(
                "p (i c) -> p i c", c=128)
            src8 = w8t.ap()[:, oc * 128:(oc + 1) * 128].rearrange(
                "(i p) c -> p i c", p=128)
            b.need("sy", f"w8s{slot}")
            b.dma("sy", dst8, src8, "sy")
            b.wait("dv", "sy", b.now("sy"))
            b.need("dv", f"wslot{slot}")
            b.op("dv", lambda e, s=slot: e.tensor_copy(
                wpool[:, s * 2048:(s + 1) * 2048],
                w8p[:, s * 2048:(s + 1) * 2048]), inc="dv")
            b.setg(f"w8s{slot}", "dv")
            b.wait("te", "dv", b.now("dv"))
            xc = xch[oc % 2]
            b.need("sy", f"xch{oc % 2}")
            b.dma("sy", xc[:, :], xsrc.ap()[oc * 128:(oc + 1) * 128, :], "sy")
            for tc in range(2):
                pj = psF[tc % 2]
                b.need("te", f"ps{tc % 2}")
                for i in range(NCH):
                    rh = scr32[:, i * TOK + tc * 512:i * TOK + (tc + 1) * 512]
                    b.op("te", lambda e, p=pj, l=wtile(slot, i), r=rh, i=i:
                         e.matmul(p[:, :], l, r, start=(i == 0),
                                  stop=(i == NCH - 1)),
                         inc=("te" if i == NCH - 1 else None))
            b.setg(f"wslot{slot}", "te")
            b.wait("dv", "te", b.now("te"))
            b.wait("dv", "sy", b.now("sy"))
            for tc in range(2):
                pj = psF[tc % 2]
                sl = slice(tc * 512, (tc + 1) * 512)
                b.op("dv", lambda e, p=pj, x=xc, sl=sl:
                     e.scalar_tensor_tensor(x[:, sl], p[:, :], alpha_c[:, 0:1],
                                            x[:, sl], ALU.mult, ALU.add),
                     inc="dv")
                b.setg(f"ps{tc % 2}", "dv")
            b.wait("sy", "dv", b.now("dv"))
            b.dma("sy", xdst.ap()[oc * 128:(oc + 1) * 128, :], xc[:, :], "sy")
            b.setg(f"xch{oc % 2}", "sy")
        if fuse_stats:
            for i in range(NCH):
                xc = xch[i % 2]
                b.need("sy", f"xch{i % 2}")
                b.dma("sy", xc[:, :], xdst.ap()[i * 128:(i + 1) * 128, :], "sy")
                b.wait("dv", "sy", b.now("sy"))
                dstn = scr32[:, i * TOK:(i + 1) * TOK]
                b.op("dv", lambda e, s=xc, d=dstn: e.tensor_copy(d, s[:, :]),
                     inc="dv")
                b.setg(f"xch{i % 2}", "dv")
                b.wait("ac", "dv", b.now("dv"))
                stats_chunk(i, dstn)

    first_stats()
    ln_finish(ffn=False)
    tap("xnw1", scr32, NCH * TOK)
    attn_sub("a", x_in, xbufA, fuse_stats=(n_sub >= 2), last=(n_sub == 1))
    if n_sub >= 2:
        ln_finish(ffn=True)
        ffn_sub(xbufA, xbufB, fuse_stats=(n_sub >= 3), last=(n_sub == 2))
    if n_sub >= 3:
        ln_finish(ffn=False)
        attn_sub("b", xbufB, xbufA, fuse_stats=(n_sub >= 4), last=(n_sub == 3))
    if n_sub >= 4:
        ln_finish(ffn=True)
        ffn_sub(xbufA, out_fm, fuse_stats=False, last=True)

    fin = {1: xbufA, 2: xbufB, 3: xbufA}.get(n_sub)
    if fin is not None and n_sub < 4:
        b.wait("gp", "sy", b.now("sy"))
        b.dma("gp", out_fm.ap(), fin.ap(), "gp")

    b.wait("sy", "sy", b.now("sy"))
    b.wait("sy", "gp", b.now("gp"))

    with nc.Block() as block:
        for eng_name, attr in (("te", "tensor"), ("dv", "vector"),
                               ("ac", "scalar"), ("gp", "gpsimd"),
                               ("sy", "sync")):
            ops = b.q[eng_name]

            def runner(e, ops=ops):
                for f in ops:
                    f(e)
            getattr(block, attr)(runner)

    b.stack.close()
    return nc


_NC_CACHE = {}


def _get_nc():
    key = N_SUB
    if key not in _NC_CACHE:
        _NC_CACHE[key] = build_nc(key)
    return _NC_CACHE[key]


def _rope_tables():
    dd = 64
    inv = (1.0 / np.float32(10000.0) **
           (np.arange(0, dd, 2, dtype=np.float32) / np.float32(dd)))
    t = np.arange(S, dtype=np.float32)
    freqs = (t[:, None] * inv[None, :]).astype(np.float32)
    freqs = np.repeat(freqs, 2, axis=1)
    cos = np.cos(freqs).astype(np.float32)
    sin = np.sin(freqs).astype(np.float32)
    cosf = np.concatenate([cos.T, cos.T], axis=0)
    sinf = np.concatenate([sin.T, sin.T], axis=0)
    return (np.ascontiguousarray(cosf).astype(ml_dtypes.bfloat16),
            np.ascontiguousarray(sinf).astype(ml_dtypes.bfloat16))


def _host_prep(inputs):
    x = np.asarray(inputs["L0_attn_a_x"])
    Wq = np.asarray(inputs["L0_attn_a_Wq"])
    Wk = np.asarray(inputs["L0_attn_a_Wk"])
    Wv = np.asarray(inputs["L0_attn_a_Wv"])
    Wo = np.asarray(inputs["L0_attn_a_Wo"])
    WK = np.asarray(inputs["L0_attn_b_W_K"])
    WQ = np.asarray(inputs["L0_attn_b_W_Q"])
    WV = np.asarray(inputs["L0_attn_b_W_V"])
    WO = np.asarray(inputs["L0_attn_b_W_O"])
    w8 = np.asarray(inputs["L0_ffn_weight"])
    wsc = np.float32(inputs["L0_ffn_weight_scale"])
    isc = np.float32(inputs["L0_ffn_input_scale"])
    lam = np.concatenate([np.asarray(inputs["L0_attn_a_lambda_q1"]),
                          np.asarray(inputs["L0_attn_a_lambda_k1"]),
                          np.asarray(inputs["L0_attn_a_lambda_q2"]),
                          np.asarray(inputs["L0_attn_a_lambda_k2"])]
                         ).astype(np.float32).reshape(1, 256)
    lnw = np.asarray(inputs["L0_attn_a_ln_w"]).astype(np.float32).reshape(128, 1)
    lnb = np.asarray(inputs["L0_attn_a_ln_b"]).astype(np.float32).reshape(128, 1)

    cosb, sinb = _rope_tables()
    rot = np.zeros((128, 128), dtype=np.float32)
    for p in range(0, 128, 2):
        rot[p + 1, p] = -1.0
        rot[p, p + 1] = 1.0
    rot = np.ascontiguousarray(rot).astype(ml_dtypes.bfloat16)
    tri = np.ascontiguousarray(
        np.triu(np.ones((128, 128), dtype=np.float32))).astype(
        ml_dtypes.bfloat16)
    eye = np.ascontiguousarray(np.eye(128, dtype=np.float32)).astype(
        ml_dtypes.bfloat16)

    w8t = np.ascontiguousarray(w8.T).view(ml_dtypes.float8_e4m3)

    in_maps = []
    for c in range(8):
        bb, h = c >> 1, c & 1
        tok = slice(1024 * h, 1024 * h + 1024)
        qk = slice(1024 * h, 1024 * h + 1024)
        heads = slice(8 * h, 8 * h + 8)
        x_fm = np.ascontiguousarray(x[bb, tok, :].T)
        wbk_c = np.ascontiguousarray(
            WK[heads].transpose(2, 0, 1).reshape(D, 1024))
        wbq_c = np.ascontiguousarray(
            WQ[heads].transpose(2, 0, 1).reshape(D, 1024))
        wbv_c = np.ascontiguousarray(
            WV[heads].transpose(2, 0, 1).reshape(D, 1024))
        wbo_c = np.ascontiguousarray(WO[:, qk].T)
        offs = np.array([[(c & ~1) * D * TOK,
                          (c & ~1) * 2 * D * TOK + h * D * TOK, 0, 0]],
                        dtype=np.int32)
        in_maps.append({
            "x_in": x_fm,
            "wq": np.ascontiguousarray(Wq[:, qk]),
            "wk": np.ascontiguousarray(Wk[:, qk]),
            "wv": np.ascontiguousarray(Wv[:, qk]),
            "wo": np.ascontiguousarray(Wo[qk, :]),
            "wbk": wbk_c, "wbq": wbq_c, "wbv": wbv_c, "wbo": wbo_c,
            "w8t": w8t,
            "cos_in": cosb, "sin_in": sinb, "rot_in": rot,
            "tri_in": tri,
            "eye_in": eye,
            "lam_in": lam, "lnw_in": lnw, "lnb_in": lnb,
            "wsc_in": wsc.reshape(1, 1),
            "isc_in": isc.reshape(1, 1),
            "offs_in": offs,
        })
    return in_maps


def kernel(**inputs):
    in_maps = _host_prep(inputs)
    nc = _get_nc()
    res = run_bass_kernel_spmd(nc, in_maps, core_ids=list(range(8)))
    out = np.empty((4, S, D), dtype=np.float32)
    for c in range(8):
        bb, h = c >> 1, c & 1
        out[bb, 1024 * h:1024 * h + 1024, :] = res.results[c]["out_fm"].T
    if TAPS:
        kernel.tap_results = [
            {k: v for k, v in r.items() if k.startswith("tap_")}
            for r in res.results]
    return out


# version 03
# speedup vs baseline: 1.2780x; 1.1838x over previous
import contextlib
import math

import numpy as np
import ml_dtypes

import concourse.bass as bass
import concourse.mybir as mybir
from concourse.bass_utils import run_bass_kernel_spmd

F32 = mybir.dt.float32
BF16 = mybir.dt.bfloat16
FP8 = mybir.dt.float8e4
I32 = mybir.dt.int32
AF = mybir.ActivationFunctionType
ALU = mybir.AluOpType
AXX = mybir.AxisListType.X

D = 2048
S = 2048
TOK = 1024
NCH = 16
HEADS = 8
LAMBDA_INIT = 0.8 - 0.6 * math.exp(-0.3 * 12)
EPS_LN = 1e-5
EPS_RMS = 1e-8
SCALE_A = 64.0 ** -0.5
SCALE_B = 128.0 ** -0.5
N_SUB = 4
TAPS = ()

_ENGS = ("te", "dv", "ac", "gp", "sy")


class _B:

    def __init__(self, nc):
        self.nc = nc
        self.q = {e: [] for e in _ENGS}
        self.cnt = {}
        self.sems = {}
        self.stack = contextlib.ExitStack()
        self._lastwait = {e: {} for e in _ENGS}
        self.guard = {}

    def sem(self, name):
        self.sems[name] = self.stack.enter_context(self.nc.semaphore(name))
        self.cnt[name] = 0

    def wait(self, eng, sem, val=None):
        v = self.cnt[sem] if val is None else val
        if v <= 0:
            return
        if self._lastwait[eng].get(sem, -1) >= v:
            return
        self._lastwait[eng][sem] = v
        h = self.sems[sem]
        self.q[eng].append(lambda e, h=h, v=v: e.wait_ge(h, v))

    def op(self, eng, fn, inc=None, n=1):
        if inc is None:
            self.q[eng].append(fn)
        else:
            h = self.sems[inc]
            self.q[eng].append(lambda e, fn=fn, h=h, n=n: fn(e).then_inc(h, n))
            self.cnt[inc] += n

    def dma(self, eng, out, in_, inc):
        self.op(eng, lambda e, o=out, i=in_: e.dma_start(out=o, in_=i),
                inc=inc, n=16)

    def now(self, sem):
        return self.cnt[sem]

    def need(self, eng, key):
        g = self.guard.get(key)
        if g:
            self.wait(eng, g[0], g[1])

    def setg(self, key, sem):
        self.guard[key] = (sem, self.cnt[sem])

    def dvf(self, fn):
        self.op("dv", fn, inc="dv")
        self.wait("dv", "dv")


def build_nc(n_sub=N_SUB):
    nc = bass.Bass("TRN2", target_bir_lowering=False)
    b = _B(nc)
    st = b.stack
    sb = lambda name, shape, dt: st.enter_context(
        nc.sbuf_tensor(name, list(shape), dt))

    def din(name, shape, dt):
        return nc.dram_tensor(name, list(shape), dt, kind="ExternalInput")

    x_in = din("x_in", [D, TOK], F32)
    wq = din("wq", [D, TOK], F32)
    wk = din("wk", [D, TOK], F32)
    wv = din("wv", [D, TOK], F32)
    wo = din("wo", [TOK, D], F32)
    wbk = din("wbk", [D, TOK], F32)
    wbq = din("wbq", [D, TOK], F32)
    wbv = din("wbv", [D, TOK], F32)
    wbo = din("wbo", [TOK, D], F32)
    w8t = din("w8t", [D, D], FP8)
    cos_in = din("cos_in", [128, S], BF16)
    sin_in = din("sin_in", [128, S], BF16)
    rot_in = din("rot_in", [128, 128], BF16)
    tri_in = din("tri_in", [128, 128], BF16)
    eye_in = din("eye_in", [128, 128], BF16)
    lam_in = din("lam_in", [1, 256], F32)
    lnw_in = din("lnw_in", [128, 1], F32)
    lnb_in = din("lnb_in", [128, 1], F32)
    wsc_in = din("wsc_in", [1, 1], F32)
    isc_in = din("isc_in", [1, 1], F32)
    offs_in = din("offs_in", [1, 4], I32)

    out_fm = nc.dram_tensor("out_fm", [D, TOK], F32, kind="ExternalOutput")

    xbufA = nc.dram_tensor("xbufA", [D, TOK], F32)
    xbufB = nc.dram_tensor("xbufB", [D, TOK], F32)
    agi1 = {a: nc.dram_tensor(f"ag1{a}_in", [D, TOK], BF16) for a in "ab"}
    ago1 = {a: nc.dram_tensor(f"ag1{a}_out", [8 * D, TOK], BF16,
                              addr_space="Shared") for a in "ab"}
    agi2 = {a: nc.dram_tensor(f"ag2{a}_in", [2 * D, TOK], BF16) for a in "ab"}
    ago2 = {a: nc.dram_tensor(f"ag2{a}_out", [16 * D, TOK], BF16,
                              addr_space="Shared") for a in "ab"}
    xnloc = {a: nc.dram_tensor(f"xnloc{a}", [2 * D, TOK], BF16) for a in "ab"}
    parloc = {a: nc.dram_tensor(f"parloc{a}", [2 * D, TOK], BF16) for a in "ab"}

    xn_full = sb("xn_full", [128, NCH * S], BF16)
    scr32 = sb("scr32", [128, NCH * TOK], BF16)
    ab16 = sb("ab16", [128, 8192], BF16)
    k4 = sb("k4", [128, 8192], BF16)
    v4 = sb("v4", [128, 8192], BF16)
    cs16 = sb("cs16", [128, 4096], BF16)
    wpool = sb("wpool", [128, 4096], BF16)
    w8p = sb("w8p", [128, 4096], FP8)
    xch = [sb(f"xch{i}", [128, TOK], F32) for i in (0, 1)]
    sq1 = sb("sq1", [128, TOK], BF16)
    ptile = [sb(f"ptile{i}", [128, 512], BF16) for i in (0, 1)]
    qraw = sb("qraw", [128, 512], BF16)
    vtmp = sb("vtmp", [128, 512], BF16)
    f8t = sb("f8t", [128, 512], FP8)
    vtmp2 = sb("vtmp2", [128, 512], F32)
    zbt = sb("zbt", [128, 1024], BF16)
    w1t = sb("w1t", [128, 512], F32)
    rows = sb("rows", [1, 2560], F32)
    rowsb = sb("rowsb", [1, 2048], BF16)
    maskt = sb("maskt", [128, 128], BF16)
    rotmb = sb("rotmb", [128, 128], BF16)
    eyeb = sb("eyeb", [128, 128], BF16)
    ones_c = sb("ones_c", [128, 1], BF16)
    ones_r = sb("ones_r", [1, 128], BF16)
    lamv = sb("lamv", [1, 256], F32)
    lnw = sb("lnw", [128, 1], F32)
    lnb = sb("lnb", [128, 1], F32)
    lnw_eff = sb("lnw_eff", [128, 1], F32)
    lnb_eff = sb("lnb_eff", [128, 1], F32)
    sca = sb("sca", [1, 8], F32)
    onef = sb("onef", [1, 128], F32)
    alpha_c = sb("alpha_c", [128, 1], F32)
    lamn_c = sb("lamn_c", [128, 1], F32)
    offs_sb = sb("offs_sb", [1, 4], I32)

    mb_t = ab16[:, 0:1024]
    rb_t = ab16[:, 1024:2048]
    q4 = ab16
    par_sl = [ab16[:, 0:2048], ab16[:, 2048:4096]]
    cosb = cs16[:, 0:2048]
    sinb = cs16[:, 2048:4096]
    po_sl = [cs16[:, 0:2048], cs16[:, 2048:4096]]

    psF = [st.enter_context(nc.psum_tensor(f"ps{i}", [128, 512], F32))
           for i in range(6)]
    psT = [st.enter_context(nc.psum_tensor(f"tb{i}", [128, 1024], BF16))
           for i in range(2)]

    for s_name in ("sy", "gp", "te", "dv", "ac",
                   "cca1", "cca2", "ccb1", "ccb2"):
        b.sem(s_name)

    tap_t = {}

    def tap(point, buf, cols, waits=("dv", "ac", "te")):
        if point not in TAPS:
            return
        t = nc.dram_tensor(f"tap_{point}", [128, cols], BF16,
                           kind="ExternalOutput")
        tap_t[point] = t
        for w in waits:
            b.wait("sy", w, b.now(w))
        b.dma("sy", t.ap(), buf[:, 0:cols], "sy")

    def tapf(point, buf, cols, waits=("dv", "ac", "te")):
        if point not in TAPS:
            return
        t = nc.dram_tensor(f"tap_{point}", [128, cols], F32,
                           kind="ExternalOutput")
        tap_t[point] = t
        for w in waits:
            b.wait("sy", w, b.now(w))
        b.dma("sy", t.ap(), buf[:, 0:cols], "sy")

    r_xnb = st.enter_context(nc.gpsimd.register("r_xnb"))
    r_parb = st.enter_context(nc.gpsimd.register("r_parb"))
    r_tmp = st.enter_context(nc.gpsimd.register("r_tmp"))

    for src, dst in ((cos_in, cosb), (sin_in, sinb), (rot_in, rotmb[:, :]),
                     (tri_in, maskt[:, :]), (eye_in, eyeb[:, :]),
                     (lam_in, lamv[:, :]), (lnw_in, lnw[:, :]),
                     (lnb_in, lnb[:, :]), (offs_in, offs_sb[:, :])):
        b.dma("sy", dst, src.ap(), "sy")
    b.dma("sy", sca[0:1, 0:1], wsc_in.ap(), "sy")
    b.dma("sy", sca[0:1, 1:2], isc_in.ap(), "sy")

    b.wait("dv", "sy", b.now("sy"))
    b.op("dv", lambda e: e.memset(ones_c[:, :], 1.0))
    b.op("dv", lambda e: e.memset(sca[0:1, 6:7], EPS_LN))
    b.op("dv", lambda e: e.memset(sca[0:1, 7:8], EPS_RMS))
    b.op("dv", lambda e: e.memset(ones_r[:, :], 1.0))
    b.op("dv", lambda e: e.memset(onef[:, :], 1.0))
    b.op("dv", lambda e: e.tensor_mul(rows[0:1, 0:64], lamv[0:1, 0:64],
                                      lamv[0:1, 64:128]))
    b.dvf(lambda e: e.tensor_mul(rows[0:1, 64:128], lamv[0:1, 128:192],
                                 lamv[0:1, 192:256]))
    b.op("dv", lambda e: e.reduce_sum(rows[0:1, 128:129], rows[0:1, 0:64], axis=AXX))
    b.op("dv", lambda e: e.reduce_sum(rows[0:1, 129:130], rows[0:1, 64:128],
                                      axis=AXX), inc="dv")
    b.wait("ac", "dv", b.now("dv"))
    b.op("ac", lambda e: e.activation(rows[0:1, 130:132], rows[0:1, 128:130],
                                      AF.Exp), inc="ac")
    b.wait("dv", "ac", b.now("ac"))
    b.dvf(lambda e: e.tensor_sub(rows[0:1, 132:133], rows[0:1, 130:131],
                                 rows[0:1, 131:132]))
    b.dvf(lambda e: e.tensor_scalar_add(sca[0:1, 4:5], rows[0:1, 132:133],
                                        LAMBDA_INIT))
    b.dvf(lambda e: e.tensor_scalar_mul(sca[0:1, 5:6], sca[0:1, 4:5], -1.0))
    b.dvf(lambda e: e.reciprocal(sca[0:1, 2:3], sca[0:1, 1:2]))
    b.op("dv", lambda e: e.tensor_mul(sca[0:1, 3:4], sca[0:1, 0:1], sca[0:1, 1:2]),
         inc="dv")
    b.wait("ac", "dv", b.now("dv"))
    b.op("ac", lambda e: e.activation(lnw_eff[:, :], lnw[:, :], AF.Copy,
                                      scale=1.0 - LAMBDA_INIT))
    b.op("ac", lambda e: e.activation(lnb_eff[:, :], lnb[:, :], AF.Copy,
                                      scale=1.0 - LAMBDA_INIT), inc="ac")
    b.wait("te", "dv", b.now("dv"))
    b.op("te", lambda e: e.matmul(psF[4][:, 0:1], onef[0:1, :], sca[0:1, 3:4],
                                  start=True, stop=True))
    b.op("te", lambda e: e.matmul(psF[4][:, 1:2], onef[0:1, :], sca[0:1, 5:6],
                                  start=True, stop=True), inc="te")
    b.wait("ac", "te", b.now("te"))
    b.op("ac", lambda e: e.activation(alpha_c[:, :], psF[4][:, 0:1], AF.Copy))
    b.op("ac", lambda e: e.activation(lamn_c[:, :], psF[4][:, 1:2], AF.Copy),
         inc="ac")
    b.setg("ps4", "ac")

    if "setup" in TAPS:
        tp = nc.dram_tensor("tap_sca", [1, 8], F32, kind="ExternalOutput")
        tap_t["sca"] = tp
        tp2 = nc.dram_tensor("tap_cols", [2, 128], F32, kind="ExternalOutput")
        tap_t["cols"] = tp2
        tp3 = nc.dram_tensor("tap_rows0", [1, 256], F32, kind="ExternalOutput")
        tap_t["rows0"] = tp3
        b.wait("sy", "ac", b.now("ac"))
        b.wait("sy", "dv", b.now("dv"))
        b.dma("sy", tp.ap(), sca[0:1, :], "sy")
        b.dma("sy", tp2.ap()[0:1, :].rearrange("a p -> p a"), alpha_c[:, :], "sy")
        b.dma("sy", tp2.ap()[1:2, :].rearrange("a p -> p a"), lamn_c[:, :], "sy")
        b.dma("sy", tp3.ap(), rows[0:1, 0:256], "sy")
    b.wait("gp", "sy", b.now("sy"))
    b.op("gp", lambda e: e.reg_load(r_xnb, offs_sb[0:1, 0:1]))
    b.op("gp", lambda e: e.reg_load(r_parb, offs_sb[0:1, 1:2]))

    def stats_chunk(i, src_bf):
        b.need("ac", "sq1")
        b.op("ac", lambda e, s=src_bf: e.activation(sq1[:, :], s, AF.Square),
             inc="ac")
        b.wait("te", "ac", b.now("ac"))
        for tc in range(2):
            sl = slice(tc * 512, (tc + 1) * 512)
            if i == 0:
                b.need("te", f"ps{2 + tc}")
                b.need("te", f"ps{4 + tc}")
            b.op("te", lambda e, s=src_bf, sl=sl, tc=tc, i=i:
                 e.matmul(psF[2 + tc][0:1, :], ones_c[:, 0:1], s[:, sl],
                          start=(i == 0), stop=(i == NCH - 1)))
            b.op("te", lambda e, sl=sl, tc=tc, i=i:
                 e.matmul(psF[4 + tc][0:1, :], ones_c[:, 0:1], sq1[:, sl],
                          start=(i == 0), stop=(i == NCH - 1)),
                 inc=("te" if (i == NCH - 1 and tc == 1) else None))
        b.setg("sq1", "te")

    def ln_finish(ffn):
        b.wait("ac", "te", b.now("te"))
        for tc in range(2):
            b.op("ac", lambda e, tc=tc: e.activation(
                rows[0:1, tc * 512:(tc + 1) * 512], psF[2 + tc][0:1, :],
                AF.Copy, scale=1.0 / D))
            b.op("ac", lambda e, tc=tc: e.activation(
                rows[0:1, 1024 + tc * 512:1024 + (tc + 1) * 512],
                psF[4 + tc][0:1, :], AF.Copy, scale=1.0 / D), inc="ac")
        for i in (2, 3, 4, 5):
            b.setg(f"ps{i}", "ac")
        b.wait("dv", "ac", b.now("ac"))
        for tc in range(2):
            m_h = rows[0:1, tc * 512:(tc + 1) * 512]
            e_h = rows[0:1, 1024 + tc * 512:1024 + (tc + 1) * 512]
            t_h = rows[0:1, 2048:2560]
            b.dvf(lambda e, m=m_h, t=t_h: e.tensor_mul(t, m, m))
            b.op("dv", lambda e, eh=e_h, t=t_h: e.tensor_sub(eh, eh, t),
                 inc="dv")
            b.wait("dv", "dv")
        b.wait("ac", "dv", b.now("dv"))
        var_r = rows[0:1, 1024:2048]
        b.op("ac", lambda e: e.activation(var_r, var_r, AF.Ln, bias=sca[0:1, 6:7]))
        b.op("ac", lambda e: e.activation(var_r, var_r, AF.Exp, scale=-0.5),
             inc="ac")
        b.wait("dv", "ac", b.now("ac"))
        if ffn:
            b.dvf(lambda e: e.tensor_scalar(var_r, var_r, sca[0:1, 2:3],
                                            None, ALU.mult))
        b.op("dv", lambda e: e.tensor_copy(rowsb[0:1, 0:1024], rows[0:1, 0:1024]))
        b.op("dv", lambda e: e.tensor_copy(rowsb[0:1, 1024:2048], var_r),
             inc="dv")
        b.wait("te", "dv", b.now("dv"))
        for tc in range(2):
            b.need("te", f"ps{2 + tc}")
            b.need("te", f"ps{4 + tc}")
            b.op("te", lambda e, tc=tc: e.matmul(
                psF[2 + tc][:, :], ones_r[0:1, :],
                rowsb[0:1, tc * 512:(tc + 1) * 512], start=True, stop=True))
            b.op("te", lambda e, tc=tc: e.matmul(
                psF[4 + tc][:, :], ones_r[0:1, :],
                rowsb[0:1, 1024 + tc * 512:1024 + (tc + 1) * 512],
                start=True, stop=True), inc="te")
        b.wait("ac", "te", b.now("te"))
        b.need("ac", "ab16")
        for tc in range(2):
            b.op("ac", lambda e, tc=tc: e.activation(
                mb_t[:, tc * 512:(tc + 1) * 512], psF[2 + tc][:, :], AF.Copy))
            b.op("ac", lambda e, tc=tc: e.activation(
                rb_t[:, tc * 512:(tc + 1) * 512], psF[4 + tc][:, :], AF.Copy),
                inc="ac")
        for i in (2, 3, 4, 5):
            b.setg(f"ps{i}", "ac")
        b.wait("dv", "ac", b.now("ac"))
        for i in range(NCH):
            for tc in range(2):
                s_ = scr32[:, i * TOK + tc * 512:i * TOK + (tc + 1) * 512]
                b.dvf(lambda e, s=s_, tc=tc:
                      e.tensor_sub(s, s, mb_t[:, tc * 512:(tc + 1) * 512]))
                b.op("dv", lambda e, s=s_, tc=tc:
                     e.tensor_mul(s, s, rb_t[:, tc * 512:(tc + 1) * 512]),
                     inc=("dv" if (i == NCH - 1 and tc == 1) else None))
        b.setg("ab16", "dv")

    def first_stats():
        for i in range(NCH):
            xc = xch[i % 2]
            b.need("sy", f"xch{i % 2}")
            b.dma("sy", xc[:, :], x_in.ap()[i * 128:(i + 1) * 128, :], "sy")
            b.wait("dv", "sy", b.now("sy"))
            dst = scr32[:, i * TOK:(i + 1) * TOK]
            b.op("dv", lambda e, s=xc, d=dst: e.tensor_copy(d, s[:, :]), inc="dv")
            b.setg(f"xch{i % 2}", "dv")
            b.wait("ac", "dv", b.now("dv"))
            stats_chunk(i, dst)

    def xn_exchange(which):
        agi, ago = agi1[which], ago1[which]
        cc = "cca1" if which == "a" else "ccb1"
        dstd = agi.ap().rearrange("(i p) c -> p i c", p=128)
        srcd = scr32.ap().rearrange("p (i c) -> p i c", c=TOK)
        b.wait("sy", "dv", b.now("dv"))
        b.dma("sy", dstd, srcd, "sy")
        b.wait("gp", "sy", b.now("sy"))
        b.op("gp", lambda e, i=agi, o=ago: e.collective_compute(
            "AllGather", ALU.bypass, replica_groups=[list(range(8))],
            ins=[i.ap().opt()], outs=[o.ap().opt()]), inc=cc)
        b.wait("gp", cc, 1)
        xl = xnloc[which]
        srcflat = bass.AP(ago, r_xnb, [[1, 2 * D * TOK]])
        dstflat = bass.AP(xl, 0, [[1, 2 * D * TOK]])
        b.op("gp", lambda e, d=dstflat, s=srcflat:
             e.dma_start(out=d, in_=s), inc="gp", n=16)
        b.wait("sy", "gp", b.now("gp"))
        b.need("sy", "xn_full")
        for r in (0, 1):
            dstap = bass.AP(xn_full, r * TOK,
                            [[NCH * S, 128], [S, NCH], [1, TOK]])
            srcap = bass.AP(xl, r * D * TOK,
                            [[TOK, 128], [128 * TOK, NCH], [1, TOK]])
            b.dma("sy", dstap, srcap, "sy")
        b.wait("te", "sy", b.now("sy"))

    def wblock(src_dram, j, slot, rows_n=NCH):
        dst = wpool.ap()[:, slot * 2048: slot * 2048 + rows_n * 128].rearrange(
            "p (i c) -> p i c", c=128)
        src = src_dram.ap()[:, j * 128:(j + 1) * 128].rearrange(
            "(i p) c -> p i c", p=128)
        b.need("gp", f"wslot{slot}")
        b.dma("gp", dst, src, "gp")

    def wtile(slot, i):
        return wpool[:, slot * 2048 + i * 128: slot * 2048 + (i + 1) * 128]

    def attn_sub(which, xsrc, xdst, fuse_stats, last):
        is_a = which == "a"
        wlq, wlk, wrv, wout = (wq, wk, wv, wo) if is_a else (wbq, wbk, wbv, wbo)
        scale = SCALE_A if is_a else SCALE_B
        cc2 = "cca2" if is_a else "ccb2"
        nhalf = 2 if is_a else 1
        kdim = 64 if is_a else 128

        xn_exchange(which)
        if which == "a":
            b.wait("sy", "gp", b.now("gp"))
            tap("xnfull", xn_full, NCH * S, waits=())

        for g in range(2):
            for kind in ("k", "q", "v"):
                wsrc = {"k": wlk, "q": wlq, "v": wrv}[kind]
                dbuf = {"k": k4, "q": q4, "v": None}[kind]
                for jj in range(4):
                    j = 4 * g + jj
                    slot = jj % 2
                    wblock(wsrc, j, slot)
                    b.wait("te", "gp", b.now("gp"))
                    for tc in range(4):
                        pj = psF[tc % 2]
                        b.need("te", f"ps{tc % 2}")
                        for i in range(NCH):
                            rh = xn_full[:, i * S + tc * 512:
                                         i * S + (tc + 1) * 512]
                            b.op("te", lambda e, p=pj, l=wtile(slot, i), r=rh,
                                 i=i: e.matmul(p[:, :], l, r, start=(i == 0),
                                               stop=(i == NCH - 1)),
                                 inc=("te" if i == NCH - 1 else None))
                        if kind == "v":
                            b.wait("ac", "te", b.now("te"))
                            b.need("ac", "vtmp")
                            b.op("ac", lambda e, p=pj: e.activation(
                                vtmp[:, :], p[:, :], AF.Copy), inc="ac")
                            b.setg(f"ps{tc % 2}", "ac")
                            b.wait("te", "ac", b.now("ac"))
                            for m in range(4):
                                tb = psT[m % 2]
                                b.need("te", f"tb{m % 2}")
                                b.op("te", lambda e, t=tb, m=m: e.transpose(
                                    t[:, 0:128], vtmp[:, m * 128:(m + 1) * 128],
                                    eyeb[:, :]), inc="te")
                                b.wait("ac", "te", b.now("te"))
                                kc = tc * 4 + m
                                dsl = v4[:, kc * 512 + jj * 128:
                                         kc * 512 + (jj + 1) * 128]
                                b.op("ac", lambda e, t=tb, d=dsl: e.activation(
                                    d, t[:, 0:128], AF.Copy), inc="ac")
                                b.setg(f"tb{m % 2}", "ac")
                            b.setg("vtmp", "te")
                        elif is_a:
                            b.wait("ac", "te", b.now("te"))
                            b.need("ac", "qraw")
                            b.op("ac", lambda e, p=pj: e.activation(
                                qraw[:, :], p[:, :], AF.Copy), inc="ac")
                            b.setg(f"ps{tc % 2}", "ac")
                            b.wait("te", "ac", b.now("ac"))
                            prot = psF[2 + tc % 2]
                            b.need("te", f"ps{2 + tc % 2}")
                            b.op("te", lambda e, p=prot: e.matmul(
                                p[:, :], rotmb[:, :], qraw[:, :],
                                start=True, stop=True), inc="te")
                            dsl = dbuf[:, jj * S + tc * 512:
                                       jj * S + (tc + 1) * 512]
                            csl = cosb[:, tc * 512:(tc + 1) * 512]
                            ssl = sinb[:, tc * 512:(tc + 1) * 512]
                            b.wait("dv", "te", b.now("te"))
                            b.op("dv", lambda e, d=dsl, c=csl:
                                 e.tensor_mul(d, qraw[:, :], c))
                            b.dvf(lambda e, p=prot, s=ssl:
                                  e.tensor_mul(w1t[:, :], p[:, :], s))
                            b.op("dv", lambda e, d=dsl:
                                 e.tensor_add(d, d, w1t[:, :]), inc="dv")
                            b.setg("qraw", "dv")
                            b.setg(f"ps{2 + tc % 2}", "dv")
                        else:
                            dsl = dbuf[:, jj * S + tc * 512:
                                       jj * S + (tc + 1) * 512]
                            b.wait("ac", "te", b.now("te"))
                            b.op("ac", lambda e, p=pj, d=dsl: e.activation(
                                d, p[:, :], AF.Copy), inc="ac")
                            b.setg(f"ps{tc % 2}", "ac")
                    b.setg(f"wslot{slot}", "te")
            if g == 1:
                b.setg("xn_full", "te")
            if which == "a" and g == 0:
                tap("q4", q4, 8192)
                tap("k4", k4, 8192)
                tap("v4", v4, 8192)

            b.wait("te", "ac", b.now("ac"))
            b.wait("te", "dv", b.now("dv"))
            for jj in range(4):
                head = 4 * g + jj
                for qc in range(4):
                    nkc = 4 * (qc + 1)
                    for half in range(nhalf):
                        base = 64 * half if is_a else 0
                        pvb = psF[4 + half]
                        zbank = psF[2 + half]
                        b.need("te", f"ps{4 + half}")
                        b.need("te", f"ps{2 + half}")
                        def emit_S(kc):
                            sbk_ = psF[kc % 2]
                            lt_ = k4[:, jj * S + kc * 128:
                                     jj * S + (kc + 1) * 128]
                            rh_ = q4[:, jj * S + qc * 512:
                                     jj * S + (qc + 1) * 512]
                            b.need("te", f"ps{kc % 2}")
                            b.op("te", lambda e, p=sbk_, l=lt_, r=rh_, b0=base,
                                 kd=kdim: e.matmul(p[:, :], l[b0:b0 + kd, :],
                                                   r[b0:b0 + kd, :],
                                                   start=True, stop=True),
                                 inc="te")
                            return b.now("te")

                        s_done = {0: emit_S(0)}
                        for kc in range(nkc):
                            pt = ptile[kc % 2]
                            sbk = psF[kc % 2]
                            if kc + 1 < nkc:
                                s_done[kc + 1] = emit_S(kc + 1)
                            b.wait("ac", "te", s_done[kc])
                            b.need("ac", f"pt{kc % 2}")
                            b.op("ac", lambda e, p=sbk, t=pt: e.activation(
                                t[:, :], p[:, :], AF.Exp, scale=scale),
                                inc="ac")
                            b.setg(f"ps{kc % 2}", "ac")
                            m = kc - 4 * qc
                            if m >= 0:
                                b.wait("dv", "ac", b.now("ac"))
                                if m > 0:
                                    b.op("dv", lambda e, t=pt, m=m: e.memset(
                                        t[:, 0:m * 128], 0.0))
                                b.op("dv", lambda e, t=pt, m=m: e.tensor_mul(
                                    t[:, m * 128:(m + 1) * 128],
                                    t[:, m * 128:(m + 1) * 128], maskt[:, :]),
                                    inc="dv")
                                b.wait("te", "dv", b.now("dv"))
                            else:
                                b.wait("te", "ac", b.now("ac"))
                            vl = v4[:, kc * 512 + jj * 128:
                                    kc * 512 + (jj + 1) * 128]
                            if ("ptap" in TAPS and g == 0 and jj == 0
                                    and qc <= 1):
                                nm = f"p_{which}_q{qc}_h{half}_k{kc}"
                                tp = nc.dram_tensor(f"tap_{nm}", [128, 512],
                                                    BF16, kind="ExternalOutput")
                                tap_t[nm] = tp
                                b.wait("sy", "ac", b.now("ac"))
                                b.wait("sy", "dv", b.now("dv"))
                                b.dma("sy", tp.ap(), pt[:, :], "sy")
                                b.wait("te", "sy", b.now("sy"))
                            b.op("te", lambda e, p=pvb, v=vl, t=pt, kc=kc:
                                 e.matmul(p[:, :], v, t[:, :],
                                          start=(kc == 0), stop=(kc == nkc - 1)))
                            b.op("te", lambda e, p=zbank, t=pt, kc=kc:
                                 e.matmul(p[0:1, :], ones_c[:, 0:1], t[:, :],
                                          start=(kc == 0), stop=(kc == nkc - 1)),
                                 inc="te")
                            b.setg(f"pt{kc % 2}", "te")
                    b.wait("dv", "te", b.now("te"))
                    if "ptap" in TAPS and g == 0 and jj == 0 and qc <= 1:
                        nm = f"z_{which}_q{qc}"
                        tp = nc.dram_tensor(f"tap_{nm}", [1, 1024], F32,
                                            kind="ExternalOutput")
                        tap_t[nm] = tp
                        b.wait("ac", "te", b.now("te"))
                        b.op("ac", lambda e: e.activation(
                            rows[0:1, 1536:2048], psF[2][0:1, :], AF.Copy))
                        b.op("ac", lambda e: e.activation(
                            rows[0:1, 2048:2560], psF[3][0:1, :], AF.Copy),
                            inc="ac")
                        b.wait("sy", "ac", b.now("ac"))
                        b.dma("sy", tp.ap(), rows[0:1, 1536:2560], "sy")
                        b.wait("dv", "sy", b.now("sy"))
                    zsl = scr32[:, head * S + qc * 512:head * S + (qc + 1) * 512]
                    if is_a:
                        b.dvf(lambda e: e.reciprocal(rows[0:1, 0:512],
                                                     psF[2][0:1, :]))
                        b.dvf(lambda e: e.tensor_mul(
                            rows[0:1, 512:1024], psF[3][0:1, :],
                            rows[0:1, 0:512]))
                        b.op("dv", lambda e: e.tensor_copy(
                            rowsb[0:1, 0:512], rows[0:1, 512:1024]), inc="dv")
                        b.setg("ps2", "dv")
                        b.setg("ps3", "dv")
                        b.wait("te", "dv", b.now("dv"))
                        b.need("te", "ps1")
                        b.op("te", lambda e: e.matmul(
                            psF[1][:, :], ones_r[0:1, :], rowsb[0:1, 0:512],
                            start=True, stop=True), inc="te")
                        b.wait("ac", "te", b.now("te"))
                        b.op("ac", lambda e: e.activation(
                            zbt[:, 0:512], psF[1][:, :], AF.Copy), inc="ac")
                        b.setg("ps1", "ac")
                        b.wait("dv", "ac", b.now("ac"))
                        b.dvf(lambda e: e.tensor_mul(
                            w1t[:, :], psF[4][:, :], zbt[:, 0:512]))
                        b.op("dv", lambda e, z=zsl: e.scalar_tensor_tensor(
                            z, psF[5][:, :], lamn_c[:, 0:1], w1t[:, :],
                            ALU.mult, ALU.add), inc="dv")
                        if "ptap" in TAPS and g == 0 and jj == 0 and qc <= 1:
                            for nm, src_ in ((f"pv1_q{qc}", psF[4]),
                                             (f"pv2_q{qc}", psF[5])):
                                tp = nc.dram_tensor(f"tap_{nm}", [128, 512],
                                                    F32, kind="ExternalOutput")
                                tap_t[nm] = tp
                                b.wait("ac", "dv", b.now("dv"))
                                b.op("ac", lambda e, s=src_: e.activation(
                                    vtmp2[:, :], s[:, :], AF.Copy), inc="ac")
                                b.wait("sy", "ac", b.now("ac"))
                                b.dma("sy", tp.ap(), vtmp2[:, :], "sy")
                                b.wait("ac", "sy", b.now("sy"))
                            nm = f"w_q{qc}"
                            tp = nc.dram_tensor(f"tap_{nm}", [128, 512], BF16,
                                                kind="ExternalOutput")
                            tap_t[nm] = tp
                            b.wait("sy", "dv", b.now("dv"))
                            b.dma("sy", tp.ap(), zsl, "sy")
                            b.wait("dv", "sy", b.now("sy"))
                            nm = f"r12_q{qc}"
                            tp = nc.dram_tensor(f"tap_{nm}", [1, 512], F32,
                                                kind="ExternalOutput")
                            tap_t[nm] = tp
                            b.dma("sy", tp.ap(), rows[0:1, 512:1024], "sy")
                        b.setg("ps4", "dv")
                        b.setg("ps5", "dv")
                    else:
                        b.dvf(lambda e: e.reciprocal(rows[0:1, 0:512],
                                                     psF[2][0:1, :]))
                        b.op("dv", lambda e: e.tensor_copy(
                            rowsb[0:1, 0:512], rows[0:1, 0:512]), inc="dv")
                        b.setg("ps2", "dv")
                        b.wait("te", "dv", b.now("dv"))
                        b.need("te", "ps1")
                        b.op("te", lambda e: e.matmul(
                            psF[1][:, :], ones_r[0:1, :], rowsb[0:1, 0:512],
                            start=True, stop=True), inc="te")
                        b.wait("ac", "te", b.now("te"))
                        b.op("ac", lambda e: e.activation(
                            zbt[:, 0:512], psF[1][:, :], AF.Copy), inc="ac")
                        b.setg("ps1", "ac")
                        b.wait("dv", "ac", b.now("ac"))
                        b.op("dv", lambda e, z=zsl: e.tensor_mul(
                            z, psF[4][:, :], zbt[:, 0:512]), inc="dv")
                        b.setg("ps4", "dv")
                if is_a:
                    for seg in range(2):
                        hsl = scr32[:, head * S + seg * TOK:
                                    head * S + (seg + 1) * TOK]
                        b.need("ac", "sq1")
                        b.wait("ac", "dv", b.now("dv"))
                        b.op("ac", lambda e, h=hsl: e.activation(
                            sq1[:, :], h, AF.Square), inc="ac")
                        b.wait("te", "ac", b.now("ac"))
                        for sc in range(2):
                            pb = psF[2 + sc]
                            b.need("te", f"ps{2 + sc}")
                            b.op("te", lambda e, p=pb, sc=sc: e.matmul(
                                p[0:1, :], ones_c[:, 0:1],
                                sq1[:, sc * 512:(sc + 1) * 512],
                                start=True, stop=True), inc="te")
                        b.setg("sq1", "te")
                        b.wait("ac", "te", b.now("te"))
                        for sc in range(2):
                            b.op("ac", lambda e, sc=sc, seg=seg: e.activation(
                                rows[0:1, seg * 1024 + sc * 512:
                                     seg * 1024 + (sc + 1) * 512],
                                psF[2 + sc][0:1, :], AF.Copy, scale=1.0 / 128),
                                inc="ac")
                        b.setg("ps2", "ac")
                        b.setg("ps3", "ac")
                    b.op("ac", lambda e: e.activation(
                        rows[0:1, 0:2048], rows[0:1, 0:2048], AF.Ln,
                        bias=sca[0:1, 7:8]))
                    b.op("ac", lambda e: e.activation(
                        rows[0:1, 0:2048], rows[0:1, 0:2048], AF.Exp,
                        scale=-0.5), inc="ac")
                    b.wait("dv", "ac", b.now("ac"))
                    b.op("dv", lambda e: e.tensor_copy(rowsb[0:1, 0:2048],
                                                       rows[0:1, 0:2048]),
                         inc="dv")
                    b.wait("te", "dv", b.now("dv"))
                    for sc in range(4):
                        pb = psF[2 + sc % 2]
                        b.need("te", f"ps{2 + sc % 2}")
                        b.op("te", lambda e, p=pb, sc=sc: e.matmul(
                            p[:, :], ones_r[0:1, :],
                            rowsb[0:1, sc * 512:(sc + 1) * 512],
                            start=True, stop=True), inc="te")
                        b.wait("ac", "te", b.now("te"))
                        b.op("ac", lambda e, p=pb, sc=sc: e.activation(
                            zbt[:, (sc % 2) * 512:(sc % 2) * 512 + 512],
                            p[:, :], AF.Copy), inc="ac")
                        b.setg(f"ps{2 + sc % 2}", "ac")
                        b.wait("dv", "ac", b.now("ac"))
                        hsl = scr32[:, head * S + sc * 512:
                                    head * S + (sc + 1) * 512]
                        b.op("dv", lambda e, h=hsl, sc=sc:
                             e.tensor_mul(h, h, zbt[:, (sc % 2) * 512:
                                                     (sc % 2) * 512 + 512]),
                             inc="dv")
                        b.wait("dv", "dv")
                    b.wait("dv", "dv")
                    for q2 in range(4):
                        hsl = scr32[:, head * S + q2 * 512:
                                    head * S + (q2 + 1) * 512]
                        b.op("dv", lambda e, h=hsl: e.tensor_scalar(
                            h, h, lnw_eff[:, 0:1], lnb_eff[:, 0:1],
                            ALU.mult, ALU.add), inc="dv")

        if which == "a":
            tap("zbuf", scr32, NCH * TOK)
        agi, ago = agi2[which], ago2[which]
        b.wait("te", "ac", b.now("ac"))
        b.wait("te", "dv", b.now("dv"))
        b.wait("ac", "te", b.now("te"))
        for oc in range(NCH):
            slot = oc % 2
            wblock(wout, oc, slot, rows_n=8)
            b.wait("te", "gp", b.now("gp"))
            psl = par_sl[oc % 2]
            b.need("ac", f"par{oc % 2}")
            for tc in range(4):
                pj = psF[tc % 2]
                b.need("te", f"ps{tc % 2}")
                for i in range(8):
                    rh = scr32[:, i * S + tc * 512:i * S + (tc + 1) * 512]
                    b.op("te", lambda e, p=pj, l=wtile(slot, i), r=rh, i=i:
                         e.matmul(p[:, :], l, r, start=(i == 0),
                                  stop=(i == 7)),
                         inc=("te" if i == 7 else None))
                b.wait("ac", "te", b.now("te"))
                b.op("ac", lambda e, p=pj, d=psl[:, tc * 512:(tc + 1) * 512]:
                     e.activation(d, p[:, :], AF.Copy), inc="ac")
                b.setg(f"ps{tc % 2}", "ac")
            b.setg(f"wslot{slot}", "te")
            b.wait("sy", "ac", b.now("ac"))
            for hh in (0, 1):
                b.dma("sy",
                      agi.ap()[hh * D + oc * 128:hh * D + (oc + 1) * 128, :],
                      psl[:, hh * TOK:(hh + 1) * TOK], "sy")
            b.setg(f"par{oc % 2}", "sy")
        b.wait("gp", "sy", b.now("sy"))
        b.op("gp", lambda e, i=agi, o=ago: e.collective_compute(
            "AllGather", ALU.bypass, replica_groups=[list(range(8))],
            ins=[i.ap().opt()], outs=[o.ap().opt()]), inc=cc2)
        b.wait("gp", cc2, 1)

        if "par" in TAPS and which == "a":
            tpa = nc.dram_tensor("tap_agi2", [2 * D, TOK], BF16,
                                 kind="ExternalOutput")
            tap_t["agi2"] = tpa
            b.dma("gp", tpa.ap(), agi.ap(), "gp")
        stream_update(which, xsrc, xdst, fuse_stats, last, ago)

    def stream_update(which, xsrc, xdst, fuse_stats, last, ago):
        pl = parloc[which]
        for r in (0, 1):
            dstrot = bass.AP(pl, r * D * TOK, [[1, D * TOK]])
            if r == 0:
                srcrot = bass.AP(ago, r_parb, [[1, D * TOK]])
            else:
                b.op("gp", lambda e: e.reg_mov(r_tmp, 2 * D * TOK))
                b.op("gp", lambda e: e.reg_add(r_tmp, r_tmp, r_parb))
                srcrot = bass.AP(ago, r_tmp, [[1, D * TOK]])
            b.op("gp", lambda e, d=dstrot, s=srcrot: e.dma_start(out=d, in_=s),
                 inc="gp", n=16)
        if "par" in TAPS and which == "a":
            tpo = nc.dram_tensor("tap_ago2", [4 * D, TOK], BF16,
                                 kind="ExternalOutput")
            tap_t["ago2"] = tpo
            b.dma("gp", tpo.ap(), ago.ap()[0:4 * D, :], "gp")
            tpp = nc.dram_tensor("tap_parloc", [2 * D, TOK], BF16,
                                 kind="ExternalOutput")
            tap_t["parloc"] = tpp
            b.dma("gp", tpp.ap(), pl.ap(), "gp")
        for i in range(NCH):
            pob = po_sl[i % 2]
            b.need("gp", f"po{i % 2}")
            srcap = bass.AP(pl, i * 128 * TOK,
                            [[TOK, 128], [D * TOK, 2], [1, TOK]])
            dstap = pob.rearrange("p (r c) -> p r c", c=TOK)
            b.op("gp", lambda e, d=dstap, s=srcap: e.dma_start(out=d, in_=s),
                 inc="gp", n=16)
            xc = xch[i % 2]
            b.need("sy", f"xch{i % 2}")
            b.dma("sy", xc[:, :], xsrc.ap()[i * 128:(i + 1) * 128, :], "sy")
            b.wait("dv", "gp", b.now("gp"))
            b.wait("dv", "sy", b.now("sy"))
            for tc in range(2):
                sl = slice(tc * 512, (tc + 1) * 512)
                sl2 = slice(1024 + tc * 512, 1024 + (tc + 1) * 512)
                b.dvf(lambda e, p=pob, sl=sl, sl2=sl2:
                      e.tensor_add(w1t[:, :], p[:, sl], p[:, sl2]))
                b.op("dv", lambda e, x=xc, sl=sl:
                     e.tensor_add(x[:, sl], x[:, sl], w1t[:, :]), inc="dv")
                b.wait("dv", "dv")
            b.setg(f"po{i % 2}", "dv")
            if fuse_stats:
                dstn = scr32[:, i * TOK:(i + 1) * TOK]
                b.op("dv", lambda e, s=xc, d=dstn: e.tensor_copy(d, s[:, :]),
                     inc="dv")
                b.wait("ac", "dv", b.now("dv"))
                stats_chunk(i, dstn)
            b.wait("sy", "dv", b.now("dv"))
            b.dma("sy", xdst.ap()[i * 128:(i + 1) * 128, :], xc[:, :], "sy")
            b.setg(f"xch{i % 2}", "sy")

    def ffn_sub(xsrc, xdst, fuse_stats, last):
        b.wait("dv", "ac", b.now("ac"))
        for i in range(NCH):
            for tc in range(2):
                s_ = scr32[:, i * TOK + tc * 512:i * TOK + (tc + 1) * 512]
                b.dvf(lambda e, s=s_: e.tensor_scalar(
                    f8t[:, :], s, 240.0, -240.0, ALU.min, ALU.max))
                b.op("dv", lambda e, s=s_: e.tensor_copy(s, f8t[:, :]),
                     inc="dv")
                b.wait("dv", "dv")
        for oc in range(NCH):
            slot = oc % 2
            dst8 = w8p.ap()[:, slot * 2048:(slot + 1) * 2048].rearrange(
                "p (i c) -> p i c", c=128)
            src8 = w8t.ap()[:, oc * 128:(oc + 1) * 128].rearrange(
                "(i p) c -> p i c", p=128)
            b.need("sy", f"w8s{slot}")
            b.dma("sy", dst8, src8, "sy")
            b.wait("dv", "sy", b.now("sy"))
            b.need("dv", f"wslot{slot}")
            b.op("dv", lambda e, s=slot: e.tensor_copy(
                wpool[:, s * 2048:(s + 1) * 2048],
                w8p[:, s * 2048:(s + 1) * 2048]), inc="dv")
            b.setg(f"w8s{slot}", "dv")
            b.wait("te", "dv", b.now("dv"))
            xc = xch[oc % 2]
            b.need("sy", f"xch{oc % 2}")
            b.dma("sy", xc[:, :], xsrc.ap()[oc * 128:(oc + 1) * 128, :], "sy")
            for tc in range(2):
                pj = psF[tc % 2]
                b.need("te", f"ps{tc % 2}")
                for i in range(NCH):
                    rh = scr32[:, i * TOK + tc * 512:i * TOK + (tc + 1) * 512]
                    b.op("te", lambda e, p=pj, l=wtile(slot, i), r=rh, i=i:
                         e.matmul(p[:, :], l, r, start=(i == 0),
                                  stop=(i == NCH - 1)),
                         inc=("te" if i == NCH - 1 else None))
            b.setg(f"wslot{slot}", "te")
            b.wait("dv", "te", b.now("te"))
            b.wait("dv", "sy", b.now("sy"))
            for tc in range(2):
                pj = psF[tc % 2]
                sl = slice(tc * 512, (tc + 1) * 512)
                b.op("dv", lambda e, p=pj, x=xc, sl=sl:
                     e.scalar_tensor_tensor(x[:, sl], p[:, :], alpha_c[:, 0:1],
                                            x[:, sl], ALU.mult, ALU.add),
                     inc="dv")
                b.setg(f"ps{tc % 2}", "dv")
            b.wait("sy", "dv", b.now("dv"))
            b.dma("sy", xdst.ap()[oc * 128:(oc + 1) * 128, :], xc[:, :], "sy")
            b.setg(f"xch{oc % 2}", "sy")
        if fuse_stats:
            for i in range(NCH):
                xc = xch[i % 2]
                b.need("sy", f"xch{i % 2}")
                b.dma("sy", xc[:, :], xdst.ap()[i * 128:(i + 1) * 128, :], "sy")
                b.wait("dv", "sy", b.now("sy"))
                dstn = scr32[:, i * TOK:(i + 1) * TOK]
                b.op("dv", lambda e, s=xc, d=dstn: e.tensor_copy(d, s[:, :]),
                     inc="dv")
                b.setg(f"xch{i % 2}", "dv")
                b.wait("ac", "dv", b.now("dv"))
                stats_chunk(i, dstn)

    first_stats()
    ln_finish(ffn=False)
    tap("xnw1", scr32, NCH * TOK)
    attn_sub("a", x_in, xbufA, fuse_stats=(n_sub >= 2), last=(n_sub == 1))
    if n_sub >= 2:
        ln_finish(ffn=True)
        ffn_sub(xbufA, xbufB, fuse_stats=(n_sub >= 3), last=(n_sub == 2))
    if n_sub >= 3:
        ln_finish(ffn=False)
        attn_sub("b", xbufB, xbufA, fuse_stats=(n_sub >= 4), last=(n_sub == 3))
    if n_sub >= 4:
        ln_finish(ffn=True)
        ffn_sub(xbufA, out_fm, fuse_stats=False, last=True)

    fin = {1: xbufA, 2: xbufB, 3: xbufA}.get(n_sub)
    if fin is not None and n_sub < 4:
        b.wait("gp", "sy", b.now("sy"))
        b.dma("gp", out_fm.ap(), fin.ap(), "gp")

    b.wait("sy", "sy", b.now("sy"))
    b.wait("sy", "gp", b.now("gp"))

    with nc.Block() as block:
        for eng_name, attr in (("te", "tensor"), ("dv", "vector"),
                               ("ac", "scalar"), ("gp", "gpsimd"),
                               ("sy", "sync")):
            ops = b.q[eng_name]

            def runner(e, ops=ops):
                for f in ops:
                    f(e)
            getattr(block, attr)(runner)

    b.stack.close()
    return nc


_NC_CACHE = {}


def _get_nc():
    key = N_SUB
    if key not in _NC_CACHE:
        _NC_CACHE[key] = build_nc(key)
    return _NC_CACHE[key]


def _rope_tables():
    dd = 64
    inv = (1.0 / np.float32(10000.0) **
           (np.arange(0, dd, 2, dtype=np.float32) / np.float32(dd)))
    t = np.arange(S, dtype=np.float32)
    freqs = (t[:, None] * inv[None, :]).astype(np.float32)
    freqs = np.repeat(freqs, 2, axis=1)
    cos = np.cos(freqs).astype(np.float32)
    sin = np.sin(freqs).astype(np.float32)
    cosf = np.concatenate([cos.T, cos.T], axis=0)
    sinf = np.concatenate([sin.T, sin.T], axis=0)
    return (np.ascontiguousarray(cosf).astype(ml_dtypes.bfloat16),
            np.ascontiguousarray(sinf).astype(ml_dtypes.bfloat16))


def _host_prep(inputs):
    x = np.asarray(inputs["L0_attn_a_x"])
    Wq = np.asarray(inputs["L0_attn_a_Wq"])
    Wk = np.asarray(inputs["L0_attn_a_Wk"])
    Wv = np.asarray(inputs["L0_attn_a_Wv"])
    Wo = np.asarray(inputs["L0_attn_a_Wo"])
    WK = np.asarray(inputs["L0_attn_b_W_K"])
    WQ = np.asarray(inputs["L0_attn_b_W_Q"])
    WV = np.asarray(inputs["L0_attn_b_W_V"])
    WO = np.asarray(inputs["L0_attn_b_W_O"])
    w8 = np.asarray(inputs["L0_ffn_weight"])
    wsc = np.float32(inputs["L0_ffn_weight_scale"])
    isc = np.float32(inputs["L0_ffn_input_scale"])
    lam = np.concatenate([np.asarray(inputs["L0_attn_a_lambda_q1"]),
                          np.asarray(inputs["L0_attn_a_lambda_k1"]),
                          np.asarray(inputs["L0_attn_a_lambda_q2"]),
                          np.asarray(inputs["L0_attn_a_lambda_k2"])]
                         ).astype(np.float32).reshape(1, 256)
    lnw = np.asarray(inputs["L0_attn_a_ln_w"]).astype(np.float32).reshape(128, 1)
    lnb = np.asarray(inputs["L0_attn_a_ln_b"]).astype(np.float32).reshape(128, 1)

    cosb, sinb = _rope_tables()
    rot = np.zeros((128, 128), dtype=np.float32)
    for p in range(0, 128, 2):
        rot[p + 1, p] = -1.0
        rot[p, p + 1] = 1.0
    rot = np.ascontiguousarray(rot).astype(ml_dtypes.bfloat16)
    tri = np.ascontiguousarray(
        np.triu(np.ones((128, 128), dtype=np.float32))).astype(
        ml_dtypes.bfloat16)
    eye = np.ascontiguousarray(np.eye(128, dtype=np.float32)).astype(
        ml_dtypes.bfloat16)

    w8t = np.ascontiguousarray(w8.T).view(ml_dtypes.float8_e4m3)

    in_maps = []
    for c in range(8):
        bb, h = c >> 1, c & 1
        tok = slice(1024 * h, 1024 * h + 1024)
        qk = slice(1024 * h, 1024 * h + 1024)
        heads = slice(8 * h, 8 * h + 8)
        x_fm = np.ascontiguousarray(x[bb, tok, :].T)
        wbk_c = np.ascontiguousarray(
            WK[heads].transpose(2, 0, 1).reshape(D, 1024))
        wbq_c = np.ascontiguousarray(
            WQ[heads].transpose(2, 0, 1).reshape(D, 1024))
        wbv_c = np.ascontiguousarray(
            WV[heads].transpose(2, 0, 1).reshape(D, 1024))
        wbo_c = np.ascontiguousarray(WO[:, qk].T)
        offs = np.array([[(c & ~1) * D * TOK,
                          (c & ~1) * 2 * D * TOK + h * D * TOK, 0, 0]],
                        dtype=np.int32)
        in_maps.append({
            "x_in": x_fm,
            "wq": np.ascontiguousarray(Wq[:, qk]),
            "wk": np.ascontiguousarray(Wk[:, qk]),
            "wv": np.ascontiguousarray(Wv[:, qk]),
            "wo": np.ascontiguousarray(Wo[qk, :]),
            "wbk": wbk_c, "wbq": wbq_c, "wbv": wbv_c, "wbo": wbo_c,
            "w8t": w8t,
            "cos_in": cosb, "sin_in": sinb, "rot_in": rot,
            "tri_in": tri,
            "eye_in": eye,
            "lam_in": lam, "lnw_in": lnw, "lnb_in": lnb,
            "wsc_in": wsc.reshape(1, 1),
            "isc_in": isc.reshape(1, 1),
            "offs_in": offs,
        })
    return in_maps


def kernel(**inputs):
    in_maps = _host_prep(inputs)
    nc = _get_nc()
    res = run_bass_kernel_spmd(nc, in_maps, core_ids=list(range(8)))
    out = np.empty((4, S, D), dtype=np.float32)
    for c in range(8):
        bb, h = c >> 1, c & 1
        out[bb, 1024 * h:1024 * h + 1024, :] = res.results[c]["out_fm"].T
    if TAPS:
        kernel.tap_results = [
            {k: v for k, v in r.items() if k.startswith("tap_")}
            for r in res.results]
    return out


# version 07
# speedup vs baseline: 1.3408x; 1.0491x over previous
import contextlib
import math

import numpy as np
import ml_dtypes

import concourse.bass as bass
import concourse.mybir as mybir
from concourse.bass_utils import run_bass_kernel_spmd

F32 = mybir.dt.float32
BF16 = mybir.dt.bfloat16
FP8 = mybir.dt.float8e4
I32 = mybir.dt.int32
AF = mybir.ActivationFunctionType
ALU = mybir.AluOpType
AXX = mybir.AxisListType.X

D = 2048
S = 2048
TOK = 1024
NCH = 16
HEADS = 8
LAMBDA_INIT = 0.8 - 0.6 * math.exp(-0.3 * 12)
EPS_LN = 1e-5
EPS_RMS = 1e-8
SCALE_A = 64.0 ** -0.5
SCALE_B = 128.0 ** -0.5
N_SUB = 4
TAPS = ()

_ENGS = ("te", "dv", "ac", "gp", "sy")


class _B:

    def __init__(self, nc):
        self.nc = nc
        self.q = {e: [] for e in _ENGS}
        self.cnt = {}
        self.sems = {}
        self.stack = contextlib.ExitStack()
        self._lastwait = {e: {} for e in _ENGS}
        self.guard = {}

    def sem(self, name):
        self.sems[name] = self.stack.enter_context(self.nc.semaphore(name))
        self.cnt[name] = 0

    def wait(self, eng, sem, val=None):
        v = self.cnt[sem] if val is None else val
        if v <= 0:
            return
        if self._lastwait[eng].get(sem, -1) >= v:
            return
        self._lastwait[eng][sem] = v
        h = self.sems[sem]
        self.q[eng].append(lambda e, h=h, v=v: e.wait_ge(h, v))

    def op(self, eng, fn, inc=None, n=1):
        if inc is None:
            self.q[eng].append(fn)
        else:
            h = self.sems[inc]
            self.q[eng].append(lambda e, fn=fn, h=h, n=n: fn(e).then_inc(h, n))
            self.cnt[inc] += n

    def dma(self, eng, out, in_, inc):
        self.op(eng, lambda e, o=out, i=in_: e.dma_start(out=o, in_=i),
                inc=inc, n=16)

    def now(self, sem):
        return self.cnt[sem]

    def need(self, eng, key):
        g = self.guard.get(key)
        if g:
            self.wait(eng, g[0], g[1])

    def setg(self, key, sem):
        self.guard[key] = (sem, self.cnt[sem])

    def dvf(self, fn):
        self.op("dv", fn, inc="dv")
        self.wait("dv", "dv")


def build_nc(n_sub=N_SUB):
    nc = bass.Bass("TRN2", target_bir_lowering=False)
    b = _B(nc)
    st = b.stack
    sb = lambda name, shape, dt: st.enter_context(
        nc.sbuf_tensor(name, list(shape), dt))

    def din(name, shape, dt):
        return nc.dram_tensor(name, list(shape), dt, kind="ExternalInput")

    x_in = din("x_in", [D, TOK], F32)
    wq = din("wq", [D, TOK], F32)
    wk = din("wk", [D, TOK], F32)
    wv = din("wv", [D, TOK], F32)
    wo = din("wo", [TOK, D], F32)
    wbk = din("wbk", [D, TOK], F32)
    wbq = din("wbq", [D, TOK], F32)
    wbv = din("wbv", [D, TOK], F32)
    wbo = din("wbo", [TOK, D], F32)
    w8t = din("w8t", [D, D], FP8)
    cos_in = din("cos_in", [128, S], BF16)
    sin_in = din("sin_in", [128, S], BF16)
    rot_in = din("rot_in", [128, 128], BF16)
    tri_in = din("tri_in", [128, 128], BF16)
    eye_in = din("eye_in", [128, 128], BF16)
    lam_in = din("lam_in", [1, 256], F32)
    lnw_in = din("lnw_in", [128, 1], F32)
    lnb_in = din("lnb_in", [128, 1], F32)
    wsc_in = din("wsc_in", [1, 1], F32)
    isc_in = din("isc_in", [1, 1], F32)
    offs_in = din("offs_in", [1, 4], I32)

    out_fm = nc.dram_tensor("out_fm", [D, TOK], F32, kind="ExternalOutput")

    xbufA = nc.dram_tensor("xbufA", [D, TOK], F32)
    xbufB = nc.dram_tensor("xbufB", [D, TOK], F32)
    agi1 = {a: nc.dram_tensor(f"ag1{a}_in", [D, TOK], BF16) for a in "ab"}
    ago1 = {a: nc.dram_tensor(f"ag1{a}_out", [8 * D, TOK], BF16,
                              addr_space="Shared") for a in "ab"}
    agi2 = {a: nc.dram_tensor(f"ag2{a}_in", [2 * D, TOK], BF16) for a in "ab"}
    ago2 = {a: nc.dram_tensor(f"ag2{a}_out", [16 * D, TOK], BF16,
                              addr_space="Shared") for a in "ab"}
    xnloc = {a: nc.dram_tensor(f"xnloc{a}", [2 * D, TOK], BF16) for a in "ab"}
    parloc = {a: nc.dram_tensor(f"parloc{a}", [2 * D, TOK], BF16) for a in "ab"}

    xn_full = sb("xn_full", [128, NCH * S], BF16)
    scr32 = sb("scr32", [128, NCH * TOK], BF16)
    ab16 = sb("ab16", [128, 8192], BF16)
    k4 = sb("k4", [128, 8192], BF16)
    v4 = sb("v4", [128, 8192], BF16)
    cs16 = sb("cs16", [128, 4096], BF16)
    wpool = sb("wpool", [128, 4096], BF16)
    w8p = sb("w8p", [128, 4096], FP8)
    xch = [sb(f"xch{i}", [128, TOK], F32) for i in (0, 1)]
    sq1 = sb("sq1", [128, TOK], BF16)
    ptile = [sb(f"ptile{i}", [128, 512], BF16) for i in (0, 1)]
    qraw = sb("qraw", [128, 512], BF16)
    vtmp = sb("vtmp", [128, 512], BF16)
    f8t = sb("f8t", [128, 512], FP8)
    vtmp2 = sb("vtmp2", [128, 512], F32)
    zbt = sb("zbt", [128, 1024], BF16)
    w1t = sb("w1t", [128, 512], F32)
    rows = sb("rows", [1, 2560], F32)
    rowsb = sb("rowsb", [1, 2048], BF16)
    maskt = sb("maskt", [128, 128], BF16)
    rotmb = sb("rotmb", [128, 128], BF16)
    eyeb = sb("eyeb", [128, 128], BF16)
    ones_c = sb("ones_c", [128, 1], BF16)
    ones_r = sb("ones_r", [1, 128], BF16)
    lamv = sb("lamv", [1, 256], F32)
    lnw = sb("lnw", [128, 1], F32)
    lnb = sb("lnb", [128, 1], F32)
    lnw_eff = sb("lnw_eff", [128, 1], F32)
    lnb_eff = sb("lnb_eff", [128, 1], F32)
    sca = sb("sca", [1, 8], F32)
    onef = sb("onef", [1, 128], F32)
    alpha_c = sb("alpha_c", [128, 1], F32)
    lamn_c = sb("lamn_c", [128, 1], F32)
    offs_sb = sb("offs_sb", [1, 4], I32)

    mb_t = ab16[:, 0:1024]
    rb_t = ab16[:, 1024:2048]
    q4 = ab16
    par_sl = [ab16[:, 0:2048], ab16[:, 2048:4096]]
    cosb = cs16[:, 0:2048]
    sinb = cs16[:, 2048:4096]
    po_sl = [cs16[:, 0:2048], cs16[:, 2048:4096]]

    psF = [st.enter_context(nc.psum_tensor(f"ps{i}", [128, 512], F32))
           for i in range(6)]
    psT = [st.enter_context(nc.psum_tensor(f"tb{i}", [128, 1024], BF16))
           for i in range(2)]

    for s_name in ("sy", "gp", "te", "dv", "ac",
                   "cca1", "cca2", "ccb1", "ccb2"):
        b.sem(s_name)

    tap_t = {}

    def tap(point, buf, cols, waits=("dv", "ac", "te")):
        if point not in TAPS:
            return
        t = nc.dram_tensor(f"tap_{point}", [128, cols], BF16,
                           kind="ExternalOutput")
        tap_t[point] = t
        for w in waits:
            b.wait("sy", w, b.now(w))
        b.dma("sy", t.ap(), buf[:, 0:cols], "sy")

    def tapf(point, buf, cols, waits=("dv", "ac", "te")):
        if point not in TAPS:
            return
        t = nc.dram_tensor(f"tap_{point}", [128, cols], F32,
                           kind="ExternalOutput")
        tap_t[point] = t
        for w in waits:
            b.wait("sy", w, b.now(w))
        b.dma("sy", t.ap(), buf[:, 0:cols], "sy")

    r_xnb = st.enter_context(nc.gpsimd.register("r_xnb"))
    r_parb = st.enter_context(nc.gpsimd.register("r_parb"))
    r_tmp = st.enter_context(nc.gpsimd.register("r_tmp"))

    for src, dst in ((cos_in, cosb), (sin_in, sinb), (rot_in, rotmb[:, :]),
                     (tri_in, maskt[:, :]), (eye_in, eyeb[:, :]),
                     (lam_in, lamv[:, :]), (lnw_in, lnw[:, :]),
                     (lnb_in, lnb[:, :]), (offs_in, offs_sb[:, :])):
        b.dma("sy", dst, src.ap(), "sy")
    b.dma("sy", sca[0:1, 0:1], wsc_in.ap(), "sy")
    b.dma("sy", sca[0:1, 1:2], isc_in.ap(), "sy")

    b.wait("dv", "sy", b.now("sy"))
    b.op("dv", lambda e: e.memset(ones_c[:, :], 1.0))
    b.op("dv", lambda e: e.memset(sca[0:1, 6:7], EPS_LN))
    b.op("dv", lambda e: e.memset(sca[0:1, 7:8], EPS_RMS))
    b.op("dv", lambda e: e.memset(ones_r[:, :], 1.0))
    b.op("dv", lambda e: e.memset(onef[:, :], 1.0))
    b.op("dv", lambda e: e.tensor_mul(rows[0:1, 0:64], lamv[0:1, 0:64],
                                      lamv[0:1, 64:128]))
    b.dvf(lambda e: e.tensor_mul(rows[0:1, 64:128], lamv[0:1, 128:192],
                                 lamv[0:1, 192:256]))
    b.op("dv", lambda e: e.reduce_sum(rows[0:1, 128:129], rows[0:1, 0:64], axis=AXX))
    b.op("dv", lambda e: e.reduce_sum(rows[0:1, 129:130], rows[0:1, 64:128],
                                      axis=AXX), inc="dv")
    b.wait("ac", "dv", b.now("dv"))
    b.op("ac", lambda e: e.activation(rows[0:1, 130:132], rows[0:1, 128:130],
                                      AF.Exp), inc="ac")
    b.wait("dv", "ac", b.now("ac"))
    b.dvf(lambda e: e.tensor_sub(rows[0:1, 132:133], rows[0:1, 130:131],
                                 rows[0:1, 131:132]))
    b.dvf(lambda e: e.tensor_scalar_add(sca[0:1, 4:5], rows[0:1, 132:133],
                                        LAMBDA_INIT))
    b.dvf(lambda e: e.tensor_scalar_mul(sca[0:1, 5:6], sca[0:1, 4:5], -1.0))
    b.dvf(lambda e: e.reciprocal(sca[0:1, 2:3], sca[0:1, 1:2]))
    b.op("dv", lambda e: e.tensor_mul(sca[0:1, 3:4], sca[0:1, 0:1], sca[0:1, 1:2]),
         inc="dv")
    b.wait("ac", "dv", b.now("dv"))
    b.op("ac", lambda e: e.activation(lnw_eff[:, :], lnw[:, :], AF.Copy,
                                      scale=1.0 - LAMBDA_INIT))
    b.op("ac", lambda e: e.activation(lnb_eff[:, :], lnb[:, :], AF.Copy,
                                      scale=1.0 - LAMBDA_INIT), inc="ac")
    b.wait("te", "dv", b.now("dv"))
    b.op("te", lambda e: e.matmul(psF[4][:, 0:1], onef[0:1, :], sca[0:1, 3:4],
                                  start=True, stop=True))
    b.op("te", lambda e: e.matmul(psF[4][:, 1:2], onef[0:1, :], sca[0:1, 5:6],
                                  start=True, stop=True), inc="te")
    b.wait("ac", "te", b.now("te"))
    b.op("ac", lambda e: e.activation(alpha_c[:, :], psF[4][:, 0:1], AF.Copy))
    b.op("ac", lambda e: e.activation(lamn_c[:, :], psF[4][:, 1:2], AF.Copy),
         inc="ac")
    b.setg("ps4", "ac")

    if "setup" in TAPS:
        tp = nc.dram_tensor("tap_sca", [1, 8], F32, kind="ExternalOutput")
        tap_t["sca"] = tp
        tp2 = nc.dram_tensor("tap_cols", [2, 128], F32, kind="ExternalOutput")
        tap_t["cols"] = tp2
        tp3 = nc.dram_tensor("tap_rows0", [1, 256], F32, kind="ExternalOutput")
        tap_t["rows0"] = tp3
        b.wait("sy", "ac", b.now("ac"))
        b.wait("sy", "dv", b.now("dv"))
        b.dma("sy", tp.ap(), sca[0:1, :], "sy")
        b.dma("sy", tp2.ap()[0:1, :].rearrange("a p -> p a"), alpha_c[:, :], "sy")
        b.dma("sy", tp2.ap()[1:2, :].rearrange("a p -> p a"), lamn_c[:, :], "sy")
        b.dma("sy", tp3.ap(), rows[0:1, 0:256], "sy")
    b.wait("gp", "sy", b.now("sy"))
    b.op("gp", lambda e: e.reg_load(r_xnb, offs_sb[0:1, 0:1]))
    b.op("gp", lambda e: e.reg_load(r_parb, offs_sb[0:1, 1:2]))

    def stats_chunk(i, src_bf):
        b.need("ac", "sq1")
        b.op("ac", lambda e, s=src_bf: e.activation(sq1[:, :], s, AF.Square),
             inc="ac")
        b.wait("te", "ac", b.now("ac"))
        for tc in range(2):
            sl = slice(tc * 512, (tc + 1) * 512)
            if i == 0:
                b.need("te", f"ps{2 + tc}")
                b.need("te", f"ps{4 + tc}")
            b.op("te", lambda e, s=src_bf, sl=sl, tc=tc, i=i:
                 e.matmul(psF[2 + tc][0:1, :], ones_c[:, 0:1], s[:, sl],
                          start=(i == 0), stop=(i == NCH - 1)))
            b.op("te", lambda e, sl=sl, tc=tc, i=i:
                 e.matmul(psF[4 + tc][0:1, :], ones_c[:, 0:1], sq1[:, sl],
                          start=(i == 0), stop=(i == NCH - 1)),
                 inc=("te" if (i == NCH - 1 and tc == 1) else None))
        b.setg("sq1", "te")

    def ln_finish(ffn):
        b.wait("ac", "te", b.now("te"))
        for tc in range(2):
            b.op("ac", lambda e, tc=tc: e.activation(
                rows[0:1, tc * 512:(tc + 1) * 512], psF[2 + tc][0:1, :],
                AF.Copy, scale=1.0 / D))
            b.op("ac", lambda e, tc=tc: e.activation(
                rows[0:1, 1024 + tc * 512:1024 + (tc + 1) * 512],
                psF[4 + tc][0:1, :], AF.Copy, scale=1.0 / D), inc="ac")
        for i in (2, 3, 4, 5):
            b.setg(f"ps{i}", "ac")
        b.wait("dv", "ac", b.now("ac"))
        for tc in range(2):
            m_h = rows[0:1, tc * 512:(tc + 1) * 512]
            e_h = rows[0:1, 1024 + tc * 512:1024 + (tc + 1) * 512]
            t_h = rows[0:1, 2048:2560]
            b.dvf(lambda e, m=m_h, t=t_h: e.tensor_mul(t, m, m))
            b.op("dv", lambda e, eh=e_h, t=t_h: e.tensor_sub(eh, eh, t),
                 inc="dv")
            b.wait("dv", "dv")
        b.wait("ac", "dv", b.now("dv"))
        var_r = rows[0:1, 1024:2048]
        b.op("ac", lambda e: e.activation(var_r, var_r, AF.Ln, bias=sca[0:1, 6:7]))
        b.op("ac", lambda e: e.activation(var_r, var_r, AF.Exp, scale=-0.5),
             inc="ac")
        b.wait("dv", "ac", b.now("ac"))
        if ffn:
            b.dvf(lambda e: e.tensor_scalar(var_r, var_r, sca[0:1, 2:3],
                                            None, ALU.mult))
        b.op("dv", lambda e: e.tensor_copy(rowsb[0:1, 0:1024], rows[0:1, 0:1024]))
        b.op("dv", lambda e: e.tensor_copy(rowsb[0:1, 1024:2048], var_r),
             inc="dv")
        b.wait("te", "dv", b.now("dv"))
        for tc in range(2):
            b.need("te", f"ps{2 + tc}")
            b.need("te", f"ps{4 + tc}")
            b.op("te", lambda e, tc=tc: e.matmul(
                psF[2 + tc][:, :], ones_r[0:1, :],
                rowsb[0:1, tc * 512:(tc + 1) * 512], start=True, stop=True))
            b.op("te", lambda e, tc=tc: e.matmul(
                psF[4 + tc][:, :], ones_r[0:1, :],
                rowsb[0:1, 1024 + tc * 512:1024 + (tc + 1) * 512],
                start=True, stop=True), inc="te")
        b.wait("ac", "te", b.now("te"))
        b.need("ac", "ab16")
        for tc in range(2):
            b.op("ac", lambda e, tc=tc: e.activation(
                mb_t[:, tc * 512:(tc + 1) * 512], psF[2 + tc][:, :], AF.Copy))
            b.op("ac", lambda e, tc=tc: e.activation(
                rb_t[:, tc * 512:(tc + 1) * 512], psF[4 + tc][:, :], AF.Copy),
                inc="ac")
        for i in (2, 3, 4, 5):
            b.setg(f"ps{i}", "ac")
        b.wait("dv", "ac", b.now("ac"))
        for i in range(NCH):
            for tc in range(2):
                s_ = scr32[:, i * TOK + tc * 512:i * TOK + (tc + 1) * 512]
                b.dvf(lambda e, s=s_, tc=tc:
                      e.tensor_sub(s, s, mb_t[:, tc * 512:(tc + 1) * 512]))
                b.op("dv", lambda e, s=s_, tc=tc:
                     e.tensor_mul(s, s, rb_t[:, tc * 512:(tc + 1) * 512]),
                     inc=("dv" if (i == NCH - 1 and tc == 1) else None))
        b.setg("ab16", "dv")

    def first_stats():
        for i in range(NCH):
            xc = xch[i % 2]
            b.need("sy", f"xch{i % 2}")
            b.dma("sy", xc[:, :], x_in.ap()[i * 128:(i + 1) * 128, :], "sy")
            b.wait("dv", "sy", b.now("sy"))
            dst = scr32[:, i * TOK:(i + 1) * TOK]
            b.op("dv", lambda e, s=xc, d=dst: e.tensor_copy(d, s[:, :]), inc="dv")
            b.setg(f"xch{i % 2}", "dv")
            b.wait("ac", "dv", b.now("dv"))
            stats_chunk(i, dst)

    def xn_exchange(which):
        agi, ago = agi1[which], ago1[which]
        cc = "cca1" if which == "a" else "ccb1"
        dstd = agi.ap().rearrange("(i p) c -> p i c", p=128)
        srcd = scr32.ap().rearrange("p (i c) -> p i c", c=TOK)
        b.wait("sy", "dv", b.now("dv"))
        b.dma("sy", dstd, srcd, "sy")
        b.wait("gp", "sy", b.now("sy"))
        b.op("gp", lambda e, i=agi, o=ago: e.collective_compute(
            "AllGather", ALU.bypass, replica_groups=[list(range(8))],
            ins=[i.ap().opt()], outs=[o.ap().opt()]), inc=cc)
        b.wait("gp", cc, 1)
        xl = xnloc[which]
        srcflat = bass.AP(ago, r_xnb, [[1, 2 * D * TOK]])
        dstflat = bass.AP(xl, 0, [[1, 2 * D * TOK]])
        b.op("gp", lambda e, d=dstflat, s=srcflat:
             e.dma_start(out=d, in_=s), inc="gp", n=16)
        b.wait("sy", "gp", b.now("gp"))
        b.need("sy", "xn_full")
        for r in (0, 1):
            dstap = bass.AP(xn_full, r * TOK,
                            [[NCH * S, 128], [S, NCH], [1, TOK]])
            srcap = bass.AP(xl, r * D * TOK,
                            [[TOK, 128], [128 * TOK, NCH], [1, TOK]])
            b.dma("sy", dstap, srcap, "sy")
        b.wait("te", "sy", b.now("sy"))

    def wblock(src_dram, j, slot, rows_n=NCH):
        dst = wpool.ap()[:, slot * 2048: slot * 2048 + rows_n * 128].rearrange(
            "p (i c) -> p i c", c=128)
        src = src_dram.ap()[:, j * 128:(j + 1) * 128].rearrange(
            "(i p) c -> p i c", p=128)
        b.need("gp", f"wslot{slot}")
        b.dma("gp", dst, src, "gp")

    def wtile(slot, i):
        return wpool[:, slot * 2048 + i * 128: slot * 2048 + (i + 1) * 128]

    def attn_sub(which, xsrc, xdst, fuse_stats, last):
        is_a = which == "a"
        wlq, wlk, wrv, wout = (wq, wk, wv, wo) if is_a else (wbq, wbk, wbv, wbo)
        scale = SCALE_A if is_a else SCALE_B
        cc2 = "cca2" if is_a else "ccb2"
        nhalf = 2 if is_a else 1
        kdim = 64 if is_a else 128

        xn_exchange(which)
        if which == "a":
            b.wait("sy", "gp", b.now("gp"))
            tap("xnfull", xn_full, NCH * S, waits=())

        for g in range(2):
            kinds = ("k", "q", "v")
            blocklist = [(kind, jj) for kind in kinds for jj in range(4)]
            gp_after = {}

            def load_blk(idx):
                kind, jj = blocklist[idx]
                wsrc = {"k": wlk, "q": wlq, "v": wrv}[kind]
                wblock(wsrc, 4 * g + jj, idx % 2)
                gp_after[idx] = b.now("gp")

            load_blk(0)
            for bi, (kind, jj) in enumerate(blocklist):
                    wsrc = {"k": wlk, "q": wlq, "v": wrv}[kind]
                    dbuf = {"k": k4, "q": q4, "v": None}[kind]
                    j = 4 * g + jj
                    slot = bi % 2
                    if bi + 1 < len(blocklist):
                        load_blk(bi + 1)
                    b.wait("te", "gp", gp_after[bi])
                    for tc in range(4):
                        pj = psF[tc % 2]
                        b.need("te", f"ps{tc % 2}")
                        for i in range(NCH):
                            rh = xn_full[:, i * S + tc * 512:
                                         i * S + (tc + 1) * 512]
                            b.op("te", lambda e, p=pj, l=wtile(slot, i), r=rh,
                                 i=i: e.matmul(p[:, :], l, r, start=(i == 0),
                                               stop=(i == NCH - 1)),
                                 inc=("te" if i == NCH - 1 else None))
                        if kind == "v":
                            b.wait("ac", "te", b.now("te"))
                            b.need("ac", "vtmp")
                            b.op("ac", lambda e, p=pj: e.activation(
                                vtmp[:, :], p[:, :], AF.Copy), inc="ac")
                            b.setg(f"ps{tc % 2}", "ac")
                            b.wait("te", "ac", b.now("ac"))
                            for m in range(4):
                                tb = psT[m % 2]
                                b.need("te", f"tb{m % 2}")
                                b.op("te", lambda e, t=tb, m=m: e.transpose(
                                    t[:, 0:128], vtmp[:, m * 128:(m + 1) * 128],
                                    eyeb[:, :]), inc="te")
                                b.wait("ac", "te", b.now("te"))
                                kc = tc * 4 + m
                                dsl = v4[:, kc * 512 + jj * 128:
                                         kc * 512 + (jj + 1) * 128]
                                b.op("ac", lambda e, t=tb, d=dsl: e.activation(
                                    d, t[:, 0:128], AF.Copy), inc="ac")
                                b.setg(f"tb{m % 2}", "ac")
                            b.setg("vtmp", "te")
                        elif is_a:
                            b.wait("ac", "te", b.now("te"))
                            b.need("ac", "qraw")
                            b.op("ac", lambda e, p=pj: e.activation(
                                qraw[:, :], p[:, :], AF.Copy), inc="ac")
                            b.setg(f"ps{tc % 2}", "ac")
                            b.wait("te", "ac", b.now("ac"))
                            prot = psF[2 + tc % 2]
                            b.need("te", f"ps{2 + tc % 2}")
                            b.op("te", lambda e, p=prot: e.matmul(
                                p[:, :], rotmb[:, :], qraw[:, :],
                                start=True, stop=True), inc="te")
                            dsl = dbuf[:, jj * S + tc * 512:
                                       jj * S + (tc + 1) * 512]
                            csl = cosb[:, tc * 512:(tc + 1) * 512]
                            ssl = sinb[:, tc * 512:(tc + 1) * 512]
                            b.wait("dv", "te", b.now("te"))
                            b.op("dv", lambda e, d=dsl, c=csl:
                                 e.tensor_mul(d, qraw[:, :], c))
                            b.dvf(lambda e, p=prot, s=ssl:
                                  e.tensor_mul(w1t[:, :], p[:, :], s))
                            b.op("dv", lambda e, d=dsl:
                                 e.tensor_add(d, d, w1t[:, :]), inc="dv")
                            b.setg("qraw", "dv")
                            b.setg(f"ps{2 + tc % 2}", "dv")
                        else:
                            dsl = dbuf[:, jj * S + tc * 512:
                                       jj * S + (tc + 1) * 512]
                            b.wait("ac", "te", b.now("te"))
                            b.op("ac", lambda e, p=pj, d=dsl: e.activation(
                                d, p[:, :], AF.Copy), inc="ac")
                            b.setg(f"ps{tc % 2}", "ac")
                    b.setg(f"wslot{slot}", "te")
            if g == 1:
                b.setg("xn_full", "te")
            if which == "a" and g == 0:
                tap("q4", q4, 8192)
                tap("k4", k4, 8192)
                tap("v4", v4, 8192)

            b.wait("te", "ac", b.now("ac"))
            b.wait("te", "dv", b.now("dv"))
            for jj in range(4):
                head = 4 * g + jj
                for qc in range(4):
                    nkc = 4 * (qc + 1)
                    for half in range(nhalf):
                        base = 64 * half if is_a else 0
                        pvb = psF[4 + half]
                        zbank = psF[2 + half]
                        b.need("te", f"ps{4 + half}")
                        b.need("te", f"ps{2 + half}")
                        def emit_S(kc):
                            sbk_ = psF[kc % 2]
                            lt_ = k4[:, jj * S + kc * 128:
                                     jj * S + (kc + 1) * 128]
                            rh_ = q4[:, jj * S + qc * 512:
                                     jj * S + (qc + 1) * 512]
                            b.need("te", f"ps{kc % 2}")
                            b.op("te", lambda e, p=sbk_, l=lt_, r=rh_, b0=base,
                                 kd=kdim: e.matmul(p[:, :], l[b0:b0 + kd, :],
                                                   r[b0:b0 + kd, :],
                                                   start=True, stop=True),
                                 inc="te")
                            return b.now("te")

                        s_done = {0: emit_S(0)}
                        for kc in range(nkc):
                            pt = ptile[kc % 2]
                            sbk = psF[kc % 2]
                            if kc + 1 < nkc:
                                s_done[kc + 1] = emit_S(kc + 1)
                            b.wait("ac", "te", s_done[kc])
                            b.need("ac", f"pt{kc % 2}")
                            b.op("ac", lambda e, p=sbk, t=pt: e.activation(
                                t[:, :], p[:, :], AF.Exp, scale=scale),
                                inc="ac")
                            b.setg(f"ps{kc % 2}", "ac")
                            m = kc - 4 * qc
                            if m >= 0:
                                b.wait("dv", "ac", b.now("ac"))
                                if m > 0:
                                    b.op("dv", lambda e, t=pt, m=m: e.memset(
                                        t[:, 0:m * 128], 0.0))
                                b.op("dv", lambda e, t=pt, m=m: e.tensor_mul(
                                    t[:, m * 128:(m + 1) * 128],
                                    t[:, m * 128:(m + 1) * 128], maskt[:, :]),
                                    inc="dv")
                                b.wait("te", "dv", b.now("dv"))
                            else:
                                b.wait("te", "ac", b.now("ac"))
                            vl = v4[:, kc * 512 + jj * 128:
                                    kc * 512 + (jj + 1) * 128]
                            if ("ptap" in TAPS and g == 0 and jj == 0
                                    and qc <= 1):
                                nm = f"p_{which}_q{qc}_h{half}_k{kc}"
                                tp = nc.dram_tensor(f"tap_{nm}", [128, 512],
                                                    BF16, kind="ExternalOutput")
                                tap_t[nm] = tp
                                b.wait("sy", "ac", b.now("ac"))
                                b.wait("sy", "dv", b.now("dv"))
                                b.dma("sy", tp.ap(), pt[:, :], "sy")
                                b.wait("te", "sy", b.now("sy"))
                            b.op("te", lambda e, p=pvb, v=vl, t=pt, kc=kc:
                                 e.matmul(p[:, :], v, t[:, :],
                                          start=(kc == 0), stop=(kc == nkc - 1)))
                            b.op("te", lambda e, p=zbank, t=pt, kc=kc:
                                 e.matmul(p[0:1, :], ones_c[:, 0:1], t[:, :],
                                          start=(kc == 0), stop=(kc == nkc - 1)),
                                 inc="te")
                            b.setg(f"pt{kc % 2}", "te")
                    b.wait("dv", "te", b.now("te"))
                    if "ptap" in TAPS and g == 0 and jj == 0 and qc <= 1:
                        nm = f"z_{which}_q{qc}"
                        tp = nc.dram_tensor(f"tap_{nm}", [1, 1024], F32,
                                            kind="ExternalOutput")
                        tap_t[nm] = tp
                        b.wait("ac", "te", b.now("te"))
                        b.op("ac", lambda e: e.activation(
                            rows[0:1, 1536:2048], psF[2][0:1, :], AF.Copy))
                        b.op("ac", lambda e: e.activation(
                            rows[0:1, 2048:2560], psF[3][0:1, :], AF.Copy),
                            inc="ac")
                        b.wait("sy", "ac", b.now("ac"))
                        b.dma("sy", tp.ap(), rows[0:1, 1536:2560], "sy")
                        b.wait("dv", "sy", b.now("sy"))
                    zsl = scr32[:, head * S + qc * 512:head * S + (qc + 1) * 512]
                    if is_a:
                        b.dvf(lambda e: e.reciprocal(rows[0:1, 0:512],
                                                     psF[2][0:1, :]))
                        b.dvf(lambda e: e.tensor_mul(
                            rows[0:1, 512:1024], psF[3][0:1, :],
                            rows[0:1, 0:512]))
                        b.op("dv", lambda e: e.tensor_copy(
                            rowsb[0:1, 0:512], rows[0:1, 512:1024]), inc="dv")
                        b.setg("ps2", "dv")
                        b.setg("ps3", "dv")
                        b.wait("te", "dv", b.now("dv"))
                        b.need("te", "ps1")
                        b.op("te", lambda e: e.matmul(
                            psF[1][:, :], ones_r[0:1, :], rowsb[0:1, 0:512],
                            start=True, stop=True), inc="te")
                        b.wait("ac", "te", b.now("te"))
                        b.op("ac", lambda e: e.activation(
                            zbt[:, 0:512], psF[1][:, :], AF.Copy), inc="ac")
                        b.setg("ps1", "ac")
                        b.wait("dv", "ac", b.now("ac"))
                        b.dvf(lambda e: e.tensor_mul(
                            w1t[:, :], psF[4][:, :], zbt[:, 0:512]))
                        b.op("dv", lambda e, z=zsl: e.scalar_tensor_tensor(
                            z, psF[5][:, :], lamn_c[:, 0:1], w1t[:, :],
                            ALU.mult, ALU.add), inc="dv")
                        if "ptap" in TAPS and g == 0 and jj == 0 and qc <= 1:
                            for nm, src_ in ((f"pv1_q{qc}", psF[4]),
                                             (f"pv2_q{qc}", psF[5])):
                                tp = nc.dram_tensor(f"tap_{nm}", [128, 512],
                                                    F32, kind="ExternalOutput")
                                tap_t[nm] = tp
                                b.wait("ac", "dv", b.now("dv"))
                                b.op("ac", lambda e, s=src_: e.activation(
                                    vtmp2[:, :], s[:, :], AF.Copy), inc="ac")
                                b.wait("sy", "ac", b.now("ac"))
                                b.dma("sy", tp.ap(), vtmp2[:, :], "sy")
                                b.wait("ac", "sy", b.now("sy"))
                            nm = f"w_q{qc}"
                            tp = nc.dram_tensor(f"tap_{nm}", [128, 512], BF16,
                                                kind="ExternalOutput")
                            tap_t[nm] = tp
                            b.wait("sy", "dv", b.now("dv"))
                            b.dma("sy", tp.ap(), zsl, "sy")
                            b.wait("dv", "sy", b.now("sy"))
                            nm = f"r12_q{qc}"
                            tp = nc.dram_tensor(f"tap_{nm}", [1, 512], F32,
                                                kind="ExternalOutput")
                            tap_t[nm] = tp
                            b.dma("sy", tp.ap(), rows[0:1, 512:1024], "sy")
                        b.setg("ps4", "dv")
                        b.setg("ps5", "dv")
                    else:
                        b.dvf(lambda e: e.reciprocal(rows[0:1, 0:512],
                                                     psF[2][0:1, :]))
                        b.op("dv", lambda e: e.tensor_copy(
                            rowsb[0:1, 0:512], rows[0:1, 0:512]), inc="dv")
                        b.setg("ps2", "dv")
                        b.wait("te", "dv", b.now("dv"))
                        b.need("te", "ps1")
                        b.op("te", lambda e: e.matmul(
                            psF[1][:, :], ones_r[0:1, :], rowsb[0:1, 0:512],
                            start=True, stop=True), inc="te")
                        b.wait("ac", "te", b.now("te"))
                        b.op("ac", lambda e: e.activation(
                            zbt[:, 0:512], psF[1][:, :], AF.Copy), inc="ac")
                        b.setg("ps1", "ac")
                        b.wait("dv", "ac", b.now("ac"))
                        b.op("dv", lambda e, z=zsl: e.tensor_mul(
                            z, psF[4][:, :], zbt[:, 0:512]), inc="dv")
                        b.setg("ps4", "dv")
                if is_a:
                    for seg in range(2):
                        hsl = scr32[:, head * S + seg * TOK:
                                    head * S + (seg + 1) * TOK]
                        b.need("ac", "sq1")
                        b.wait("ac", "dv", b.now("dv"))
                        b.op("ac", lambda e, h=hsl: e.activation(
                            sq1[:, :], h, AF.Square), inc="ac")
                        b.wait("te", "ac", b.now("ac"))
                        for sc in range(2):
                            pb = psF[2 + sc]
                            b.need("te", f"ps{2 + sc}")
                            b.op("te", lambda e, p=pb, sc=sc: e.matmul(
                                p[0:1, :], ones_c[:, 0:1],
                                sq1[:, sc * 512:(sc + 1) * 512],
                                start=True, stop=True), inc="te")
                        b.setg("sq1", "te")
                        b.wait("ac", "te", b.now("te"))
                        for sc in range(2):
                            b.op("ac", lambda e, sc=sc, seg=seg: e.activation(
                                rows[0:1, seg * 1024 + sc * 512:
                                     seg * 1024 + (sc + 1) * 512],
                                psF[2 + sc][0:1, :], AF.Copy, scale=1.0 / 128),
                                inc="ac")
                        b.setg("ps2", "ac")
                        b.setg("ps3", "ac")
                    b.op("ac", lambda e: e.activation(
                        rows[0:1, 0:2048], rows[0:1, 0:2048], AF.Ln,
                        bias=sca[0:1, 7:8]))
                    b.op("ac", lambda e: e.activation(
                        rows[0:1, 0:2048], rows[0:1, 0:2048], AF.Exp,
                        scale=-0.5), inc="ac")
                    b.wait("dv", "ac", b.now("ac"))
                    b.op("dv", lambda e: e.tensor_copy(rowsb[0:1, 0:2048],
                                                       rows[0:1, 0:2048]),
                         inc="dv")
                    b.wait("te", "dv", b.now("dv"))
                    for sc in range(4):
                        pb = psF[2 + sc % 2]
                        b.need("te", f"ps{2 + sc % 2}")
                        b.op("te", lambda e, p=pb, sc=sc: e.matmul(
                            p[:, :], ones_r[0:1, :],
                            rowsb[0:1, sc * 512:(sc + 1) * 512],
                            start=True, stop=True), inc="te")
                        b.wait("ac", "te", b.now("te"))
                        b.op("ac", lambda e, p=pb, sc=sc: e.activation(
                            zbt[:, (sc % 2) * 512:(sc % 2) * 512 + 512],
                            p[:, :], AF.Copy), inc="ac")
                        b.setg(f"ps{2 + sc % 2}", "ac")
                        b.wait("dv", "ac", b.now("ac"))
                        hsl = scr32[:, head * S + sc * 512:
                                    head * S + (sc + 1) * 512]
                        b.op("dv", lambda e, h=hsl, sc=sc:
                             e.tensor_mul(h, h, zbt[:, (sc % 2) * 512:
                                                     (sc % 2) * 512 + 512]),
                             inc="dv")
                        b.wait("dv", "dv")
                    b.wait("dv", "dv")
                    for q2 in range(4):
                        hsl = scr32[:, head * S + q2 * 512:
                                    head * S + (q2 + 1) * 512]
                        b.op("dv", lambda e, h=hsl: e.tensor_scalar(
                            h, h, lnw_eff[:, 0:1], lnb_eff[:, 0:1],
                            ALU.mult, ALU.add), inc="dv")

        if which == "a":
            tap("zbuf", scr32, NCH * TOK)
        agi, ago = agi2[which], ago2[which]
        b.wait("te", "ac", b.now("ac"))
        b.wait("te", "dv", b.now("dv"))
        b.wait("ac", "te", b.now("te"))
        gp_after_wo = {}

        def load_wo(oc):
            wblock(wout, oc, oc % 2, rows_n=8)
            gp_after_wo[oc] = b.now("gp")

        load_wo(0)
        for oc in range(NCH):
            slot = oc % 2
            if oc + 1 < NCH:
                load_wo(oc + 1)
            b.wait("te", "gp", gp_after_wo[oc])
            psl = par_sl[oc % 2]
            b.need("ac", f"par{oc % 2}")
            for tc in range(4):
                pj = psF[tc % 2]
                b.need("te", f"ps{tc % 2}")
                for i in range(8):
                    rh = scr32[:, i * S + tc * 512:i * S + (tc + 1) * 512]
                    b.op("te", lambda e, p=pj, l=wtile(slot, i), r=rh, i=i:
                         e.matmul(p[:, :], l, r, start=(i == 0),
                                  stop=(i == 7)),
                         inc=("te" if i == 7 else None))
                b.wait("ac", "te", b.now("te"))
                b.op("ac", lambda e, p=pj, d=psl[:, tc * 512:(tc + 1) * 512]:
                     e.activation(d, p[:, :], AF.Copy), inc="ac")
                b.setg(f"ps{tc % 2}", "ac")
            b.setg(f"wslot{slot}", "te")
            b.wait("sy", "ac", b.now("ac"))
            for hh in (0, 1):
                b.dma("sy",
                      agi.ap()[hh * D + oc * 128:hh * D + (oc + 1) * 128, :],
                      psl[:, hh * TOK:(hh + 1) * TOK], "sy")
            b.setg(f"par{oc % 2}", "sy")
        b.wait("gp", "sy", b.now("sy"))
        b.op("gp", lambda e, i=agi, o=ago: e.collective_compute(
            "AllGather", ALU.bypass, replica_groups=[list(range(8))],
            ins=[i.ap().opt()], outs=[o.ap().opt()]), inc=cc2)
        b.wait("gp", cc2, 1)

        if "par" in TAPS and which == "a":
            tpa = nc.dram_tensor("tap_agi2", [2 * D, TOK], BF16,
                                 kind="ExternalOutput")
            tap_t["agi2"] = tpa
            b.dma("gp", tpa.ap(), agi.ap(), "gp")
        stream_update(which, xsrc, xdst, fuse_stats, last, ago)

    def stream_update(which, xsrc, xdst, fuse_stats, last, ago):
        pl = parloc[which]
        for r in (0, 1):
            dstrot = bass.AP(pl, r * D * TOK, [[1, D * TOK]])
            if r == 0:
                srcrot = bass.AP(ago, r_parb, [[1, D * TOK]])
            else:
                b.op("gp", lambda e: e.reg_mov(r_tmp, 2 * D * TOK))
                b.op("gp", lambda e: e.reg_add(r_tmp, r_tmp, r_parb))
                srcrot = bass.AP(ago, r_tmp, [[1, D * TOK]])
            b.op("gp", lambda e, d=dstrot, s=srcrot: e.dma_start(out=d, in_=s),
                 inc="gp", n=16)
        if "par" in TAPS and which == "a":
            tpo = nc.dram_tensor("tap_ago2", [4 * D, TOK], BF16,
                                 kind="ExternalOutput")
            tap_t["ago2"] = tpo
            b.dma("gp", tpo.ap(), ago.ap()[0:4 * D, :], "gp")
            tpp = nc.dram_tensor("tap_parloc", [2 * D, TOK], BF16,
                                 kind="ExternalOutput")
            tap_t["parloc"] = tpp
            b.dma("gp", tpp.ap(), pl.ap(), "gp")
        for i in range(NCH):
            pob = po_sl[i % 2]
            b.need("gp", f"po{i % 2}")
            srcap = bass.AP(pl, i * 128 * TOK,
                            [[TOK, 128], [D * TOK, 2], [1, TOK]])
            dstap = pob.rearrange("p (r c) -> p r c", c=TOK)
            b.op("gp", lambda e, d=dstap, s=srcap: e.dma_start(out=d, in_=s),
                 inc="gp", n=16)
            xc = xch[i % 2]
            b.need("sy", f"xch{i % 2}")
            b.dma("sy", xc[:, :], xsrc.ap()[i * 128:(i + 1) * 128, :], "sy")
            b.wait("dv", "gp", b.now("gp"))
            b.wait("dv", "sy", b.now("sy"))
            for tc in range(2):
                sl = slice(tc * 512, (tc + 1) * 512)
                sl2 = slice(1024 + tc * 512, 1024 + (tc + 1) * 512)
                b.dvf(lambda e, p=pob, sl=sl, sl2=sl2:
                      e.tensor_add(w1t[:, :], p[:, sl], p[:, sl2]))
                b.op("dv", lambda e, x=xc, sl=sl:
                     e.tensor_add(x[:, sl], x[:, sl], w1t[:, :]), inc="dv")
                b.wait("dv", "dv")
            b.setg(f"po{i % 2}", "dv")
            if fuse_stats:
                dstn = scr32[:, i * TOK:(i + 1) * TOK]
                b.op("dv", lambda e, s=xc, d=dstn: e.tensor_copy(d, s[:, :]),
                     inc="dv")
                b.wait("ac", "dv", b.now("dv"))
                stats_chunk(i, dstn)
            b.wait("sy", "dv", b.now("dv"))
            b.dma("sy", xdst.ap()[i * 128:(i + 1) * 128, :], xc[:, :], "sy")
            b.setg(f"xch{i % 2}", "sy")

    def ffn_sub(xsrc, xdst, fuse_stats, last):
        b.wait("dv", "ac", b.now("ac"))
        for i in range(NCH):
            for tc in range(2):
                s_ = scr32[:, i * TOK + tc * 512:i * TOK + (tc + 1) * 512]
                b.dvf(lambda e, s=s_: e.tensor_scalar(
                    f8t[:, :], s, 240.0, -240.0, ALU.min, ALU.max))
                b.op("dv", lambda e, s=s_: e.tensor_copy(s, f8t[:, :]),
                     inc="dv")
                b.wait("dv", "dv")
        sy_after_w8 = {}

        def load_w8(oc):
            slot_ = oc % 2
            dst8 = w8p.ap()[:, slot_ * 2048:(slot_ + 1) * 2048].rearrange(
                "p (i c) -> p i c", c=128)
            src8 = w8t.ap()[:, oc * 128:(oc + 1) * 128].rearrange(
                "(i p) c -> p i c", p=128)
            b.need("sy", f"w8s{slot_}")
            b.dma("sy", dst8, src8, "sy")
            sy_after_w8[oc] = b.now("sy")

        load_w8(0)
        for oc in range(NCH):
            slot = oc % 2
            if oc + 1 < NCH:
                load_w8(oc + 1)
            b.wait("dv", "sy", sy_after_w8[oc])
            b.need("dv", f"wslot{slot}")
            b.op("dv", lambda e, s=slot: e.tensor_copy(
                wpool[:, s * 2048:(s + 1) * 2048],
                w8p[:, s * 2048:(s + 1) * 2048]), inc="dv")
            b.setg(f"w8s{slot}", "dv")
            b.wait("te", "dv", b.now("dv"))
            xc = xch[oc % 2]
            b.need("sy", f"xch{oc % 2}")
            b.dma("sy", xc[:, :], xsrc.ap()[oc * 128:(oc + 1) * 128, :], "sy")
            for tc in range(2):
                pj = psF[tc % 2]
                b.need("te", f"ps{tc % 2}")
                for i in range(NCH):
                    rh = scr32[:, i * TOK + tc * 512:i * TOK + (tc + 1) * 512]
                    b.op("te", lambda e, p=pj, l=wtile(slot, i), r=rh, i=i:
                         e.matmul(p[:, :], l, r, start=(i == 0),
                                  stop=(i == NCH - 1)),
                         inc=("te" if i == NCH - 1 else None))
            b.setg(f"wslot{slot}", "te")
            b.wait("dv", "te", b.now("te"))
            b.wait("dv", "sy", b.now("sy"))
            for tc in range(2):
                pj = psF[tc % 2]
                sl = slice(tc * 512, (tc + 1) * 512)
                b.op("dv", lambda e, p=pj, x=xc, sl=sl:
                     e.scalar_tensor_tensor(x[:, sl], p[:, :], alpha_c[:, 0:1],
                                            x[:, sl], ALU.mult, ALU.add),
                     inc="dv")
                b.setg(f"ps{tc % 2}", "dv")
            b.wait("sy", "dv", b.now("dv"))
            b.dma("sy", xdst.ap()[oc * 128:(oc + 1) * 128, :], xc[:, :], "sy")
            b.setg(f"xch{oc % 2}", "sy")
        if fuse_stats:
            for i in range(NCH):
                xc = xch[i % 2]
                b.need("sy", f"xch{i % 2}")
                b.dma("sy", xc[:, :], xdst.ap()[i * 128:(i + 1) * 128, :], "sy")
                b.wait("dv", "sy", b.now("sy"))
                dstn = scr32[:, i * TOK:(i + 1) * TOK]
                b.op("dv", lambda e, s=xc, d=dstn: e.tensor_copy(d, s[:, :]),
                     inc="dv")
                b.setg(f"xch{i % 2}", "dv")
                b.wait("ac", "dv", b.now("dv"))
                stats_chunk(i, dstn)

    first_stats()
    ln_finish(ffn=False)
    tap("xnw1", scr32, NCH * TOK)
    attn_sub("a", x_in, xbufA, fuse_stats=(n_sub >= 2), last=(n_sub == 1))
    if n_sub >= 2:
        ln_finish(ffn=True)
        ffn_sub(xbufA, xbufB, fuse_stats=(n_sub >= 3), last=(n_sub == 2))
    if n_sub >= 3:
        ln_finish(ffn=False)
        attn_sub("b", xbufB, xbufA, fuse_stats=(n_sub >= 4), last=(n_sub == 3))
    if n_sub >= 4:
        ln_finish(ffn=True)
        ffn_sub(xbufA, out_fm, fuse_stats=False, last=True)

    fin = {1: xbufA, 2: xbufB, 3: xbufA}.get(n_sub)
    if fin is not None and n_sub < 4:
        b.wait("gp", "sy", b.now("sy"))
        b.dma("gp", out_fm.ap(), fin.ap(), "gp")

    b.wait("sy", "sy", b.now("sy"))
    b.wait("sy", "gp", b.now("gp"))

    with nc.Block() as block:
        for eng_name, attr in (("te", "tensor"), ("dv", "vector"),
                               ("ac", "scalar"), ("gp", "gpsimd"),
                               ("sy", "sync")):
            ops = b.q[eng_name]

            def runner(e, ops=ops):
                for f in ops:
                    f(e)
            getattr(block, attr)(runner)

    b.stack.close()
    return nc


_NC_CACHE = {}


def _get_nc():
    key = N_SUB
    if key not in _NC_CACHE:
        _NC_CACHE[key] = build_nc(key)
    return _NC_CACHE[key]


def _rope_tables():
    dd = 64
    inv = (1.0 / np.float32(10000.0) **
           (np.arange(0, dd, 2, dtype=np.float32) / np.float32(dd)))
    t = np.arange(S, dtype=np.float32)
    freqs = (t[:, None] * inv[None, :]).astype(np.float32)
    freqs = np.repeat(freqs, 2, axis=1)
    cos = np.cos(freqs).astype(np.float32)
    sin = np.sin(freqs).astype(np.float32)
    cosf = np.concatenate([cos.T, cos.T], axis=0)
    sinf = np.concatenate([sin.T, sin.T], axis=0)
    return (np.ascontiguousarray(cosf).astype(ml_dtypes.bfloat16),
            np.ascontiguousarray(sinf).astype(ml_dtypes.bfloat16))


def _host_prep(inputs):
    x = np.asarray(inputs["L0_attn_a_x"])
    Wq = np.asarray(inputs["L0_attn_a_Wq"])
    Wk = np.asarray(inputs["L0_attn_a_Wk"])
    Wv = np.asarray(inputs["L0_attn_a_Wv"])
    Wo = np.asarray(inputs["L0_attn_a_Wo"])
    WK = np.asarray(inputs["L0_attn_b_W_K"])
    WQ = np.asarray(inputs["L0_attn_b_W_Q"])
    WV = np.asarray(inputs["L0_attn_b_W_V"])
    WO = np.asarray(inputs["L0_attn_b_W_O"])
    w8 = np.asarray(inputs["L0_ffn_weight"])
    wsc = np.float32(inputs["L0_ffn_weight_scale"])
    isc = np.float32(inputs["L0_ffn_input_scale"])
    lam = np.concatenate([np.asarray(inputs["L0_attn_a_lambda_q1"]),
                          np.asarray(inputs["L0_attn_a_lambda_k1"]),
                          np.asarray(inputs["L0_attn_a_lambda_q2"]),
                          np.asarray(inputs["L0_attn_a_lambda_k2"])]
                         ).astype(np.float32).reshape(1, 256)
    lnw = np.asarray(inputs["L0_attn_a_ln_w"]).astype(np.float32).reshape(128, 1)
    lnb = np.asarray(inputs["L0_attn_a_ln_b"]).astype(np.float32).reshape(128, 1)

    cosb, sinb = _rope_tables()
    rot = np.zeros((128, 128), dtype=np.float32)
    for p in range(0, 128, 2):
        rot[p + 1, p] = -1.0
        rot[p, p + 1] = 1.0
    rot = np.ascontiguousarray(rot).astype(ml_dtypes.bfloat16)
    tri = np.ascontiguousarray(
        np.triu(np.ones((128, 128), dtype=np.float32))).astype(
        ml_dtypes.bfloat16)
    eye = np.ascontiguousarray(np.eye(128, dtype=np.float32)).astype(
        ml_dtypes.bfloat16)

    w8t = np.ascontiguousarray(w8.T).view(ml_dtypes.float8_e4m3)

    in_maps = []
    for c in range(8):
        bb, h = c >> 1, c & 1
        tok = slice(1024 * h, 1024 * h + 1024)
        qk = slice(1024 * h, 1024 * h + 1024)
        heads = slice(8 * h, 8 * h + 8)
        x_fm = np.ascontiguousarray(x[bb, tok, :].T)
        wbk_c = np.ascontiguousarray(
            WK[heads].transpose(2, 0, 1).reshape(D, 1024))
        wbq_c = np.ascontiguousarray(
            WQ[heads].transpose(2, 0, 1).reshape(D, 1024))
        wbv_c = np.ascontiguousarray(
            WV[heads].transpose(2, 0, 1).reshape(D, 1024))
        wbo_c = np.ascontiguousarray(WO[:, qk].T)
        offs = np.array([[(c & ~1) * D * TOK,
                          (c & ~1) * 2 * D * TOK + h * D * TOK, 0, 0]],
                        dtype=np.int32)
        in_maps.append({
            "x_in": x_fm,
            "wq": np.ascontiguousarray(Wq[:, qk]),
            "wk": np.ascontiguousarray(Wk[:, qk]),
            "wv": np.ascontiguousarray(Wv[:, qk]),
            "wo": np.ascontiguousarray(Wo[qk, :]),
            "wbk": wbk_c, "wbq": wbq_c, "wbv": wbv_c, "wbo": wbo_c,
            "w8t": w8t,
            "cos_in": cosb, "sin_in": sinb, "rot_in": rot,
            "tri_in": tri,
            "eye_in": eye,
            "lam_in": lam, "lnw_in": lnw, "lnb_in": lnb,
            "wsc_in": wsc.reshape(1, 1),
            "isc_in": isc.reshape(1, 1),
            "offs_in": offs,
        })
    return in_maps


def kernel(**inputs):
    in_maps = _host_prep(inputs)
    nc = _get_nc()
    res = run_bass_kernel_spmd(nc, in_maps, core_ids=list(range(8)))
    out = np.empty((4, S, D), dtype=np.float32)
    for c in range(8):
        bb, h = c >> 1, c & 1
        out[bb, 1024 * h:1024 * h + 1024, :] = res.results[c]["out_fm"].T
    if TAPS:
        kernel.tap_results = [
            {k: v for k, v in r.items() if k.startswith("tap_")}
            for r in res.results]
    return out


# version 09
# speedup vs baseline: 1.3951x; 1.0405x over previous
import contextlib
import math

import numpy as np
import ml_dtypes

import concourse.bass as bass
import concourse.mybir as mybir
from concourse.bass_utils import run_bass_kernel_spmd

F32 = mybir.dt.float32
BF16 = mybir.dt.bfloat16
FP8 = mybir.dt.float8e4
I32 = mybir.dt.int32
AF = mybir.ActivationFunctionType
ALU = mybir.AluOpType
AXX = mybir.AxisListType.X

D = 2048
S = 2048
TOK = 1024
NCH = 16
HEADS = 8
LAMBDA_INIT = 0.8 - 0.6 * math.exp(-0.3 * 12)
EPS_LN = 1e-5
EPS_RMS = 1e-8
SCALE_A = 64.0 ** -0.5
SCALE_B = 128.0 ** -0.5
N_SUB = 4
TAPS = ()

_ENGS = ("te", "dv", "ac", "gp", "sy")


class _B:

    def __init__(self, nc):
        self.nc = nc
        self.q = {e: [] for e in _ENGS}
        self.cnt = {}
        self.sems = {}
        self.stack = contextlib.ExitStack()
        self._lastwait = {e: {} for e in _ENGS}
        self.guard = {}

    def sem(self, name):
        self.sems[name] = self.stack.enter_context(self.nc.semaphore(name))
        self.cnt[name] = 0

    def wait(self, eng, sem, val=None):
        v = self.cnt[sem] if val is None else val
        if v <= 0:
            return
        if self._lastwait[eng].get(sem, -1) >= v:
            return
        self._lastwait[eng][sem] = v
        h = self.sems[sem]
        self.q[eng].append(lambda e, h=h, v=v: e.wait_ge(h, v))

    def op(self, eng, fn, inc=None, n=1):
        if inc is None:
            self.q[eng].append(fn)
        else:
            h = self.sems[inc]
            self.q[eng].append(lambda e, fn=fn, h=h, n=n: fn(e).then_inc(h, n))
            self.cnt[inc] += n

    def dma(self, eng, out, in_, inc):
        self.op(eng, lambda e, o=out, i=in_: e.dma_start(out=o, in_=i),
                inc=inc, n=16)

    def now(self, sem):
        return self.cnt[sem]

    def need(self, eng, key):
        g = self.guard.get(key)
        if g:
            self.wait(eng, g[0], g[1])

    def setg(self, key, sem):
        self.guard[key] = (sem, self.cnt[sem])

    def dvf(self, fn):
        self.op("dv", fn, inc="dv")
        self.wait("dv", "dv")


def build_nc(n_sub=N_SUB):
    nc = bass.Bass("TRN2", target_bir_lowering=False)
    b = _B(nc)
    st = b.stack
    sb = lambda name, shape, dt: st.enter_context(
        nc.sbuf_tensor(name, list(shape), dt))

    def din(name, shape, dt):
        return nc.dram_tensor(name, list(shape), dt, kind="ExternalInput")

    x_in = din("x_in", [D, TOK], F32)
    wq = din("wq", [D, TOK], F32)
    wk = din("wk", [D, TOK], F32)
    wv = din("wv", [D, TOK], F32)
    wo = din("wo", [TOK, D], F32)
    wbk = din("wbk", [D, TOK], F32)
    wbq = din("wbq", [D, TOK], F32)
    wbv = din("wbv", [D, TOK], F32)
    wbo = din("wbo", [TOK, D], F32)
    w8t = din("w8t", [D, D], FP8)
    cos_in = din("cos_in", [128, S], BF16)
    sin_in = din("sin_in", [128, S], BF16)
    rot_in = din("rot_in", [128, 128], BF16)
    tri_in = din("tri_in", [128, 128], BF16)
    lam_in = din("lam_in", [1, 256], F32)
    lnw_in = din("lnw_in", [128, 1], F32)
    lnb_in = din("lnb_in", [128, 1], F32)
    wsc_in = din("wsc_in", [1, 1], F32)
    isc_in = din("isc_in", [1, 1], F32)
    offs_in = din("offs_in", [1, 4], I32)

    out_fm = nc.dram_tensor("out_fm", [D, TOK], F32, kind="ExternalOutput")

    xbufA = nc.dram_tensor("xbufA", [D, TOK], F32)
    xbufB = nc.dram_tensor("xbufB", [D, TOK], F32)
    agi1 = {a: nc.dram_tensor(f"ag1{a}_in", [D, TOK], BF16) for a in "ab"}
    ago1 = {a: nc.dram_tensor(f"ag1{a}_out", [8 * D, TOK], BF16,
                              addr_space="Shared") for a in "ab"}
    agi2 = {a: nc.dram_tensor(f"ag2{a}_in", [2 * D, TOK], BF16) for a in "ab"}
    ago2 = {a: nc.dram_tensor(f"ag2{a}_out", [16 * D, TOK], BF16,
                              addr_space="Shared") for a in "ab"}
    xnloc = {a: nc.dram_tensor(f"xnloc{a}", [2 * D, TOK], BF16) for a in "ab"}
    parloc = {a: nc.dram_tensor(f"parloc{a}", [2 * D, TOK], BF16) for a in "ab"}

    xn_full = sb("xn_full", [128, NCH * S], BF16)
    scr32 = sb("scr32", [128, NCH * TOK], BF16)
    ab16 = sb("ab16", [128, 8192], BF16)
    k4 = sb("k4", [128, 8192], BF16)
    v4 = sb("v4", [128, 8192], BF16)
    cs16 = sb("cs16", [128, 4096], BF16)
    wpool = sb("wpool", [128, 4096], BF16)
    w8p = sb("w8p", [128, 4096], FP8)
    xch = [sb(f"xch{i}", [128, TOK], F32) for i in (0, 1)]
    sq1 = sb("sq1", [128, TOK], BF16)
    ptile = [sb(f"ptile{i}", [128, 1024], BF16) for i in (0, 1)]
    qraw = sb("qraw", [128, 512], BF16)
    vtmp2 = sb("vtmp2", [128, 512], F32)
    f8t = sb("f8t", [128, 512], FP8)
    w1t = sb("w1t", [128, 512], F32)
    rows = sb("rows", [1, 2560], F32)
    rowsb = sb("rowsb", [1, 2048], BF16)
    maskt = sb("maskt", [128, 128], BF16)
    rotmb = sb("rotmb", [128, 128], BF16)
    ones_m = sb("ones_m", [128, 128], BF16)
    ones_r = sb("ones_r", [1, 128], BF16)
    lamv = sb("lamv", [1, 256], F32)
    lnw = sb("lnw", [128, 1], F32)
    lnb = sb("lnb", [128, 1], F32)
    lnw_eff = sb("lnw_eff", [128, 1], F32)
    lnb_eff = sb("lnb_eff", [128, 1], F32)
    sca = sb("sca", [1, 8], F32)
    onef = sb("onef", [1, 128], F32)
    alpha_c = sb("alpha_c", [128, 1], F32)
    lamn_c = sb("lamn_c", [128, 1], F32)
    offs_sb = sb("offs_sb", [1, 4], I32)

    mb_t = ab16[:, 0:1024]
    rb_t = ab16[:, 1024:2048]
    q4 = ab16
    par_sl = [ab16[:, 0:2048], ab16[:, 2048:4096]]
    cosb = cs16[:, 0:2048]
    sinb = cs16[:, 2048:4096]
    po_sl = [cs16[:, 0:2048], cs16[:, 2048:4096]]

    psS = [st.enter_context(nc.psum_tensor(f"psS{i}", [128, 1024], F32))
           for i in range(2)]
    ps5 = [st.enter_context(nc.psum_tensor(f"c{i}", [128, 512], F32))
           for i in range(4)]

    for s_name in ("sy", "gp", "te", "dv", "ac",
                   "cca1", "cca2", "ccb1", "ccb2"):
        b.sem(s_name)

    tap_t = {}

    def tap(point, buf, cols, waits=("dv", "ac", "te")):
        if point not in TAPS:
            return
        t = nc.dram_tensor(f"tap_{point}", [128, cols], BF16,
                           kind="ExternalOutput")
        tap_t[point] = t
        for w in waits:
            b.wait("sy", w, b.now(w))
        b.dma("sy", t.ap(), buf[:, 0:cols], "sy")

    r_xnb = st.enter_context(nc.gpsimd.register("r_xnb"))
    r_parb = st.enter_context(nc.gpsimd.register("r_parb"))
    r_tmp = st.enter_context(nc.gpsimd.register("r_tmp"))

    for src, dst in ((cos_in, cosb), (sin_in, sinb), (rot_in, rotmb[:, :]),
                     (tri_in, maskt[:, :]), (lam_in, lamv[:, :]),
                     (lnw_in, lnw[:, :]), (lnb_in, lnb[:, :]),
                     (offs_in, offs_sb[:, :])):
        b.dma("sy", dst, src.ap(), "sy")
    b.dma("sy", sca[0:1, 0:1], wsc_in.ap(), "sy")
    b.dma("sy", sca[0:1, 1:2], isc_in.ap(), "sy")

    b.wait("dv", "sy", b.now("sy"))
    b.op("dv", lambda e: e.memset(ones_m[:, :], 1.0))
    b.op("dv", lambda e: e.memset(sca[0:1, 6:7], EPS_LN))
    b.op("dv", lambda e: e.memset(sca[0:1, 7:8], EPS_RMS))
    b.op("dv", lambda e: e.memset(ones_r[:, :], 1.0))
    b.op("dv", lambda e: e.memset(onef[:, :], 1.0))
    b.op("dv", lambda e: e.tensor_mul(rows[0:1, 0:64], lamv[0:1, 0:64],
                                      lamv[0:1, 64:128]))
    b.dvf(lambda e: e.tensor_mul(rows[0:1, 64:128], lamv[0:1, 128:192],
                                 lamv[0:1, 192:256]))
    b.op("dv", lambda e: e.reduce_sum(rows[0:1, 128:129], rows[0:1, 0:64],
                                      axis=AXX))
    b.op("dv", lambda e: e.reduce_sum(rows[0:1, 129:130], rows[0:1, 64:128],
                                      axis=AXX), inc="dv")
    b.wait("ac", "dv", b.now("dv"))
    b.op("ac", lambda e: e.activation(rows[0:1, 130:132], rows[0:1, 128:130],
                                      AF.Exp), inc="ac")
    b.wait("dv", "ac", b.now("ac"))
    b.dvf(lambda e: e.tensor_sub(rows[0:1, 132:133], rows[0:1, 130:131],
                                 rows[0:1, 131:132]))
    b.dvf(lambda e: e.tensor_scalar_add(sca[0:1, 4:5], rows[0:1, 132:133],
                                        LAMBDA_INIT))
    b.dvf(lambda e: e.tensor_scalar_mul(sca[0:1, 5:6], sca[0:1, 4:5], -1.0))
    b.dvf(lambda e: e.reciprocal(sca[0:1, 2:3], sca[0:1, 1:2]))
    b.op("dv", lambda e: e.tensor_mul(sca[0:1, 3:4], sca[0:1, 0:1],
                                      sca[0:1, 1:2]), inc="dv")
    b.wait("ac", "dv", b.now("dv"))
    b.op("ac", lambda e: e.activation(lnw_eff[:, :], lnw[:, :], AF.Copy,
                                      scale=1.0 - LAMBDA_INIT))
    b.op("ac", lambda e: e.activation(lnb_eff[:, :], lnb[:, :], AF.Copy,
                                      scale=1.0 - LAMBDA_INIT), inc="ac")
    b.wait("te", "dv", b.now("dv"))
    b.op("te", lambda e: e.matmul(ps5[0][:, 0:1], onef[0:1, :], sca[0:1, 3:4],
                                  start=True, stop=True))
    b.op("te", lambda e: e.matmul(ps5[0][:, 1:2], onef[0:1, :], sca[0:1, 5:6],
                                  start=True, stop=True), inc="te")
    b.wait("ac", "te", b.now("te"))
    b.op("ac", lambda e: e.activation(alpha_c[:, :], ps5[0][:, 0:1], AF.Copy))
    b.op("ac", lambda e: e.activation(lamn_c[:, :], ps5[0][:, 1:2], AF.Copy),
         inc="ac")
    b.setg("c0", "ac")

    b.wait("gp", "sy", b.now("sy"))
    b.op("gp", lambda e: e.reg_load(r_xnb, offs_sb[0:1, 0:1]))
    b.op("gp", lambda e: e.reg_load(r_parb, offs_sb[0:1, 1:2]))

    def stats_chunk(i, src_bf):
        b.need("ac", "sq1")
        b.op("ac", lambda e, s=src_bf: e.activation(sq1[:, :], s, AF.Square),
             inc="ac")
        b.wait("te", "ac", b.now("ac"))
        for tc in range(2):
            sl = slice(tc * 512, (tc + 1) * 512)
            if i == 0:
                b.need("te", "psS0")
                b.need("te", "psS1")
            b.op("te", lambda e, s=src_bf, sl=sl, i=i:
                 e.matmul(psS[0][:, sl], ones_m[:, :], s[:, sl],
                          start=(i == 0), stop=(i == NCH - 1)))
            b.op("te", lambda e, sl=sl, i=i:
                 e.matmul(psS[1][:, sl], ones_m[:, :], sq1[:, sl],
                          start=(i == 0), stop=(i == NCH - 1)),
                 inc=("te" if (i == NCH - 1 and tc == 1) else None))
        b.setg("sq1", "te")

    def ln_finish(ffn):
        b.wait("ac", "te", b.now("te"))
        for tc in range(2):
            b.op("ac", lambda e, tc=tc: e.activation(
                rows[0:1, tc * 512:(tc + 1) * 512],
                psS[0][0:1, tc * 512:(tc + 1) * 512], AF.Copy, scale=1.0 / D))
            b.op("ac", lambda e, tc=tc: e.activation(
                rows[0:1, 1024 + tc * 512:1024 + (tc + 1) * 512],
                psS[1][0:1, tc * 512:(tc + 1) * 512], AF.Copy, scale=1.0 / D),
                inc="ac")
        b.setg("psS0", "ac")
        b.setg("psS1", "ac")
        b.wait("dv", "ac", b.now("ac"))
        for tc in range(2):
            m_h = rows[0:1, tc * 512:(tc + 1) * 512]
            e_h = rows[0:1, 1024 + tc * 512:1024 + (tc + 1) * 512]
            t_h = rows[0:1, 2048:2560]
            b.dvf(lambda e, m=m_h, t=t_h: e.tensor_mul(t, m, m))
            b.op("dv", lambda e, eh=e_h, t=t_h: e.tensor_sub(eh, eh, t),
                 inc="dv")
            b.wait("dv", "dv")
        b.wait("ac", "dv", b.now("dv"))
        var_r = rows[0:1, 1024:2048]
        b.op("ac", lambda e: e.activation(var_r, var_r, AF.Ln,
                                          bias=sca[0:1, 6:7]))
        b.op("ac", lambda e: e.activation(var_r, var_r, AF.Exp, scale=-0.5),
             inc="ac")
        b.wait("dv", "ac", b.now("ac"))
        if ffn:
            b.dvf(lambda e: e.tensor_scalar(var_r, var_r, sca[0:1, 2:3],
                                            None, ALU.mult))
        b.op("dv", lambda e: e.tensor_copy(rowsb[0:1, 0:1024],
                                           rows[0:1, 0:1024]))
        b.op("dv", lambda e: e.tensor_copy(rowsb[0:1, 1024:2048], var_r),
             inc="dv")
        b.wait("te", "dv", b.now("dv"))
        for tc in range(2):
            b.need("te", f"c{tc}")
            b.need("te", f"c{2 + tc}")
            b.op("te", lambda e, tc=tc: e.matmul(
                ps5[tc][:, :], ones_r[0:1, :],
                rowsb[0:1, tc * 512:(tc + 1) * 512], start=True, stop=True))
            b.op("te", lambda e, tc=tc: e.matmul(
                ps5[2 + tc][:, :], ones_r[0:1, :],
                rowsb[0:1, 1024 + tc * 512:1024 + (tc + 1) * 512],
                start=True, stop=True), inc="te")
        b.wait("ac", "te", b.now("te"))
        b.need("ac", "ab16")
        for tc in range(2):
            b.op("ac", lambda e, tc=tc: e.activation(
                mb_t[:, tc * 512:(tc + 1) * 512], ps5[tc][:, :], AF.Copy))
            b.op("ac", lambda e, tc=tc: e.activation(
                rb_t[:, tc * 512:(tc + 1) * 512], ps5[2 + tc][:, :], AF.Copy),
                inc="ac")
        for i in range(4):
            b.setg(f"c{i}", "ac")
        b.wait("dv", "ac", b.now("ac"))
        for i in range(NCH):
            for tc in range(2):
                s_ = scr32[:, i * TOK + tc * 512:i * TOK + (tc + 1) * 512]
                b.dvf(lambda e, s=s_, tc=tc:
                      e.tensor_sub(s, s, mb_t[:, tc * 512:(tc + 1) * 512]))
                b.op("dv", lambda e, s=s_, tc=tc:
                     e.tensor_mul(s, s, rb_t[:, tc * 512:(tc + 1) * 512]),
                     inc=("dv" if (i == NCH - 1 and tc == 1) else None))
        b.setg("ab16", "dv")

    def first_stats():
        for i in range(NCH):
            xc = xch[i % 2]
            b.need("sy", f"xch{i % 2}")
            b.dma("sy", xc[:, :], x_in.ap()[i * 128:(i + 1) * 128, :], "sy")
            b.wait("dv", "sy", b.now("sy"))
            dst = scr32[:, i * TOK:(i + 1) * TOK]
            b.op("dv", lambda e, s=xc, d=dst: e.tensor_copy(d, s[:, :]),
                 inc="dv")
            b.setg(f"xch{i % 2}", "dv")
            b.wait("ac", "dv", b.now("dv"))
            stats_chunk(i, dst)

    def xn_exchange(which):
        agi, ago = agi1[which], ago1[which]
        cc = "cca1" if which == "a" else "ccb1"
        dstd = agi.ap().rearrange("(i p) c -> p i c", p=128)
        srcd = scr32.ap().rearrange("p (i c) -> p i c", c=TOK)
        b.wait("sy", "dv", b.now("dv"))
        b.dma("sy", dstd, srcd, "sy")
        b.wait("gp", "sy", b.now("sy"))
        b.op("gp", lambda e, i=agi, o=ago: e.collective_compute(
            "AllGather", ALU.bypass, replica_groups=[list(range(8))],
            ins=[i.ap().opt()], outs=[o.ap().opt()]), inc=cc)
        b.wait("gp", cc, 1)
        xl = xnloc[which]
        srcflat = bass.AP(ago, r_xnb, [[1, 2 * D * TOK]])
        dstflat = bass.AP(xl, 0, [[1, 2 * D * TOK]])
        b.op("gp", lambda e, d=dstflat, s=srcflat:
             e.dma_start(out=d, in_=s), inc="gp", n=16)
        b.wait("sy", "gp", b.now("gp"))
        b.need("sy", "xn_full")
        for r in (0, 1):
            dstap = bass.AP(xn_full, r * TOK,
                            [[NCH * S, 128], [S, NCH], [1, TOK]])
            srcap = bass.AP(xl, r * D * TOK,
                            [[TOK, 128], [128 * TOK, NCH], [1, TOK]])
            b.dma("sy", dstap, srcap, "sy")
        b.wait("te", "sy", b.now("sy"))

    def wblock(src_dram, j, slot, rows_n=NCH):
        dst = wpool.ap()[:, slot * 2048: slot * 2048 + rows_n * 128].rearrange(
            "p (i c) -> p i c", c=128)
        src = src_dram.ap()[:, j * 128:(j + 1) * 128].rearrange(
            "(i p) c -> p i c", p=128)
        b.need("gp", f"wslot{slot}")
        b.dma("gp", dst, src, "gp")

    def wtile(slot, i):
        return wpool[:, slot * 2048 + i * 128: slot * 2048 + (i + 1) * 128]

    def wvblock(src_dram, col0):
        dst = wpool.ap().rearrange("p (i c) -> p i c", c=256)
        src = src_dram.ap()[:, col0:col0 + 256].rearrange(
            "(i p) c -> p i c", p=128)
        b.need("gp", "wslot0")
        b.need("gp", "wslot1")
        b.dma("gp", dst, src, "gp")

    def wvtile(i):
        return wpool[:, i * 256:(i + 1) * 256]

    def attn_sub(which, xsrc, xdst, fuse_stats, last):
        is_a = which == "a"
        wlq, wlk, wrv, wout = (wq, wk, wv, wo) if is_a else (wbq, wbk, wbv, wbo)
        scale = SCALE_A if is_a else SCALE_B
        cc2 = "cca2" if is_a else "ccb2"

        xn_exchange(which)

        for g in range(2):
            blocklist = [(kind, jj) for kind in ("k", "q") for jj in range(4)]
            gp_after = {}

            def load_blk(idx):
                kind, jj = blocklist[idx]
                wsrc = {"k": wlk, "q": wlq}[kind]
                wblock(wsrc, 4 * g + jj, idx % 2)
                gp_after[idx] = b.now("gp")

            load_blk(0)
            for bi, (kind, jj) in enumerate(blocklist):
                dbuf = k4 if kind == "k" else q4
                slot = bi % 2
                if bi + 1 < len(blocklist):
                    load_blk(bi + 1)
                b.wait("te", "gp", gp_after[bi])
                for tc in range(4):
                    pj = ps5[tc % 2]
                    b.need("te", f"c{tc % 2}")
                    for i in range(NCH):
                        rh = xn_full[:, i * S + tc * 512:
                                     i * S + (tc + 1) * 512]
                        b.op("te", lambda e, p=pj, l=wtile(slot, i), r=rh,
                             i=i: e.matmul(p[:, :], l, r, start=(i == 0),
                                           stop=(i == NCH - 1)),
                             inc=("te" if i == NCH - 1 else None))
                    dsl = dbuf[:, jj * S + tc * 512:jj * S + (tc + 1) * 512]
                    if is_a:
                        b.wait("ac", "te", b.now("te"))
                        b.need("ac", "qraw")
                        b.op("ac", lambda e, p=pj: e.activation(
                            qraw[:, :], p[:, :], AF.Copy), inc="ac")
                        b.setg(f"c{tc % 2}", "ac")
                        b.wait("te", "ac", b.now("ac"))
                        prot = ps5[2 + tc % 2]
                        b.need("te", f"c{2 + tc % 2}")
                        b.op("te", lambda e, p=prot: e.matmul(
                            p[:, :], rotmb[:, :], qraw[:, :],
                            start=True, stop=True), inc="te")
                        csl = cosb[:, tc * 512:(tc + 1) * 512]
                        ssl = sinb[:, tc * 512:(tc + 1) * 512]
                        b.wait("dv", "te", b.now("te"))
                        b.op("dv", lambda e, d=dsl, c=csl:
                             e.tensor_mul(d, qraw[:, :], c))
                        b.dvf(lambda e, p=prot, s=ssl:
                              e.tensor_mul(w1t[:, :], p[:, :], s))
                        b.op("dv", lambda e, d=dsl:
                             e.tensor_add(d, d, w1t[:, :]), inc="dv")
                        b.setg("qraw", "dv")
                        b.setg(f"c{2 + tc % 2}", "dv")
                    else:
                        b.wait("ac", "te", b.now("te"))
                        b.op("ac", lambda e, p=pj, d=dsl: e.activation(
                            d, p[:, :], AF.Copy), inc="ac")
                        b.setg(f"c{tc % 2}", "ac")
                b.setg(f"wslot{slot}", "te")

            for vb in range(2):
                wvblock(wrv, g * 512 + vb * 256)
                b.wait("te", "gp", b.now("gp"))
                for kc in range(NCH):
                    pj = ps5[kc % 2]
                    b.need("te", f"c{kc % 2}")
                    for i in range(NCH):
                        lt = xn_full[:, i * S + kc * 128:i * S + (kc + 1) * 128]
                        b.op("te", lambda e, p=pj, l=lt, r=wvtile(i), i=i:
                             e.matmul(p[:, 0:256], l, r, start=(i == 0),
                                      stop=(i == NCH - 1)),
                             inc=("te" if i == NCH - 1 else None))
                    dsl = v4[:, kc * 512 + vb * 256:kc * 512 + (vb + 1) * 256]
                    b.wait("ac", "te", b.now("te"))
                    b.op("ac", lambda e, p=pj, d=dsl: e.activation(
                        d, p[:, 0:256], AF.Copy), inc="ac")
                    b.setg(f"c{kc % 2}", "ac")
                b.setg("wslot0", "te")
                b.setg("wslot1", "te")
            if g == 1:
                b.setg("xn_full", "te")

            b.wait("te", "ac", b.now("ac"))
            b.wait("te", "dv", b.now("dv"))
            for jj in range(4):
                head = 4 * g + jj
                for qc in range(4):
                    nkc = 4 * (qc + 1)
                    npair = nkc if is_a else nkc // 2
                    b.need("te", "c0")
                    b.need("te", "c1")
                    b.need("te", "c2")
                    b.need("te", "c3")

                    def emit_Spair(t):
                        sp = psS[t % 2]
                        b.need("te", f"psS{t % 2}")
                        rh = q4[:, jj * S + qc * 512:jj * S + (qc + 1) * 512]
                        if is_a:
                            lt = k4[:, jj * S + t * 128:jj * S + (t + 1) * 128]
                            b.op("te", lambda e, p=sp, l=lt, r=rh: e.matmul(
                                p[:, 0:512], l[0:64, :], r[0:64, :],
                                start=True, stop=True))
                            b.op("te", lambda e, p=sp, l=lt, r=rh: e.matmul(
                                p[:, 512:1024], l[64:128, :], r[64:128, :],
                                start=True, stop=True), inc="te")
                        else:
                            l0 = k4[:, jj * S + 2 * t * 128:
                                    jj * S + (2 * t + 1) * 128]
                            l1 = k4[:, jj * S + (2 * t + 1) * 128:
                                    jj * S + (2 * t + 2) * 128]
                            b.op("te", lambda e, p=sp, l=l0, r=rh: e.matmul(
                                p[:, 0:512], l, r, start=True, stop=True))
                            b.op("te", lambda e, p=sp, l=l1, r=rh: e.matmul(
                                p[:, 512:1024], l, r, start=True, stop=True),
                                inc="te")
                        return b.now("te")

                    s_done = {0: emit_Spair(0)}
                    for t in range(npair):
                        pt = ptile[t % 2]
                        sp = psS[t % 2]
                        if t + 1 < npair:
                            s_done[t + 1] = emit_Spair(t + 1)
                        b.wait("ac", "te", s_done[t])
                        b.need("ac", f"pt{t % 2}")
                        b.op("ac", lambda e, p=sp, t_=pt: e.activation(
                            t_[:, 0:512], p[:, 0:512], AF.Exp, scale=scale))
                        b.op("ac", lambda e, p=sp, t_=pt: e.activation(
                            t_[:, 512:1024], p[:, 512:1024], AF.Exp,
                            scale=scale), inc="ac")
                        b.setg(f"psS{t % 2}", "ac")
                        if is_a:
                            mask_jobs = [(t, (0, 512))]
                        else:
                            mask_jobs = [(2 * t, (0,)), (2 * t + 1, (512,))]
                        did_mask = False
                        for kc, tgt in mask_jobs:
                            m = kc - 4 * qc
                            if m < 0:
                                continue
                            if not did_mask:
                                b.wait("dv", "ac", b.now("ac"))
                            did_mask = True
                            for o2 in tgt:
                                if m > 0:
                                    b.op("dv", lambda e, t_=pt, m=m, o=o2:
                                         e.memset(t_[:, o:o + m * 128], 0.0))
                                b.op("dv", lambda e, t_=pt, m=m, o=o2:
                                     e.tensor_mul(
                                         t_[:, o + m * 128:o + (m + 1) * 128],
                                         t_[:, o + m * 128:o + (m + 1) * 128],
                                         maskt[:, :]), inc="dv")
                        if did_mask:
                            b.wait("te", "dv", b.now("dv"))
                        else:
                            b.wait("te", "ac", b.now("ac"))
                        if is_a:
                            vl = v4[:, t * 512 + jj * 128:
                                    t * 512 + (jj + 1) * 128]
                            b.op("te", lambda e, v=vl, t_=pt, t0=t: e.matmul(
                                ps5[0][:, :], v, t_[:, 0:512],
                                start=(t0 == 0), stop=(t0 == npair - 1)))
                            b.op("te", lambda e, v=vl, t_=pt, t0=t: e.matmul(
                                ps5[1][:, :], v, t_[:, 512:1024],
                                start=(t0 == 0), stop=(t0 == npair - 1)))
                            b.op("te", lambda e, t_=pt, t0=t: e.matmul(
                                ps5[2][:, :], ones_m[:, :], t_[:, 0:512],
                                start=(t0 == 0), stop=(t0 == npair - 1)))
                            b.op("te", lambda e, t_=pt, t0=t: e.matmul(
                                ps5[3][:, :], ones_m[:, :], t_[:, 512:1024],
                                start=(t0 == 0), stop=(t0 == npair - 1)),
                                inc="te")
                        else:
                            for hi, kc in enumerate((2 * t, 2 * t + 1)):
                                vl = v4[:, kc * 512 + jj * 128:
                                        kc * 512 + (jj + 1) * 128]
                                o = hi * 512
                                fst = kc == 0
                                lst = kc == nkc - 1
                                b.op("te", lambda e, v=vl, t_=pt, o=o, f=fst,
                                     l_=lst: e.matmul(
                                         ps5[0][:, :], v, t_[:, o:o + 512],
                                         start=f, stop=l_))
                                b.op("te", lambda e, t_=pt, o=o, f=fst, l_=lst:
                                     e.matmul(
                                         ps5[2][:, :], ones_m[:, :],
                                         t_[:, o:o + 512], start=f, stop=l_),
                                     inc="te")
                        b.setg(f"pt{t % 2}", "te")

                    b.wait("dv", "te", b.now("te"))
                    zsl = scr32[:, head * S + qc * 512:
                                head * S + (qc + 1) * 512]
                    if is_a:
                        b.dvf(lambda e: e.reciprocal(vtmp2[:, :],
                                                     ps5[2][:, :]))
                        b.dvf(lambda e: e.tensor_mul(w1t[:, :], ps5[3][:, :],
                                                     vtmp2[:, :]))
                        b.dvf(lambda e: e.tensor_mul(vtmp2[:, :], ps5[0][:, :],
                                                     w1t[:, :]))
                        b.op("dv", lambda e, z=zsl: e.scalar_tensor_tensor(
                            z, ps5[1][:, :], lamn_c[:, 0:1], vtmp2[:, :],
                            ALU.mult, ALU.add), inc="dv")
                    else:
                        b.dvf(lambda e: e.reciprocal(vtmp2[:, :],
                                                     ps5[2][:, :]))
                        b.op("dv", lambda e, z=zsl: e.tensor_mul(
                            z, ps5[0][:, :], vtmp2[:, :]), inc="dv")
                    for i in range(4):
                        b.setg(f"c{i}", "dv")

                if is_a:
                    b.need("te", "psS0")
                    b.need("te", "psS1")
                    for seg in range(2):
                        hsl = scr32[:, head * S + seg * TOK:
                                    head * S + (seg + 1) * TOK]
                        b.need("ac", "sq1")
                        b.wait("ac", "dv", b.now("dv"))
                        b.op("ac", lambda e, h=hsl: e.activation(
                            sq1[:, :], h, AF.Square), inc="ac")
                        b.wait("te", "ac", b.now("ac"))
                        for sc in range(2):
                            b.op("te", lambda e, sc=sc, seg=seg: e.matmul(
                                psS[seg][:, sc * 512:(sc + 1) * 512],
                                ones_m[:, :], sq1[:, sc * 512:(sc + 1) * 512],
                                start=True, stop=True),
                                 inc=("te" if sc == 1 else None))
                        b.setg("sq1", "te")
                    b.wait("ac", "te", b.now("te"))
                    for sc4 in range(4):
                        b.op("ac", lambda e, sc4=sc4: e.activation(
                            rows[0:1, sc4 * 512:(sc4 + 1) * 512],
                            psS[sc4 // 2][0:1, (sc4 % 2) * 512:
                                          (sc4 % 2) * 512 + 512],
                            AF.Copy, scale=1.0 / 128), inc="ac")
                    b.setg("psS0", "ac")
                    b.setg("psS1", "ac")
                    b.op("ac", lambda e: e.activation(
                        rows[0:1, 0:2048], rows[0:1, 0:2048], AF.Ln,
                        bias=sca[0:1, 7:8]))
                    b.op("ac", lambda e: e.activation(
                        rows[0:1, 0:2048], rows[0:1, 0:2048], AF.Exp,
                        scale=-0.5), inc="ac")
                    b.wait("dv", "ac", b.now("ac"))
                    b.op("dv", lambda e: e.tensor_copy(rowsb[0:1, 0:2048],
                                                       rows[0:1, 0:2048]),
                         inc="dv")
                    b.wait("te", "dv", b.now("dv"))
                    for sc4 in range(4):
                        pb = ps5[sc4 % 2]
                        b.need("te", f"c{sc4 % 2}")
                        b.op("te", lambda e, p=pb, sc4=sc4: e.matmul(
                            p[:, :], ones_r[0:1, :],
                            rowsb[0:1, sc4 * 512:(sc4 + 1) * 512],
                            start=True, stop=True), inc="te")
                        b.wait("dv", "te", b.now("te"))
                        hsl = scr32[:, head * S + sc4 * 512:
                                    head * S + (sc4 + 1) * 512]
                        b.op("dv", lambda e, h=hsl, p=pb:
                             e.tensor_mul(h, h, p[:, :]), inc="dv")
                        b.wait("dv", "dv")
                        b.setg(f"c{sc4 % 2}", "dv")
                    for q2 in range(4):
                        hsl = scr32[:, head * S + q2 * 512:
                                    head * S + (q2 + 1) * 512]
                        b.op("dv", lambda e, h=hsl: e.tensor_scalar(
                            h, h, lnw_eff[:, 0:1], lnb_eff[:, 0:1],
                            ALU.mult, ALU.add), inc="dv")

        if which == "a":
            tap("zbuf", scr32, NCH * TOK)
        agi, ago = agi2[which], ago2[which]
        b.wait("te", "ac", b.now("ac"))
        b.wait("te", "dv", b.now("dv"))
        b.wait("ac", "te", b.now("te"))
        gp_after_wo = {}

        def load_wo(oc):
            wblock(wout, oc, oc % 2, rows_n=8)
            gp_after_wo[oc] = b.now("gp")

        load_wo(0)
        for oc in range(NCH):
            slot = oc % 2
            if oc + 1 < NCH:
                load_wo(oc + 1)
            b.wait("te", "gp", gp_after_wo[oc])
            psl = par_sl[oc % 2]
            b.need("ac", f"par{oc % 2}")
            for tc in range(4):
                pj = ps5[tc % 2]
                b.need("te", f"c{tc % 2}")
                for i in range(8):
                    rh = scr32[:, i * S + tc * 512:i * S + (tc + 1) * 512]
                    b.op("te", lambda e, p=pj, l=wtile(slot, i), r=rh, i=i:
                         e.matmul(p[:, :], l, r, start=(i == 0),
                                  stop=(i == 7)),
                         inc=("te" if i == 7 else None))
                b.wait("ac", "te", b.now("te"))
                b.op("ac", lambda e, p=pj, d=psl[:, tc * 512:(tc + 1) * 512]:
                     e.activation(d, p[:, :], AF.Copy), inc="ac")
                b.setg(f"c{tc % 2}", "ac")
            b.setg(f"wslot{slot}", "te")
            b.wait("sy", "ac", b.now("ac"))
            for hh in (0, 1):
                b.dma("sy",
                      agi.ap()[hh * D + oc * 128:hh * D + (oc + 1) * 128, :],
                      psl[:, hh * TOK:(hh + 1) * TOK], "sy")
            b.setg(f"par{oc % 2}", "sy")
        b.wait("gp", "sy", b.now("sy"))
        if "par" in TAPS and which == "a":
            tpa = nc.dram_tensor("tap_agi2", [2 * D, TOK], BF16,
                                 kind="ExternalOutput")
            tap_t["agi2"] = tpa
            b.dma("gp", tpa.ap(), agi.ap(), "gp")
        b.op("gp", lambda e, i=agi, o=ago: e.collective_compute(
            "AllGather", ALU.bypass, replica_groups=[list(range(8))],
            ins=[i.ap().opt()], outs=[o.ap().opt()]), inc=cc2)
        b.wait("gp", cc2, 1)
        stream_update(which, xsrc, xdst, fuse_stats, last, ago)
        if "par" in TAPS and which == "a":
            tpp = nc.dram_tensor("tap_parloc", [2 * D, TOK], BF16,
                                 kind="ExternalOutput")
            tap_t["parloc"] = tpp
            b.dma("gp", tpp.ap(), parloc[which].ap(), "gp")

    def stream_update(which, xsrc, xdst, fuse_stats, last, ago):
        pl = parloc[which]
        for r in (0, 1):
            dstrot = bass.AP(pl, r * D * TOK, [[1, D * TOK]])
            if r == 0:
                srcrot = bass.AP(ago, r_parb, [[1, D * TOK]])
            else:
                b.op("gp", lambda e: e.reg_mov(r_tmp, 2 * D * TOK))
                b.op("gp", lambda e: e.reg_add(r_tmp, r_tmp, r_parb))
                srcrot = bass.AP(ago, r_tmp, [[1, D * TOK]])
            b.op("gp", lambda e, d=dstrot, s=srcrot:
                 e.dma_start(out=d, in_=s), inc="gp", n=16)
        b.wait("gp", "gp", b.now("gp"))
        gp_po = {}
        sy_xc = {}

        def load_chunk(i):
            pob = po_sl[i % 2]
            b.need("gp", f"po{i % 2}")
            srcap = bass.AP(pl, i * 128 * TOK,
                            [[TOK, 128], [D * TOK, 2], [1, TOK]])
            dstap = pob.rearrange("p (r c) -> p r c", c=TOK)
            b.op("gp", lambda e, d=dstap, s=srcap:
                 e.dma_start(out=d, in_=s), inc="gp", n=16)
            gp_po[i] = b.now("gp")
            xc = xch[i % 2]
            b.need("sy", f"xch{i % 2}")
            b.dma("sy", xc[:, :], xsrc.ap()[i * 128:(i + 1) * 128, :], "sy")
            sy_xc[i] = b.now("sy")

        for i in range(NCH):
            load_chunk(i)
            pob = po_sl[i % 2]
            xc = xch[i % 2]
            b.wait("dv", "gp", gp_po[i])
            b.wait("dv", "sy", sy_xc[i])
            for tc in range(2):
                sl = slice(tc * 512, (tc + 1) * 512)
                sl2 = slice(1024 + tc * 512, 1024 + (tc + 1) * 512)
                b.dvf(lambda e, p=pob, sl=sl, sl2=sl2:
                      e.tensor_add(w1t[:, :], p[:, sl], p[:, sl2]))
                b.op("dv", lambda e, x=xc, sl=sl:
                     e.tensor_add(x[:, sl], x[:, sl], w1t[:, :]), inc="dv")
                b.wait("dv", "dv")
            b.setg(f"po{i % 2}", "dv")
            if fuse_stats:
                dstn = scr32[:, i * TOK:(i + 1) * TOK]
                b.op("dv", lambda e, s=xc, d=dstn: e.tensor_copy(d, s[:, :]),
                     inc="dv")
                b.wait("ac", "dv", b.now("dv"))
                stats_chunk(i, dstn)
            b.wait("sy", "dv", b.now("dv"))
            b.dma("sy", xdst.ap()[i * 128:(i + 1) * 128, :], xc[:, :], "sy")
            b.setg(f"xch{i % 2}", "sy")

    def ffn_sub(xsrc, xdst, fuse_stats, last):
        b.wait("dv", "ac", b.now("ac"))
        for i in range(NCH):
            for tc in range(2):
                s_ = scr32[:, i * TOK + tc * 512:i * TOK + (tc + 1) * 512]
                b.dvf(lambda e, s=s_: e.tensor_scalar(
                    f8t[:, :], s, 240.0, -240.0, ALU.min, ALU.max))
                b.op("dv", lambda e, s=s_: e.tensor_copy(s, f8t[:, :]),
                     inc="dv")
                b.wait("dv", "dv")
        sy_after_w8 = {}

        def load_w8(oc):
            slot_ = oc % 2
            dst8 = w8p.ap()[:, slot_ * 2048:(slot_ + 1) * 2048].rearrange(
                "p (i c) -> p i c", c=128)
            src8 = w8t.ap()[:, oc * 128:(oc + 1) * 128].rearrange(
                "(i p) c -> p i c", p=128)
            b.need("sy", f"w8s{slot_}")
            b.dma("sy", dst8, src8, "sy")
            sy_after_w8[oc] = b.now("sy")

        load_w8(0)
        for oc in range(NCH):
            slot = oc % 2
            if oc + 1 < NCH:
                load_w8(oc + 1)
            b.wait("dv", "sy", sy_after_w8[oc])
            b.need("dv", f"wslot{slot}")
            b.op("dv", lambda e, s=slot: e.tensor_copy(
                wpool[:, s * 2048:(s + 1) * 2048],
                w8p[:, s * 2048:(s + 1) * 2048]), inc="dv")
            b.setg(f"w8s{slot}", "dv")
            b.wait("te", "dv", b.now("dv"))
            xc = xch[oc % 2]
            b.need("sy", f"xch{oc % 2}")
            b.dma("sy", xc[:, :], xsrc.ap()[oc * 128:(oc + 1) * 128, :], "sy")
            for tc in range(2):
                pj = ps5[tc % 2]
                b.need("te", f"c{tc % 2}")
                for i in range(NCH):
                    rh = scr32[:, i * TOK + tc * 512:i * TOK + (tc + 1) * 512]
                    b.op("te", lambda e, p=pj, l=wtile(slot, i), r=rh, i=i:
                         e.matmul(p[:, :], l, r, start=(i == 0),
                                  stop=(i == NCH - 1)),
                         inc=("te" if i == NCH - 1 else None))
            b.setg(f"wslot{slot}", "te")
            b.wait("dv", "te", b.now("te"))
            b.wait("dv", "sy", b.now("sy"))
            for tc in range(2):
                pj = ps5[tc % 2]
                sl = slice(tc * 512, (tc + 1) * 512)
                b.op("dv", lambda e, p=pj, x=xc, sl=sl:
                     e.scalar_tensor_tensor(x[:, sl], p[:, :],
                                            alpha_c[:, 0:1], x[:, sl],
                                            ALU.mult, ALU.add), inc="dv")
                b.setg(f"c{tc % 2}", "dv")
            b.wait("sy", "dv", b.now("dv"))
            b.dma("sy", xdst.ap()[oc * 128:(oc + 1) * 128, :], xc[:, :], "sy")
            b.setg(f"xch{oc % 2}", "sy")
        if fuse_stats:
            b.wait("sy", "sy", b.now("sy"))
            for i in range(NCH):
                xc = xch[i % 2]
                b.need("sy", f"xch{i % 2}")
                b.dma("sy", xc[:, :], xdst.ap()[i * 128:(i + 1) * 128, :],
                      "sy")
                b.wait("dv", "sy", b.now("sy"))
                dstn = scr32[:, i * TOK:(i + 1) * TOK]
                b.op("dv", lambda e, s=xc, d=dstn: e.tensor_copy(d, s[:, :]),
                     inc="dv")
                b.setg(f"xch{i % 2}", "dv")
                b.wait("ac", "dv", b.now("dv"))
                stats_chunk(i, dstn)

    first_stats()
    ln_finish(ffn=False)
    tap("xnw1", scr32, NCH * TOK)
    attn_sub("a", x_in, xbufA, fuse_stats=(n_sub >= 2), last=(n_sub == 1))
    if n_sub >= 2:
        ln_finish(ffn=True)
        ffn_sub(xbufA, xbufB, fuse_stats=(n_sub >= 3), last=(n_sub == 2))
    if n_sub >= 3:
        ln_finish(ffn=False)
        attn_sub("b", xbufB, xbufA, fuse_stats=(n_sub >= 4), last=(n_sub == 3))
    if n_sub >= 4:
        ln_finish(ffn=True)
        ffn_sub(xbufA, out_fm, fuse_stats=False, last=True)

    fin = {1: xbufA, 2: xbufB, 3: xbufA}.get(n_sub)
    if fin is not None and n_sub < 4:
        b.wait("gp", "sy", b.now("sy"))
        b.dma("gp", out_fm.ap(), fin.ap(), "gp")

    b.wait("sy", "sy", b.now("sy"))
    b.wait("sy", "gp", b.now("gp"))

    with nc.Block() as block:
        for eng_name, attr in (("te", "tensor"), ("dv", "vector"),
                               ("ac", "scalar"), ("gp", "gpsimd"),
                               ("sy", "sync")):
            ops = b.q[eng_name]

            def runner(e, ops=ops):
                for f in ops:
                    f(e)
            getattr(block, attr)(runner)

    b.stack.close()
    return nc


_NC_CACHE = {}


def _get_nc():
    key = N_SUB
    if key not in _NC_CACHE:
        _NC_CACHE[key] = build_nc(key)
    return _NC_CACHE[key]


def _rope_tables():
    dd = 64
    inv = (1.0 / np.float32(10000.0) **
           (np.arange(0, dd, 2, dtype=np.float32) / np.float32(dd)))
    t = np.arange(S, dtype=np.float32)
    freqs = (t[:, None] * inv[None, :]).astype(np.float32)
    freqs = np.repeat(freqs, 2, axis=1)
    cos = np.cos(freqs).astype(np.float32)
    sin = np.sin(freqs).astype(np.float32)
    cosf = np.concatenate([cos.T, cos.T], axis=0)
    sinf = np.concatenate([sin.T, sin.T], axis=0)
    return (np.ascontiguousarray(cosf).astype(ml_dtypes.bfloat16),
            np.ascontiguousarray(sinf).astype(ml_dtypes.bfloat16))


def _host_prep(inputs):
    x = np.asarray(inputs["L0_attn_a_x"])
    Wq = np.asarray(inputs["L0_attn_a_Wq"])
    Wk = np.asarray(inputs["L0_attn_a_Wk"])
    Wv = np.asarray(inputs["L0_attn_a_Wv"])
    Wo = np.asarray(inputs["L0_attn_a_Wo"])
    WK = np.asarray(inputs["L0_attn_b_W_K"])
    WQ = np.asarray(inputs["L0_attn_b_W_Q"])
    WV = np.asarray(inputs["L0_attn_b_W_V"])
    WO = np.asarray(inputs["L0_attn_b_W_O"])
    w8 = np.asarray(inputs["L0_ffn_weight"])
    wsc = np.float32(inputs["L0_ffn_weight_scale"])
    isc = np.float32(inputs["L0_ffn_input_scale"])
    lam = np.concatenate([np.asarray(inputs["L0_attn_a_lambda_q1"]),
                          np.asarray(inputs["L0_attn_a_lambda_k1"]),
                          np.asarray(inputs["L0_attn_a_lambda_q2"]),
                          np.asarray(inputs["L0_attn_a_lambda_k2"])]
                         ).astype(np.float32).reshape(1, 256)
    lnw = np.asarray(inputs["L0_attn_a_ln_w"]).astype(np.float32).reshape(128, 1)
    lnb = np.asarray(inputs["L0_attn_a_ln_b"]).astype(np.float32).reshape(128, 1)

    cosb, sinb = _rope_tables()
    rot = np.zeros((128, 128), dtype=np.float32)
    for p in range(0, 128, 2):
        rot[p + 1, p] = -1.0
        rot[p, p + 1] = 1.0
    rot = np.ascontiguousarray(rot).astype(ml_dtypes.bfloat16)
    tri = np.ascontiguousarray(
        np.triu(np.ones((128, 128), dtype=np.float32))).astype(
        ml_dtypes.bfloat16)
    w8t = np.ascontiguousarray(w8.T).view(ml_dtypes.float8_e4m3)

    in_maps = []
    for c in range(8):
        bb, h = c >> 1, c & 1
        tok = slice(1024 * h, 1024 * h + 1024)
        qk = slice(1024 * h, 1024 * h + 1024)
        heads = slice(8 * h, 8 * h + 8)
        x_fm = np.ascontiguousarray(x[bb, tok, :].T)
        wbk_c = np.ascontiguousarray(
            WK[heads].transpose(2, 0, 1).reshape(D, 1024))
        wbq_c = np.ascontiguousarray(
            WQ[heads].transpose(2, 0, 1).reshape(D, 1024))
        wbv_c = np.ascontiguousarray(
            WV[heads].transpose(2, 0, 1).reshape(D, 1024))
        wbo_c = np.ascontiguousarray(WO[:, qk].T)
        offs = np.array([[(c & ~1) * D * TOK,
                          (c & ~1) * 2 * D * TOK + h * D * TOK, 0, 0]],
                        dtype=np.int32)
        in_maps.append({
            "x_in": x_fm,
            "wq": np.ascontiguousarray(Wq[:, qk]),
            "wk": np.ascontiguousarray(Wk[:, qk]),
            "wv": np.ascontiguousarray(Wv[:, qk]),
            "wo": np.ascontiguousarray(Wo[qk, :]),
            "wbk": wbk_c, "wbq": wbq_c, "wbv": wbv_c, "wbo": wbo_c,
            "w8t": w8t,
            "cos_in": cosb, "sin_in": sinb, "rot_in": rot,
            "tri_in": tri,
            "lam_in": lam, "lnw_in": lnw, "lnb_in": lnb,
            "wsc_in": wsc.reshape(1, 1),
            "isc_in": isc.reshape(1, 1),
            "offs_in": offs,
        })
    return in_maps


def kernel(**inputs):
    in_maps = _host_prep(inputs)
    nc = _get_nc()
    res = run_bass_kernel_spmd(nc, in_maps, core_ids=list(range(8)))
    out = np.empty((4, S, D), dtype=np.float32)
    for c in range(8):
        bb, h = c >> 1, c & 1
        out[bb, 1024 * h:1024 * h + 1024, :] = res.results[c]["out_fm"].T
    if TAPS:
        kernel.tap_results = [
            {k: v for k, v in r.items() if k.startswith("tap_")}
            for r in res.results]
    return out


# version 10
# speedup vs baseline: 1.5939x; 1.1425x over previous
import contextlib
import math

import numpy as np
import ml_dtypes

import concourse.bass as bass
import concourse.mybir as mybir
from concourse.bass_utils import run_bass_kernel_spmd

F32 = mybir.dt.float32
BF16 = mybir.dt.bfloat16
FP8 = mybir.dt.float8e4
I32 = mybir.dt.int32
AF = mybir.ActivationFunctionType
ALU = mybir.AluOpType
AXX = mybir.AxisListType.X

D = 2048
S = 2048
TOK = 1024
NCH = 16
HEADS = 8
LAMBDA_INIT = 0.8 - 0.6 * math.exp(-0.3 * 12)
EPS_LN = 1e-5
EPS_RMS = 1e-8
SCALE_A = 64.0 ** -0.5
SCALE_B = 128.0 ** -0.5
N_SUB = 4
TAPS = ()
PAIRS = [[0, 1], [2, 3], [4, 5], [6, 7]]

_ENGS = ("te", "dv", "ac", "gp", "sy")


class _B:

    def __init__(self, nc):
        self.nc = nc
        self.q = {e: [] for e in _ENGS}
        self.cnt = {}
        self.sems = {}
        self.stack = contextlib.ExitStack()
        self._lastwait = {e: {} for e in _ENGS}
        self.guard = {}

    def sem(self, name):
        self.sems[name] = self.stack.enter_context(self.nc.semaphore(name))
        self.cnt[name] = 0

    def wait(self, eng, sem, val=None):
        v = self.cnt[sem] if val is None else val
        if v <= 0:
            return
        if self._lastwait[eng].get(sem, -1) >= v:
            return
        self._lastwait[eng][sem] = v
        h = self.sems[sem]
        self.q[eng].append(lambda e, h=h, v=v: e.wait_ge(h, v))

    def op(self, eng, fn, inc=None, n=1):
        if inc is None:
            self.q[eng].append(fn)
        else:
            h = self.sems[inc]
            self.q[eng].append(lambda e, fn=fn, h=h, n=n: fn(e).then_inc(h, n))
            self.cnt[inc] += n

    def dma(self, eng, out, in_, inc):
        self.op(eng, lambda e, o=out, i=in_: e.dma_start(out=o, in_=i),
                inc=inc, n=16)

    def now(self, sem):
        return self.cnt[sem]

    def need(self, eng, key):
        g = self.guard.get(key)
        if g:
            self.wait(eng, g[0], g[1])

    def setg(self, key, sem):
        self.guard[key] = (sem, self.cnt[sem])

    def dvf(self, fn):
        self.op("dv", fn, inc="dv")
        self.wait("dv", "dv")


def build_nc(n_sub=N_SUB):
    nc = bass.Bass("TRN2", target_bir_lowering=False)
    b = _B(nc)
    st = b.stack
    sb = lambda name, shape, dt: st.enter_context(
        nc.sbuf_tensor(name, list(shape), dt))

    def din(name, shape, dt):
        return nc.dram_tensor(name, list(shape), dt, kind="ExternalInput")

    x_in = din("x_in", [D, TOK], F32)
    wq = din("wq", [D, TOK], F32)
    wk = din("wk", [D, TOK], F32)
    wv = din("wv", [D, TOK], F32)
    wo = din("wo", [TOK, D], F32)
    wbk = din("wbk", [D, TOK], F32)
    wbq = din("wbq", [D, TOK], F32)
    wbv = din("wbv", [D, TOK], F32)
    wbo = din("wbo", [TOK, D], F32)
    w8t = din("w8t", [D, D], FP8)
    cos_in = din("cos_in", [128, S], BF16)
    sin_in = din("sin_in", [128, S], BF16)
    rot_in = din("rot_in", [128, 128], BF16)
    tri_in = din("tri_in", [128, 128], BF16)
    lam_in = din("lam_in", [1, 256], F32)
    lnw_in = din("lnw_in", [128, 1], F32)
    lnb_in = din("lnb_in", [128, 1], F32)
    wsc_in = din("wsc_in", [1, 1], F32)
    isc_in = din("isc_in", [1, 1], F32)
    offs_in = din("offs_in", [1, 4], I32)

    out_fm = nc.dram_tensor("out_fm", [D, TOK], F32, kind="ExternalOutput")

    xbufA = nc.dram_tensor("xbufA", [D, TOK], F32)
    xbufB = nc.dram_tensor("xbufB", [D, TOK], F32)
    agi1 = {a: nc.dram_tensor(f"ag1{a}_in", [D, TOK], BF16) for a in "ab"}
    agi2 = {a: nc.dram_tensor(f"ag2{a}_in", [2 * D, TOK], BF16) for a in "ab"}
    ago2 = {a: nc.dram_tensor(f"ag2{a}_out", [16 * D, TOK], BF16,
                              addr_space="Shared") for a in "ab"}
    xnloc = {a: nc.dram_tensor(f"xnloc{a}", [2 * D, TOK], BF16,
                               addr_space="Shared") for a in "ab"}
    parloc = {a: nc.dram_tensor(f"parloc{a}", [2 * D, TOK], BF16) for a in "ab"}
    bti = nc.dram_tensor("bti", [1, 128], BF16)
    bto = nc.dram_tensor("bto", [2, 128], BF16)

    xn_full = sb("xn_full", [128, NCH * S], BF16)
    scr32 = sb("scr32", [128, NCH * TOK], BF16)
    ab16 = sb("ab16", [128, 8192], BF16)
    k4 = sb("k4", [128, 8192], BF16)
    v4 = sb("v4", [128, 8192], BF16)
    cs16 = sb("cs16", [128, 4096], BF16)
    wpool = sb("wpool", [128, 4096], BF16)
    w8p = sb("w8p", [128, 4096], FP8)
    xch = [sb(f"xch{i}", [128, TOK], F32) for i in (0, 1)]
    sq1 = sb("sq1", [128, TOK], BF16)
    ptile = [sb(f"ptile{i}", [128, 1024], BF16) for i in (0, 1)]
    qraw = sb("qraw", [128, 512], BF16)
    vtmp2 = sb("vtmp2", [128, 512], F32)
    f8t = sb("f8t", [128, 512], FP8)
    w1t = sb("w1t", [128, 512], F32)
    rows = sb("rows", [1, 2560], F32)
    rowsb = sb("rowsb", [1, 2048], BF16)
    maskt = sb("maskt", [128, 128], BF16)
    rotmb = sb("rotmb", [128, 128], BF16)
    ones_m = sb("ones_m", [128, 128], BF16)
    ones_r = sb("ones_r", [1, 128], BF16)
    lamv = sb("lamv", [1, 256], F32)
    lnw = sb("lnw", [128, 1], F32)
    lnb = sb("lnb", [128, 1], F32)
    lnw_eff = sb("lnw_eff", [128, 1], F32)
    lnb_eff = sb("lnb_eff", [128, 1], F32)
    sca = sb("sca", [1, 8], F32)
    onef = sb("onef", [1, 128], F32)
    alpha_c = sb("alpha_c", [128, 1], F32)
    lamn_c = sb("lamn_c", [128, 1], F32)
    offs_sb = sb("offs_sb", [1, 4], I32)

    mb_t = ab16[:, 0:1024]
    rb_t = ab16[:, 1024:2048]
    q4 = ab16
    par_sl = [ab16[:, 0:2048], ab16[:, 2048:4096]]
    cosb = cs16[:, 0:2048]
    sinb = cs16[:, 2048:4096]
    po_sl = [cs16[:, 0:2048], cs16[:, 2048:4096]]

    psS = [st.enter_context(nc.psum_tensor(f"psS{i}", [128, 1024], F32))
           for i in range(2)]
    ps5 = [st.enter_context(nc.psum_tensor(f"c{i}", [128, 512], F32))
           for i in range(4)]

    for s_name in ("sy", "gp", "te", "dv", "ac",
                   "cca1", "cca2", "ccb1", "ccb2"):
        b.sem(s_name)

    tap_t = {}

    def tap(point, buf, cols, waits=("dv", "ac", "te")):
        if point not in TAPS:
            return
        t = nc.dram_tensor(f"tap_{point}", [128, cols], BF16,
                           kind="ExternalOutput")
        tap_t[point] = t
        for w in waits:
            b.wait("sy", w, b.now(w))
        b.dma("sy", t.ap(), buf[:, 0:cols], "sy")

    r_xnb = st.enter_context(nc.gpsimd.register("r_xnb"))
    r_parb = st.enter_context(nc.gpsimd.register("r_parb"))
    r_tmp = st.enter_context(nc.gpsimd.register("r_tmp"))

    for src, dst in ((cos_in, cosb), (sin_in, sinb), (rot_in, rotmb[:, :]),
                     (tri_in, maskt[:, :]), (lam_in, lamv[:, :]),
                     (lnw_in, lnw[:, :]), (lnb_in, lnb[:, :]),
                     (offs_in, offs_sb[:, :])):
        b.dma("sy", dst, src.ap(), "sy")
    b.dma("sy", sca[0:1, 0:1], wsc_in.ap(), "sy")
    b.dma("sy", sca[0:1, 1:2], isc_in.ap(), "sy")

    b.wait("dv", "sy", b.now("sy"))
    b.op("dv", lambda e: e.memset(ones_m[:, :], 1.0))
    b.op("dv", lambda e: e.memset(sca[0:1, 6:7], EPS_LN))
    b.op("dv", lambda e: e.memset(sca[0:1, 7:8], EPS_RMS))
    b.op("dv", lambda e: e.memset(ones_r[:, :], 1.0))
    b.op("dv", lambda e: e.memset(onef[:, :], 1.0))
    b.op("dv", lambda e: e.tensor_mul(rows[0:1, 0:64], lamv[0:1, 0:64],
                                      lamv[0:1, 64:128]))
    b.dvf(lambda e: e.tensor_mul(rows[0:1, 64:128], lamv[0:1, 128:192],
                                 lamv[0:1, 192:256]))
    b.op("dv", lambda e: e.reduce_sum(rows[0:1, 128:129], rows[0:1, 0:64],
                                      axis=AXX))
    b.op("dv", lambda e: e.reduce_sum(rows[0:1, 129:130], rows[0:1, 64:128],
                                      axis=AXX), inc="dv")
    b.wait("ac", "dv", b.now("dv"))
    b.op("ac", lambda e: e.activation(rows[0:1, 130:132], rows[0:1, 128:130],
                                      AF.Exp), inc="ac")
    b.wait("dv", "ac", b.now("ac"))
    b.dvf(lambda e: e.tensor_sub(rows[0:1, 132:133], rows[0:1, 130:131],
                                 rows[0:1, 131:132]))
    b.dvf(lambda e: e.tensor_scalar_add(sca[0:1, 4:5], rows[0:1, 132:133],
                                        LAMBDA_INIT))
    b.dvf(lambda e: e.tensor_scalar_mul(sca[0:1, 5:6], sca[0:1, 4:5], -1.0))
    b.dvf(lambda e: e.reciprocal(sca[0:1, 2:3], sca[0:1, 1:2]))
    b.op("dv", lambda e: e.tensor_mul(sca[0:1, 3:4], sca[0:1, 0:1],
                                      sca[0:1, 1:2]), inc="dv")
    b.wait("ac", "dv", b.now("dv"))
    b.op("ac", lambda e: e.activation(lnw_eff[:, :], lnw[:, :], AF.Copy,
                                      scale=1.0 - LAMBDA_INIT))
    b.op("ac", lambda e: e.activation(lnb_eff[:, :], lnb[:, :], AF.Copy,
                                      scale=1.0 - LAMBDA_INIT), inc="ac")
    b.wait("te", "dv", b.now("dv"))
    b.op("te", lambda e: e.matmul(ps5[0][:, 0:1], onef[0:1, :], sca[0:1, 3:4],
                                  start=True, stop=True))
    b.op("te", lambda e: e.matmul(ps5[0][:, 1:2], onef[0:1, :], sca[0:1, 5:6],
                                  start=True, stop=True), inc="te")
    b.wait("ac", "te", b.now("te"))
    b.op("ac", lambda e: e.activation(alpha_c[:, :], ps5[0][:, 0:1], AF.Copy))
    b.op("ac", lambda e: e.activation(lamn_c[:, :], ps5[0][:, 1:2], AF.Copy),
         inc="ac")
    b.setg("c0", "ac")

    b.wait("gp", "sy", b.now("sy"))
    b.op("gp", lambda e: e.reg_load(r_xnb, offs_sb[0:1, 0:1]))
    b.op("gp", lambda e: e.reg_load(r_parb, offs_sb[0:1, 1:2]))
    b.wait("gp", "dv", b.now("dv"))
    b.dma("gp", bti.ap(), ones_r[0:1, :], "gp")

    def stats_chunk(i, src_bf):
        b.need("ac", "sq1")
        b.op("ac", lambda e, s=src_bf: e.activation(sq1[:, :], s, AF.Square),
             inc="ac")
        b.wait("te", "ac", b.now("ac"))
        for tc in range(2):
            sl = slice(tc * 512, (tc + 1) * 512)
            if i == 0:
                b.need("te", "psS0")
                b.need("te", "psS1")
            b.op("te", lambda e, s=src_bf, sl=sl, i=i:
                 e.matmul(psS[0][:, sl], ones_m[:, :], s[:, sl],
                          start=(i == 0), stop=(i == NCH - 1)))
            b.op("te", lambda e, sl=sl, i=i:
                 e.matmul(psS[1][:, sl], ones_m[:, :], sq1[:, sl],
                          start=(i == 0), stop=(i == NCH - 1)),
                 inc=("te" if (i == NCH - 1 and tc == 1) else None))
        b.setg("sq1", "te")

    def ln_finish(ffn):
        b.wait("ac", "te", b.now("te"))
        for tc in range(2):
            b.op("ac", lambda e, tc=tc: e.activation(
                rows[0:1, tc * 512:(tc + 1) * 512],
                psS[0][0:1, tc * 512:(tc + 1) * 512], AF.Copy, scale=1.0 / D))
            b.op("ac", lambda e, tc=tc: e.activation(
                rows[0:1, 1024 + tc * 512:1024 + (tc + 1) * 512],
                psS[1][0:1, tc * 512:(tc + 1) * 512], AF.Copy, scale=1.0 / D),
                inc="ac")
        b.setg("psS0", "ac")
        b.setg("psS1", "ac")
        b.wait("dv", "ac", b.now("ac"))
        for tc in range(2):
            m_h = rows[0:1, tc * 512:(tc + 1) * 512]
            e_h = rows[0:1, 1024 + tc * 512:1024 + (tc + 1) * 512]
            t_h = rows[0:1, 2048:2560]
            b.dvf(lambda e, m=m_h, t=t_h: e.tensor_mul(t, m, m))
            b.op("dv", lambda e, eh=e_h, t=t_h: e.tensor_sub(eh, eh, t),
                 inc="dv")
            b.wait("dv", "dv")
        b.wait("ac", "dv", b.now("dv"))
        var_r = rows[0:1, 1024:2048]
        b.op("ac", lambda e: e.activation(var_r, var_r, AF.Ln,
                                          bias=sca[0:1, 6:7]))
        b.op("ac", lambda e: e.activation(var_r, var_r, AF.Exp, scale=-0.5),
             inc="ac")
        b.wait("dv", "ac", b.now("ac"))
        if ffn:
            b.dvf(lambda e: e.tensor_scalar(var_r, var_r, sca[0:1, 2:3],
                                            None, ALU.mult))
        b.op("dv", lambda e: e.tensor_copy(rowsb[0:1, 0:1024],
                                           rows[0:1, 0:1024]))
        b.op("dv", lambda e: e.tensor_copy(rowsb[0:1, 1024:2048], var_r),
             inc="dv")
        b.wait("te", "dv", b.now("dv"))
        for tc in range(2):
            b.need("te", f"c{tc}")
            b.need("te", f"c{2 + tc}")
            b.op("te", lambda e, tc=tc: e.matmul(
                ps5[tc][:, :], ones_r[0:1, :],
                rowsb[0:1, tc * 512:(tc + 1) * 512], start=True, stop=True))
            b.op("te", lambda e, tc=tc: e.matmul(
                ps5[2 + tc][:, :], ones_r[0:1, :],
                rowsb[0:1, 1024 + tc * 512:1024 + (tc + 1) * 512],
                start=True, stop=True), inc="te")
        b.wait("ac", "te", b.now("te"))
        b.need("ac", "ab16")
        for tc in range(2):
            b.op("ac", lambda e, tc=tc: e.activation(
                mb_t[:, tc * 512:(tc + 1) * 512], ps5[tc][:, :], AF.Copy))
            b.op("ac", lambda e, tc=tc: e.activation(
                rb_t[:, tc * 512:(tc + 1) * 512], ps5[2 + tc][:, :], AF.Copy),
                inc="ac")
        for i in range(4):
            b.setg(f"c{i}", "ac")
        b.wait("dv", "ac", b.now("ac"))
        for i in range(NCH):
            for tc in range(2):
                s_ = scr32[:, i * TOK + tc * 512:i * TOK + (tc + 1) * 512]
                b.dvf(lambda e, s=s_, tc=tc:
                      e.tensor_sub(s, s, mb_t[:, tc * 512:(tc + 1) * 512]))
                b.op("dv", lambda e, s=s_, tc=tc:
                     e.tensor_mul(s, s, rb_t[:, tc * 512:(tc + 1) * 512]),
                     inc=("dv" if (i == NCH - 1 and tc == 1) else None))
        b.setg("ab16", "dv")

    def first_stats():
        for i in range(NCH):
            xc = xch[i % 2]
            b.need("sy", f"xch{i % 2}")
            b.dma("sy", xc[:, :], x_in.ap()[i * 128:(i + 1) * 128, :], "sy")
            b.wait("dv", "sy", b.now("sy"))
            dst = scr32[:, i * TOK:(i + 1) * TOK]
            b.op("dv", lambda e, s=xc, d=dst: e.tensor_copy(d, s[:, :]),
                 inc="dv")
            b.setg(f"xch{i % 2}", "dv")
            b.wait("ac", "dv", b.now("dv"))
            stats_chunk(i, dst)

    def xn_exchange(which):
        agi = agi1[which]
        cc = "cca1" if which == "a" else "ccb1"
        dstd = agi.ap().rearrange("(i p) c -> p i c", p=128)
        srcd = scr32.ap().rearrange("p (i c) -> p i c", c=TOK)
        b.wait("sy", "dv", b.now("dv"))
        b.dma("sy", dstd, srcd, "sy")
        b.wait("gp", "sy", b.now("sy"))
        xl = xnloc[which]
        b.op("gp", lambda e, xl=xl, agi=agi: e.dma_start(
            out=bass.AP(xl, r_xnb, [[1, D * TOK]]),
            in_=bass.AP(agi, 0, [[1, D * TOK]])), inc="gp", n=16)
        b.wait("gp", "gp", b.now("gp"))
        b.op("gp", lambda e: e.collective_compute(
            "AllGather", ALU.bypass, replica_groups=PAIRS,
            ins=[bti.ap().opt()], outs=[bto.ap().opt()]), inc=cc)
        b.wait("gp", cc, 1)
        b.wait("sy", cc, 1)
        b.need("sy", "xn_full")
        for r in (0, 1):
            dstap = bass.AP(xn_full, r * TOK,
                            [[NCH * S, 128], [S, NCH], [1, TOK]])
            srcap = bass.AP(xl, r * D * TOK,
                            [[TOK, 128], [128 * TOK, NCH], [1, TOK]])
            b.dma("sy", dstap, srcap, "sy")
        b.wait("te", "sy", b.now("sy"))

    def wblock(src_dram, j, slot, rows_n=NCH):
        dst = wpool.ap()[:, slot * 2048: slot * 2048 + rows_n * 128].rearrange(
            "p (i c) -> p i c", c=128)
        src = src_dram.ap()[:, j * 128:(j + 1) * 128].rearrange(
            "(i p) c -> p i c", p=128)
        b.need("gp", f"wslot{slot}")
        b.dma("gp", dst, src, "gp")

    def wtile(slot, i):
        return wpool[:, slot * 2048 + i * 128: slot * 2048 + (i + 1) * 128]

    def wvblock(src_dram, col0):
        dst = wpool.ap().rearrange("p (i c) -> p i c", c=256)
        src = src_dram.ap()[:, col0:col0 + 256].rearrange(
            "(i p) c -> p i c", p=128)
        b.need("gp", "wslot0")
        b.need("gp", "wslot1")
        b.dma("gp", dst, src, "gp")

    def wvtile(i):
        return wpool[:, i * 256:(i + 1) * 256]

    def attn_sub(which, xsrc, xdst, fuse_stats, last):
        is_a = which == "a"
        wlq, wlk, wrv, wout = (wq, wk, wv, wo) if is_a else (wbq, wbk, wbv, wbo)
        scale = SCALE_A if is_a else SCALE_B
        cc2 = "cca2" if is_a else "ccb2"

        xn_exchange(which)

        for g in range(2):
            blocklist = [(kind, jj) for kind in ("k", "q") for jj in range(4)]
            gp_after = {}

            def load_blk(idx):
                kind, jj = blocklist[idx]
                wsrc = {"k": wlk, "q": wlq}[kind]
                wblock(wsrc, 4 * g + jj, idx % 2)
                gp_after[idx] = b.now("gp")

            load_blk(0)
            for bi, (kind, jj) in enumerate(blocklist):
                dbuf = k4 if kind == "k" else q4
                slot = bi % 2
                if bi + 1 < len(blocklist):
                    load_blk(bi + 1)
                b.wait("te", "gp", gp_after[bi])
                for tc in range(4):
                    pj = ps5[tc % 2]
                    b.need("te", f"c{tc % 2}")
                    for i in range(NCH):
                        rh = xn_full[:, i * S + tc * 512:
                                     i * S + (tc + 1) * 512]
                        b.op("te", lambda e, p=pj, l=wtile(slot, i), r=rh,
                             i=i: e.matmul(p[:, :], l, r, start=(i == 0),
                                           stop=(i == NCH - 1)),
                             inc=("te" if i == NCH - 1 else None))
                    dsl = dbuf[:, jj * S + tc * 512:jj * S + (tc + 1) * 512]
                    if is_a:
                        b.wait("ac", "te", b.now("te"))
                        b.need("ac", "qraw")
                        b.op("ac", lambda e, p=pj: e.activation(
                            qraw[:, :], p[:, :], AF.Copy), inc="ac")
                        b.setg(f"c{tc % 2}", "ac")
                        b.wait("te", "ac", b.now("ac"))
                        prot = ps5[2 + tc % 2]
                        b.need("te", f"c{2 + tc % 2}")
                        b.op("te", lambda e, p=prot: e.matmul(
                            p[:, :], rotmb[:, :], qraw[:, :],
                            start=True, stop=True), inc="te")
                        csl = cosb[:, tc * 512:(tc + 1) * 512]
                        ssl = sinb[:, tc * 512:(tc + 1) * 512]
                        b.wait("dv", "te", b.now("te"))
                        b.op("dv", lambda e, d=dsl, c=csl:
                             e.tensor_mul(d, qraw[:, :], c))
                        b.dvf(lambda e, p=prot, s=ssl:
                              e.tensor_mul(w1t[:, :], p[:, :], s))
                        b.op("dv", lambda e, d=dsl:
                             e.tensor_add(d, d, w1t[:, :]), inc="dv")
                        b.setg("qraw", "dv")
                        b.setg(f"c{2 + tc % 2}", "dv")
                    else:
                        b.wait("ac", "te", b.now("te"))
                        b.op("ac", lambda e, p=pj, d=dsl: e.activation(
                            d, p[:, :], AF.Copy), inc="ac")
                        b.setg(f"c{tc % 2}", "ac")
                b.setg(f"wslot{slot}", "te")

            for vb in range(2):
                wvblock(wrv, g * 512 + vb * 256)
                b.wait("te", "gp", b.now("gp"))
                for kc in range(NCH):
                    pj = ps5[kc % 2]
                    b.need("te", f"c{kc % 2}")
                    for i in range(NCH):
                        lt = xn_full[:, i * S + kc * 128:i * S + (kc + 1) * 128]
                        b.op("te", lambda e, p=pj, l=lt, r=wvtile(i), i=i:
                             e.matmul(p[:, 0:256], l, r, start=(i == 0),
                                      stop=(i == NCH - 1)),
                             inc=("te" if i == NCH - 1 else None))
                    dsl = v4[:, kc * 512 + vb * 256:kc * 512 + (vb + 1) * 256]
                    b.wait("ac", "te", b.now("te"))
                    b.op("ac", lambda e, p=pj, d=dsl: e.activation(
                        d, p[:, 0:256], AF.Copy), inc="ac")
                    b.setg(f"c{kc % 2}", "ac")
                b.setg("wslot0", "te")
                b.setg("wslot1", "te")
            if g == 1:
                b.setg("xn_full", "te")

            b.wait("te", "ac", b.now("ac"))
            b.wait("te", "dv", b.now("dv"))
            for jj in range(4):
                head = 4 * g + jj
                for qc in range(4):
                    nkc = 4 * (qc + 1)
                    npair = nkc if is_a else nkc // 2
                    b.need("te", "c0")
                    b.need("te", "c1")
                    b.need("te", "c2")
                    b.need("te", "c3")

                    def emit_Spair(t):
                        sp = psS[t % 2]
                        b.need("te", f"psS{t % 2}")
                        rh = q4[:, jj * S + qc * 512:jj * S + (qc + 1) * 512]
                        if is_a:
                            lt = k4[:, jj * S + t * 128:jj * S + (t + 1) * 128]
                            b.op("te", lambda e, p=sp, l=lt, r=rh: e.matmul(
                                p[:, 0:512], l[0:64, :], r[0:64, :],
                                start=True, stop=True))
                            b.op("te", lambda e, p=sp, l=lt, r=rh: e.matmul(
                                p[:, 512:1024], l[64:128, :], r[64:128, :],
                                start=True, stop=True), inc="te")
                        else:
                            l0 = k4[:, jj * S + 2 * t * 128:
                                    jj * S + (2 * t + 1) * 128]
                            l1 = k4[:, jj * S + (2 * t + 1) * 128:
                                    jj * S + (2 * t + 2) * 128]
                            b.op("te", lambda e, p=sp, l=l0, r=rh: e.matmul(
                                p[:, 0:512], l, r, start=True, stop=True))
                            b.op("te", lambda e, p=sp, l=l1, r=rh: e.matmul(
                                p[:, 512:1024], l, r, start=True, stop=True),
                                inc="te")
                        return b.now("te")

                    s_done = {0: emit_Spair(0)}
                    for t in range(npair):
                        pt = ptile[t % 2]
                        sp = psS[t % 2]
                        if t + 1 < npair:
                            s_done[t + 1] = emit_Spair(t + 1)
                        b.wait("ac", "te", s_done[t])
                        b.need("ac", f"pt{t % 2}")
                        b.op("ac", lambda e, p=sp, t_=pt: e.activation(
                            t_[:, 0:512], p[:, 0:512], AF.Exp, scale=scale))
                        b.op("ac", lambda e, p=sp, t_=pt: e.activation(
                            t_[:, 512:1024], p[:, 512:1024], AF.Exp,
                            scale=scale), inc="ac")
                        b.setg(f"psS{t % 2}", "ac")
                        if is_a:
                            mask_jobs = [(t, (0, 512))]
                        else:
                            mask_jobs = [(2 * t, (0,)), (2 * t + 1, (512,))]
                        did_mask = False
                        for kc, tgt in mask_jobs:
                            m = kc - 4 * qc
                            if m < 0:
                                continue
                            if not did_mask:
                                b.wait("dv", "ac", b.now("ac"))
                            did_mask = True
                            for o2 in tgt:
                                if m > 0:
                                    b.op("dv", lambda e, t_=pt, m=m, o=o2:
                                         e.memset(t_[:, o:o + m * 128], 0.0))
                                b.op("dv", lambda e, t_=pt, m=m, o=o2:
                                     e.tensor_mul(
                                         t_[:, o + m * 128:o + (m + 1) * 128],
                                         t_[:, o + m * 128:o + (m + 1) * 128],
                                         maskt[:, :]), inc="dv")
                        if did_mask:
                            b.wait("te", "dv", b.now("dv"))
                        else:
                            b.wait("te", "ac", b.now("ac"))
                        if is_a:
                            vl = v4[:, t * 512 + jj * 128:
                                    t * 512 + (jj + 1) * 128]
                            b.op("te", lambda e, v=vl, t_=pt, t0=t: e.matmul(
                                ps5[0][:, :], v, t_[:, 0:512],
                                start=(t0 == 0), stop=(t0 == npair - 1)))
                            b.op("te", lambda e, v=vl, t_=pt, t0=t: e.matmul(
                                ps5[1][:, :], v, t_[:, 512:1024],
                                start=(t0 == 0), stop=(t0 == npair - 1)))
                            b.op("te", lambda e, t_=pt, t0=t: e.matmul(
                                ps5[2][:, :], ones_m[:, :], t_[:, 0:512],
                                start=(t0 == 0), stop=(t0 == npair - 1)))
                            b.op("te", lambda e, t_=pt, t0=t: e.matmul(
                                ps5[3][:, :], ones_m[:, :], t_[:, 512:1024],
                                start=(t0 == 0), stop=(t0 == npair - 1)),
                                inc="te")
                        else:
                            for hi, kc in enumerate((2 * t, 2 * t + 1)):
                                vl = v4[:, kc * 512 + jj * 128:
                                        kc * 512 + (jj + 1) * 128]
                                o = hi * 512
                                fst = kc == 0
                                lst = kc == nkc - 1
                                b.op("te", lambda e, v=vl, t_=pt, o=o, f=fst,
                                     l_=lst: e.matmul(
                                         ps5[0][:, :], v, t_[:, o:o + 512],
                                         start=f, stop=l_))
                                b.op("te", lambda e, t_=pt, o=o, f=fst, l_=lst:
                                     e.matmul(
                                         ps5[2][:, :], ones_m[:, :],
                                         t_[:, o:o + 512], start=f, stop=l_),
                                     inc="te")
                        b.setg(f"pt{t % 2}", "te")

                    b.wait("dv", "te", b.now("te"))
                    zsl = scr32[:, head * S + qc * 512:
                                head * S + (qc + 1) * 512]
                    if is_a:
                        b.dvf(lambda e: e.reciprocal(vtmp2[:, :],
                                                     ps5[2][:, :]))
                        b.dvf(lambda e: e.tensor_mul(w1t[:, :], ps5[3][:, :],
                                                     vtmp2[:, :]))
                        b.dvf(lambda e: e.tensor_mul(vtmp2[:, :], ps5[0][:, :],
                                                     w1t[:, :]))
                        b.op("dv", lambda e, z=zsl: e.scalar_tensor_tensor(
                            z, ps5[1][:, :], lamn_c[:, 0:1], vtmp2[:, :],
                            ALU.mult, ALU.add), inc="dv")
                    else:
                        b.dvf(lambda e: e.reciprocal(vtmp2[:, :],
                                                     ps5[2][:, :]))
                        b.op("dv", lambda e, z=zsl: e.tensor_mul(
                            z, ps5[0][:, :], vtmp2[:, :]), inc="dv")
                    for i in range(4):
                        b.setg(f"c{i}", "dv")

                if is_a:
                    b.need("te", "psS0")
                    b.need("te", "psS1")
                    for seg in range(2):
                        hsl = scr32[:, head * S + seg * TOK:
                                    head * S + (seg + 1) * TOK]
                        b.need("ac", "sq1")
                        b.wait("ac", "dv", b.now("dv"))
                        b.op("ac", lambda e, h=hsl: e.activation(
                            sq1[:, :], h, AF.Square), inc="ac")
                        b.wait("te", "ac", b.now("ac"))
                        for sc in range(2):
                            b.op("te", lambda e, sc=sc, seg=seg: e.matmul(
                                psS[seg][:, sc * 512:(sc + 1) * 512],
                                ones_m[:, :], sq1[:, sc * 512:(sc + 1) * 512],
                                start=True, stop=True),
                                 inc=("te" if sc == 1 else None))
                        b.setg("sq1", "te")
                    b.wait("ac", "te", b.now("te"))
                    for sc4 in range(4):
                        b.op("ac", lambda e, sc4=sc4: e.activation(
                            rows[0:1, sc4 * 512:(sc4 + 1) * 512],
                            psS[sc4 // 2][0:1, (sc4 % 2) * 512:
                                          (sc4 % 2) * 512 + 512],
                            AF.Copy, scale=1.0 / 128), inc="ac")
                    b.setg("psS0", "ac")
                    b.setg("psS1", "ac")
                    b.op("ac", lambda e: e.activation(
                        rows[0:1, 0:2048], rows[0:1, 0:2048], AF.Ln,
                        bias=sca[0:1, 7:8]))
                    b.op("ac", lambda e: e.activation(
                        rows[0:1, 0:2048], rows[0:1, 0:2048], AF.Exp,
                        scale=-0.5), inc="ac")
                    b.wait("dv", "ac", b.now("ac"))
                    b.op("dv", lambda e: e.tensor_copy(rowsb[0:1, 0:2048],
                                                       rows[0:1, 0:2048]),
                         inc="dv")
                    b.wait("te", "dv", b.now("dv"))
                    for sc4 in range(4):
                        pb = ps5[sc4 % 2]
                        b.need("te", f"c{sc4 % 2}")
                        b.op("te", lambda e, p=pb, sc4=sc4: e.matmul(
                            p[:, :], ones_r[0:1, :],
                            rowsb[0:1, sc4 * 512:(sc4 + 1) * 512],
                            start=True, stop=True), inc="te")
                        b.wait("dv", "te", b.now("te"))
                        hsl = scr32[:, head * S + sc4 * 512:
                                    head * S + (sc4 + 1) * 512]
                        b.op("dv", lambda e, h=hsl, p=pb:
                             e.tensor_mul(h, h, p[:, :]), inc="dv")
                        b.wait("dv", "dv")
                        b.setg(f"c{sc4 % 2}", "dv")
                    for q2 in range(4):
                        hsl = scr32[:, head * S + q2 * 512:
                                    head * S + (q2 + 1) * 512]
                        b.op("dv", lambda e, h=hsl: e.tensor_scalar(
                            h, h, lnw_eff[:, 0:1], lnb_eff[:, 0:1],
                            ALU.mult, ALU.add), inc="dv")

        tap(f"zbuf{which}", scr32, NCH * TOK)
        agi, ago = agi2[which], ago2[which]
        b.wait("te", "ac", b.now("ac"))
        b.wait("te", "dv", b.now("dv"))
        b.wait("ac", "te", b.now("te"))
        gp_after_wo = {}

        def load_wo(oc):
            wblock(wout, oc, oc % 2, rows_n=8)
            gp_after_wo[oc] = b.now("gp")

        load_wo(0)
        for oc in range(NCH):
            slot = oc % 2
            if oc + 1 < NCH:
                load_wo(oc + 1)
            b.wait("te", "gp", gp_after_wo[oc])
            psl = par_sl[oc % 2]
            b.need("ac", f"par{oc % 2}")
            for tc in range(4):
                pj = ps5[tc % 2]
                b.need("te", f"c{tc % 2}")
                for i in range(8):
                    rh = scr32[:, i * S + tc * 512:i * S + (tc + 1) * 512]
                    b.op("te", lambda e, p=pj, l=wtile(slot, i), r=rh, i=i:
                         e.matmul(p[:, :], l, r, start=(i == 0),
                                  stop=(i == 7)),
                         inc=("te" if i == 7 else None))
                b.wait("ac", "te", b.now("te"))
                b.op("ac", lambda e, p=pj, d=psl[:, tc * 512:(tc + 1) * 512]:
                     e.activation(d, p[:, :], AF.Copy), inc="ac")
                b.setg(f"c{tc % 2}", "ac")
            b.setg(f"wslot{slot}", "te")
            b.wait("sy", "ac", b.now("ac"))
            for hh in (0, 1):
                b.dma("sy",
                      agi.ap()[hh * D + oc * 128:hh * D + (oc + 1) * 128, :],
                      psl[:, hh * TOK:(hh + 1) * TOK], "sy")
            b.setg(f"par{oc % 2}", "sy")
        b.wait("gp", "sy", b.now("sy"))
        if "par" in TAPS and which == "a":
            tpa = nc.dram_tensor("tap_agi2", [2 * D, TOK], BF16,
                                 kind="ExternalOutput")
            tap_t["agi2"] = tpa
            b.dma("gp", tpa.ap(), agi.ap(), "gp")
        b.op("gp", lambda e, i=agi, o=ago: e.collective_compute(
            "AllGather", ALU.bypass, replica_groups=[list(range(8))],
            ins=[i.ap().opt()], outs=[o.ap().opt()]), inc=cc2)
        b.wait("gp", cc2, 1)
        stream_update(which, xsrc, xdst, fuse_stats, last, ago)
        if "par" in TAPS and which == "a":
            tpp = nc.dram_tensor("tap_parloc", [2 * D, TOK], BF16,
                                 kind="ExternalOutput")
            tap_t["parloc"] = tpp
            b.dma("gp", tpp.ap(), parloc[which].ap(), "gp")

    def stream_update(which, xsrc, xdst, fuse_stats, last, ago):
        pl = parloc[which]
        for r in (0, 1):
            dstrot = bass.AP(pl, r * D * TOK, [[1, D * TOK]])
            if r == 0:
                srcrot = bass.AP(ago, r_parb, [[1, D * TOK]])
            else:
                b.op("gp", lambda e: e.reg_mov(r_tmp, 2 * D * TOK))
                b.op("gp", lambda e: e.reg_add(r_tmp, r_tmp, r_parb))
                srcrot = bass.AP(ago, r_tmp, [[1, D * TOK]])
            b.op("gp", lambda e, d=dstrot, s=srcrot:
                 e.dma_start(out=d, in_=s), inc="gp", n=16)
        b.wait("gp", "gp", b.now("gp"))
        gp_po = {}
        sy_xc = {}

        def load_chunk(i):
            pob = po_sl[i % 2]
            b.need("gp", f"po{i % 2}")
            srcap = bass.AP(pl, i * 128 * TOK,
                            [[TOK, 128], [D * TOK, 2], [1, TOK]])
            dstap = pob.rearrange("p (r c) -> p r c", c=TOK)
            b.op("gp", lambda e, d=dstap, s=srcap:
                 e.dma_start(out=d, in_=s), inc="gp", n=16)
            gp_po[i] = b.now("gp")
            xc = xch[i % 2]
            b.need("sy", f"xch{i % 2}")
            b.dma("sy", xc[:, :], xsrc.ap()[i * 128:(i + 1) * 128, :], "sy")
            sy_xc[i] = b.now("sy")

        for i in range(NCH):
            load_chunk(i)
            pob = po_sl[i % 2]
            xc = xch[i % 2]
            b.wait("dv", "gp", gp_po[i])
            b.wait("dv", "sy", sy_xc[i])
            for tc in range(2):
                sl = slice(tc * 512, (tc + 1) * 512)
                sl2 = slice(1024 + tc * 512, 1024 + (tc + 1) * 512)
                b.dvf(lambda e, p=pob, sl=sl, sl2=sl2:
                      e.tensor_add(w1t[:, :], p[:, sl], p[:, sl2]))
                b.op("dv", lambda e, x=xc, sl=sl:
                     e.tensor_add(x[:, sl], x[:, sl], w1t[:, :]), inc="dv")
                b.wait("dv", "dv")
            b.setg(f"po{i % 2}", "dv")
            if fuse_stats:
                dstn = scr32[:, i * TOK:(i + 1) * TOK]
                b.op("dv", lambda e, s=xc, d=dstn: e.tensor_copy(d, s[:, :]),
                     inc="dv")
                b.wait("ac", "dv", b.now("dv"))
                stats_chunk(i, dstn)
            b.wait("sy", "dv", b.now("dv"))
            b.dma("sy", xdst.ap()[i * 128:(i + 1) * 128, :], xc[:, :], "sy")
            b.setg(f"xch{i % 2}", "sy")

    def ffn_sub(xsrc, xdst, fuse_stats, last):
        b.wait("dv", "ac", b.now("ac"))
        for i in range(NCH):
            for tc in range(2):
                s_ = scr32[:, i * TOK + tc * 512:i * TOK + (tc + 1) * 512]
                b.dvf(lambda e, s=s_: e.tensor_scalar(
                    f8t[:, :], s, 240.0, -240.0, ALU.min, ALU.max))
                b.op("dv", lambda e, s=s_: e.tensor_copy(s, f8t[:, :]),
                     inc="dv")
                b.wait("dv", "dv")
        sy_after_w8 = {}

        def load_w8(oc):
            slot_ = oc % 2
            dst8 = w8p.ap()[:, slot_ * 2048:(slot_ + 1) * 2048].rearrange(
                "p (i c) -> p i c", c=128)
            src8 = w8t.ap()[:, oc * 128:(oc + 1) * 128].rearrange(
                "(i p) c -> p i c", p=128)
            b.need("sy", f"w8s{slot_}")
            b.dma("sy", dst8, src8, "sy")
            sy_after_w8[oc] = b.now("sy")

        load_w8(0)
        for oc in range(NCH):
            slot = oc % 2
            if oc + 1 < NCH:
                load_w8(oc + 1)
            b.wait("dv", "sy", sy_after_w8[oc])
            b.need("dv", f"wslot{slot}")
            b.op("dv", lambda e, s=slot: e.tensor_copy(
                wpool[:, s * 2048:(s + 1) * 2048],
                w8p[:, s * 2048:(s + 1) * 2048]), inc="dv")
            b.setg(f"w8s{slot}", "dv")
            b.wait("te", "dv", b.now("dv"))
            xc = xch[oc % 2]
            b.need("sy", f"xch{oc % 2}")
            b.dma("sy", xc[:, :], xsrc.ap()[oc * 128:(oc + 1) * 128, :], "sy")
            for tc in range(2):
                pj = ps5[tc % 2]
                b.need("te", f"c{tc % 2}")
                for i in range(NCH):
                    rh = scr32[:, i * TOK + tc * 512:i * TOK + (tc + 1) * 512]
                    b.op("te", lambda e, p=pj, l=wtile(slot, i), r=rh, i=i:
                         e.matmul(p[:, :], l, r, start=(i == 0),
                                  stop=(i == NCH - 1)),
                         inc=("te" if i == NCH - 1 else None))
            b.setg(f"wslot{slot}", "te")
            b.wait("dv", "te", b.now("te"))
            b.wait("dv", "sy", b.now("sy"))
            for tc in range(2):
                pj = ps5[tc % 2]
                sl = slice(tc * 512, (tc + 1) * 512)
                b.op("dv", lambda e, p=pj, x=xc, sl=sl:
                     e.scalar_tensor_tensor(x[:, sl], p[:, :],
                                            alpha_c[:, 0:1], x[:, sl],
                                            ALU.mult, ALU.add), inc="dv")
                b.setg(f"c{tc % 2}", "dv")
            b.wait("sy", "dv", b.now("dv"))
            b.dma("sy", xdst.ap()[oc * 128:(oc + 1) * 128, :], xc[:, :], "sy")
            b.setg(f"xch{oc % 2}", "sy")
        if fuse_stats:
            b.wait("sy", "sy", b.now("sy"))
            for i in range(NCH):
                xc = xch[i % 2]
                b.need("sy", f"xch{i % 2}")
                b.dma("sy", xc[:, :], xdst.ap()[i * 128:(i + 1) * 128, :],
                      "sy")
                b.wait("dv", "sy", b.now("sy"))
                dstn = scr32[:, i * TOK:(i + 1) * TOK]
                b.op("dv", lambda e, s=xc, d=dstn: e.tensor_copy(d, s[:, :]),
                     inc="dv")
                b.setg(f"xch{i % 2}", "dv")
                b.wait("ac", "dv", b.now("dv"))
                stats_chunk(i, dstn)

    first_stats()
    ln_finish(ffn=False)
    tap("xnw1", scr32, NCH * TOK)

    attn_sub("a", x_in, xbufA, fuse_stats=(n_sub >= 2), last=(n_sub == 1))
    if n_sub >= 2:
        ln_finish(ffn=True)
        ffn_sub(xbufA, xbufB, fuse_stats=(n_sub >= 3), last=(n_sub == 2))
    if n_sub >= 3:
        ln_finish(ffn=False)
        attn_sub("b", xbufB, xbufA, fuse_stats=(n_sub >= 4), last=(n_sub == 3))
    if n_sub >= 4:
        ln_finish(ffn=True)
        ffn_sub(xbufA, out_fm, fuse_stats=False, last=True)

    fin = {1: xbufA, 2: xbufB, 3: xbufA}.get(n_sub)
    if fin is not None and n_sub < 4:
        b.wait("gp", "sy", b.now("sy"))
        b.dma("gp", out_fm.ap(), fin.ap(), "gp")

    b.wait("sy", "sy", b.now("sy"))
    b.wait("sy", "gp", b.now("gp"))

    with nc.Block() as block:
        for eng_name, attr in (("te", "tensor"), ("dv", "vector"),
                               ("ac", "scalar"), ("gp", "gpsimd"),
                               ("sy", "sync")):
            ops = b.q[eng_name]

            def runner(e, ops=ops):
                for f in ops:
                    f(e)
            getattr(block, attr)(runner)

    b.stack.close()
    return nc


_NC_CACHE = {}


def _get_nc():
    key = N_SUB
    if key not in _NC_CACHE:
        _NC_CACHE[key] = build_nc(key)
    return _NC_CACHE[key]


def _rope_tables():
    dd = 64
    inv = (1.0 / np.float32(10000.0) **
           (np.arange(0, dd, 2, dtype=np.float32) / np.float32(dd)))
    t = np.arange(S, dtype=np.float32)
    freqs = (t[:, None] * inv[None, :]).astype(np.float32)
    freqs = np.repeat(freqs, 2, axis=1)
    cos = np.cos(freqs).astype(np.float32)
    sin = np.sin(freqs).astype(np.float32)
    cosf = np.concatenate([cos.T, cos.T], axis=0)
    sinf = np.concatenate([sin.T, sin.T], axis=0)
    return (np.ascontiguousarray(cosf).astype(ml_dtypes.bfloat16),
            np.ascontiguousarray(sinf).astype(ml_dtypes.bfloat16))


def _host_prep(inputs):
    x = np.asarray(inputs["L0_attn_a_x"])
    Wq = np.asarray(inputs["L0_attn_a_Wq"])
    Wk = np.asarray(inputs["L0_attn_a_Wk"])
    Wv = np.asarray(inputs["L0_attn_a_Wv"])
    Wo = np.asarray(inputs["L0_attn_a_Wo"])
    WK = np.asarray(inputs["L0_attn_b_W_K"])
    WQ = np.asarray(inputs["L0_attn_b_W_Q"])
    WV = np.asarray(inputs["L0_attn_b_W_V"])
    WO = np.asarray(inputs["L0_attn_b_W_O"])
    w8 = np.asarray(inputs["L0_ffn_weight"])
    wsc = np.float32(inputs["L0_ffn_weight_scale"])
    isc = np.float32(inputs["L0_ffn_input_scale"])
    lam = np.concatenate([np.asarray(inputs["L0_attn_a_lambda_q1"]),
                          np.asarray(inputs["L0_attn_a_lambda_k1"]),
                          np.asarray(inputs["L0_attn_a_lambda_q2"]),
                          np.asarray(inputs["L0_attn_a_lambda_k2"])]
                         ).astype(np.float32).reshape(1, 256)
    lnw = np.asarray(inputs["L0_attn_a_ln_w"]).astype(np.float32).reshape(128, 1)
    lnb = np.asarray(inputs["L0_attn_a_ln_b"]).astype(np.float32).reshape(128, 1)

    cosb, sinb = _rope_tables()
    rot = np.zeros((128, 128), dtype=np.float32)
    for p in range(0, 128, 2):
        rot[p + 1, p] = -1.0
        rot[p, p + 1] = 1.0
    rot = np.ascontiguousarray(rot).astype(ml_dtypes.bfloat16)
    tri = np.ascontiguousarray(
        np.triu(np.ones((128, 128), dtype=np.float32))).astype(
        ml_dtypes.bfloat16)
    w8t = np.ascontiguousarray(w8.T).view(ml_dtypes.float8_e4m3)

    in_maps = []
    for c in range(8):
        bb, h = c >> 1, c & 1
        tok = slice(1024 * h, 1024 * h + 1024)
        qk = slice(1024 * h, 1024 * h + 1024)
        heads = slice(8 * h, 8 * h + 8)
        x_fm = np.ascontiguousarray(x[bb, tok, :].T)
        wbk_c = np.ascontiguousarray(
            WK[heads].transpose(2, 0, 1).reshape(D, 1024))
        wbq_c = np.ascontiguousarray(
            WQ[heads].transpose(2, 0, 1).reshape(D, 1024))
        wbv_c = np.ascontiguousarray(
            WV[heads].transpose(2, 0, 1).reshape(D, 1024))
        wbo_c = np.ascontiguousarray(WO[:, qk].T)
        offs = np.array([[h * D * TOK,
                          (c & ~1) * 2 * D * TOK + h * D * TOK, 0, 0]],
                        dtype=np.int32)
        in_maps.append({
            "x_in": x_fm,
            "wq": np.ascontiguousarray(Wq[:, qk]),
            "wk": np.ascontiguousarray(Wk[:, qk]),
            "wv": np.ascontiguousarray(Wv[:, qk]),
            "wo": np.ascontiguousarray(Wo[qk, :]),
            "wbk": wbk_c, "wbq": wbq_c, "wbv": wbv_c, "wbo": wbo_c,
            "w8t": w8t,
            "cos_in": cosb, "sin_in": sinb, "rot_in": rot,
            "tri_in": tri,
            "lam_in": lam, "lnw_in": lnw, "lnb_in": lnb,
            "wsc_in": wsc.reshape(1, 1),
            "isc_in": isc.reshape(1, 1),
            "offs_in": offs,
        })
    return in_maps


def kernel(**inputs):
    in_maps = _host_prep(inputs)
    nc = _get_nc()
    res = run_bass_kernel_spmd(nc, in_maps, core_ids=list(range(8)))
    out = np.empty((4, S, D), dtype=np.float32)
    for c in range(8):
        bb, h = c >> 1, c & 1
        out[bb, 1024 * h:1024 * h + 1024, :] = res.results[c]["out_fm"].T
    if TAPS:
        kernel.tap_results = [
            {k: v for k, v in r.items() if k.startswith("tap_")}
            for r in res.results]
    return out


# version 11 (final)
# speedup vs baseline: 1.8492x; 1.1602x over previous
import contextlib
import math

import numpy as np
import ml_dtypes

import concourse.bass as bass
import concourse.mybir as mybir
from concourse.bass_utils import run_bass_kernel_spmd

F32 = mybir.dt.float32
BF16 = mybir.dt.bfloat16
FP8 = mybir.dt.float8e4
I32 = mybir.dt.int32
AF = mybir.ActivationFunctionType
ALU = mybir.AluOpType
AXX = mybir.AxisListType.X

D = 2048
S = 2048
TOK = 1024
NCH = 16
HEADS = 8
LAMBDA_INIT = 0.8 - 0.6 * math.exp(-0.3 * 12)
EPS_LN = 1e-5
EPS_RMS = 1e-8
SCALE_A = 64.0 ** -0.5
SCALE_B = 128.0 ** -0.5
N_SUB = 4
TAPS = ()
PAIRS = [[0, 1], [2, 3], [4, 5], [6, 7]]

_ENGS = ("te", "dv", "ac", "gp", "sy")


class _B:

    def __init__(self, nc):
        self.nc = nc
        self.q = {e: [] for e in _ENGS}
        self.cnt = {}
        self.sems = {}
        self.stack = contextlib.ExitStack()
        self._lastwait = {e: {} for e in _ENGS}
        self.guard = {}

    def sem(self, name):
        self.sems[name] = self.stack.enter_context(self.nc.semaphore(name))
        self.cnt[name] = 0

    def wait(self, eng, sem, val=None):
        v = self.cnt[sem] if val is None else val
        if v <= 0:
            return
        if self._lastwait[eng].get(sem, -1) >= v:
            return
        self._lastwait[eng][sem] = v
        h = self.sems[sem]
        self.q[eng].append(lambda e, h=h, v=v: e.wait_ge(h, v))

    def op(self, eng, fn, inc=None, n=1):
        if inc is None:
            self.q[eng].append(fn)
        else:
            h = self.sems[inc]
            self.q[eng].append(lambda e, fn=fn, h=h, n=n: fn(e).then_inc(h, n))
            self.cnt[inc] += n

    def dma(self, eng, out, in_, inc):
        self.op(eng, lambda e, o=out, i=in_: e.dma_start(out=o, in_=i),
                inc=inc, n=16)

    def now(self, sem):
        return self.cnt[sem]

    def need(self, eng, key):
        g = self.guard.get(key)
        if g:
            self.wait(eng, g[0], g[1])

    def setg(self, key, sem):
        self.guard[key] = (sem, self.cnt[sem])

    def dvf(self, fn):
        self.op("dv", fn, inc="dv")
        self.wait("dv", "dv")


def build_nc(n_sub=N_SUB):
    nc = bass.Bass("TRN2", target_bir_lowering=False)
    b = _B(nc)
    st = b.stack
    sb = lambda name, shape, dt: st.enter_context(
        nc.sbuf_tensor(name, list(shape), dt))

    def din(name, shape, dt):
        return nc.dram_tensor(name, list(shape), dt, kind="ExternalInput")

    x_in = din("x_in", [D, TOK], F32)
    wq = din("wq", [D, TOK], F32)
    wk = din("wk", [D, TOK], F32)
    wv = din("wv", [D, TOK], F32)
    wo = din("wo", [TOK, D], F32)
    wbk = din("wbk", [D, TOK], F32)
    wbq = din("wbq", [D, TOK], F32)
    wbv = din("wbv", [D, TOK], F32)
    wbo = din("wbo", [TOK, D], F32)
    w8t = din("w8t", [D, D], FP8)
    cos_in = din("cos_in", [128, S], BF16)
    sin_in = din("sin_in", [128, S], BF16)
    rot_in = din("rot_in", [128, 128], BF16)
    tri_in = din("tri_in", [128, 128], BF16)
    lam_in = din("lam_in", [1, 256], F32)
    lnw_in = din("lnw_in", [128, 1], F32)
    lnb_in = din("lnb_in", [128, 1], F32)
    wsc_in = din("wsc_in", [1, 1], F32)
    isc_in = din("isc_in", [1, 1], F32)
    offs_in = din("offs_in", [1, 4], I32)

    out_fm = nc.dram_tensor("out_fm", [D, TOK], F32, kind="ExternalOutput")

    xbufA = nc.dram_tensor("xbufA", [D, TOK], F32)
    xbufB = nc.dram_tensor("xbufB", [D, TOK], F32)
    agi1 = {a: nc.dram_tensor(f"ag1{a}_in", [D, TOK], BF16) for a in "ab"}
    agi2 = {a: nc.dram_tensor(f"ag2{a}_in", [2 * D, TOK], BF16) for a in "ab"}
    shpar = {a: nc.dram_tensor(f"shpar{a}", [4 * D, TOK], BF16,
                               addr_space="Shared") for a in "ab"}
    xnloc = {a: nc.dram_tensor(f"xnloc{a}", [2 * D, TOK], BF16,
                               addr_space="Shared") for a in "ab"}
    parloc = {a: nc.dram_tensor(f"parloc{a}", [2 * D, TOK], BF16) for a in "ab"}
    bti = nc.dram_tensor("bti", [1, 128], BF16)
    bto = nc.dram_tensor("bto", [2, 128], BF16)

    xn_full = sb("xn_full", [128, NCH * S], BF16)
    scr32 = sb("scr32", [128, NCH * TOK], BF16)
    ab16 = sb("ab16", [128, 8192], BF16)
    k4 = sb("k4", [128, 8192], BF16)
    v4 = sb("v4", [128, 8192], BF16)
    cs16 = sb("cs16", [128, 4096], BF16)
    wpool = sb("wpool", [128, 4096], BF16)
    w8p = sb("w8p", [128, 4096], FP8)
    xch = [sb(f"xch{i}", [128, TOK], F32) for i in (0, 1)]
    sq1 = sb("sq1", [128, TOK], BF16)
    ptile = [sb(f"ptile{i}", [128, 1024], BF16) for i in (0, 1)]
    qraw = sb("qraw", [128, 512], BF16)
    vtmp2 = sb("vtmp2", [128, 512], F32)
    f8t = sb("f8t", [128, 512], FP8)
    w1t = sb("w1t", [128, 512], F32)
    rows = sb("rows", [1, 2560], F32)
    rowsb = sb("rowsb", [1, 2048], BF16)
    maskt = sb("maskt", [128, 128], BF16)
    rotmb = sb("rotmb", [128, 128], BF16)
    ones_m = sb("ones_m", [128, 128], BF16)
    ones_r = sb("ones_r", [1, 128], BF16)
    lamv = sb("lamv", [1, 256], F32)
    lnw = sb("lnw", [128, 1], F32)
    lnb = sb("lnb", [128, 1], F32)
    lnw_eff = sb("lnw_eff", [128, 1], F32)
    lnb_eff = sb("lnb_eff", [128, 1], F32)
    sca = sb("sca", [1, 8], F32)
    onef = sb("onef", [1, 128], F32)
    alpha_c = sb("alpha_c", [128, 1], F32)
    lamn_c = sb("lamn_c", [128, 1], F32)
    offs_sb = sb("offs_sb", [1, 4], I32)

    mb_t = ab16[:, 0:1024]
    rb_t = ab16[:, 1024:2048]
    q4 = ab16
    par_sl = [ab16[:, 0:2048], ab16[:, 2048:4096]]
    cosb = cs16[:, 0:2048]
    sinb = cs16[:, 2048:4096]
    po_sl = [cs16[:, 0:2048], cs16[:, 2048:4096]]

    psS = [st.enter_context(nc.psum_tensor(f"psS{i}", [128, 1024], F32))
           for i in range(2)]
    ps5 = [st.enter_context(nc.psum_tensor(f"c{i}", [128, 512], F32))
           for i in range(4)]

    for s_name in ("sy", "gp", "te", "dv", "ac",
                   "cca1", "cca2", "ccb1", "ccb2"):
        b.sem(s_name)

    tap_t = {}

    def tap(point, buf, cols, waits=("dv", "ac", "te")):
        if point not in TAPS:
            return
        t = nc.dram_tensor(f"tap_{point}", [128, cols], BF16,
                           kind="ExternalOutput")
        tap_t[point] = t
        for w in waits:
            b.wait("sy", w, b.now(w))
        b.dma("sy", t.ap(), buf[:, 0:cols], "sy")

    r_xnb = st.enter_context(nc.gpsimd.register("r_xnb"))
    r_parb = st.enter_context(nc.gpsimd.register("r_parb"))
    r_tmp = st.enter_context(nc.gpsimd.register("r_tmp"))

    for src, dst in ((cos_in, cosb), (sin_in, sinb), (rot_in, rotmb[:, :]),
                     (tri_in, maskt[:, :]), (lam_in, lamv[:, :]),
                     (lnw_in, lnw[:, :]), (lnb_in, lnb[:, :]),
                     (offs_in, offs_sb[:, :])):
        b.dma("sy", dst, src.ap(), "sy")
    b.dma("sy", sca[0:1, 0:1], wsc_in.ap(), "sy")
    b.dma("sy", sca[0:1, 1:2], isc_in.ap(), "sy")

    b.wait("dv", "sy", b.now("sy"))
    b.op("dv", lambda e: e.memset(ones_m[:, :], 1.0))
    b.op("dv", lambda e: e.memset(sca[0:1, 6:7], EPS_LN))
    b.op("dv", lambda e: e.memset(sca[0:1, 7:8], EPS_RMS))
    b.op("dv", lambda e: e.memset(ones_r[:, :], 1.0))
    b.op("dv", lambda e: e.memset(onef[:, :], 1.0))
    b.op("dv", lambda e: e.tensor_mul(rows[0:1, 0:64], lamv[0:1, 0:64],
                                      lamv[0:1, 64:128]))
    b.dvf(lambda e: e.tensor_mul(rows[0:1, 64:128], lamv[0:1, 128:192],
                                 lamv[0:1, 192:256]))
    b.op("dv", lambda e: e.reduce_sum(rows[0:1, 128:129], rows[0:1, 0:64],
                                      axis=AXX))
    b.op("dv", lambda e: e.reduce_sum(rows[0:1, 129:130], rows[0:1, 64:128],
                                      axis=AXX), inc="dv")
    b.wait("ac", "dv", b.now("dv"))
    b.op("ac", lambda e: e.activation(rows[0:1, 130:132], rows[0:1, 128:130],
                                      AF.Exp), inc="ac")
    b.wait("dv", "ac", b.now("ac"))
    b.dvf(lambda e: e.tensor_sub(rows[0:1, 132:133], rows[0:1, 130:131],
                                 rows[0:1, 131:132]))
    b.dvf(lambda e: e.tensor_scalar_add(sca[0:1, 4:5], rows[0:1, 132:133],
                                        LAMBDA_INIT))
    b.dvf(lambda e: e.tensor_scalar_mul(sca[0:1, 5:6], sca[0:1, 4:5], -1.0))
    b.dvf(lambda e: e.reciprocal(sca[0:1, 2:3], sca[0:1, 1:2]))
    b.op("dv", lambda e: e.tensor_mul(sca[0:1, 3:4], sca[0:1, 0:1],
                                      sca[0:1, 1:2]), inc="dv")
    b.wait("ac", "dv", b.now("dv"))
    b.op("ac", lambda e: e.activation(lnw_eff[:, :], lnw[:, :], AF.Copy,
                                      scale=1.0 - LAMBDA_INIT))
    b.op("ac", lambda e: e.activation(lnb_eff[:, :], lnb[:, :], AF.Copy,
                                      scale=1.0 - LAMBDA_INIT), inc="ac")
    b.wait("te", "dv", b.now("dv"))
    b.op("te", lambda e: e.matmul(ps5[0][:, 0:1], onef[0:1, :], sca[0:1, 3:4],
                                  start=True, stop=True))
    b.op("te", lambda e: e.matmul(ps5[0][:, 1:2], onef[0:1, :], sca[0:1, 5:6],
                                  start=True, stop=True), inc="te")
    b.wait("ac", "te", b.now("te"))
    b.op("ac", lambda e: e.activation(alpha_c[:, :], ps5[0][:, 0:1], AF.Copy))
    b.op("ac", lambda e: e.activation(lamn_c[:, :], ps5[0][:, 1:2], AF.Copy),
         inc="ac")
    b.setg("c0", "ac")

    b.wait("gp", "sy", b.now("sy"))
    b.op("gp", lambda e: e.reg_load(r_xnb, offs_sb[0:1, 0:1]))
    b.op("gp", lambda e: e.reg_load(r_parb, offs_sb[0:1, 1:2]))
    b.op("gp", lambda e: e.reg_load(r_tmp, offs_sb[0:1, 2:3]))
    b.wait("gp", "dv", b.now("dv"))
    b.dma("gp", bti.ap(), ones_r[0:1, :], "gp")

    def stats_chunk(i, src_bf):
        b.need("ac", "sq1")
        b.op("ac", lambda e, s=src_bf: e.activation(sq1[:, :], s, AF.Square),
             inc="ac")
        b.wait("te", "ac", b.now("ac"))
        for tc in range(2):
            sl = slice(tc * 512, (tc + 1) * 512)
            if i == 0:
                b.need("te", "psS0")
                b.need("te", "psS1")
            b.op("te", lambda e, s=src_bf, sl=sl, i=i:
                 e.matmul(psS[0][:, sl], ones_m[:, :], s[:, sl],
                          start=(i == 0), stop=(i == NCH - 1)))
            b.op("te", lambda e, sl=sl, i=i:
                 e.matmul(psS[1][:, sl], ones_m[:, :], sq1[:, sl],
                          start=(i == 0), stop=(i == NCH - 1)),
                 inc=("te" if (i == NCH - 1 and tc == 1) else None))
        b.setg("sq1", "te")

    def ln_finish(ffn):
        b.wait("ac", "te", b.now("te"))
        for tc in range(2):
            b.op("ac", lambda e, tc=tc: e.activation(
                rows[0:1, tc * 512:(tc + 1) * 512],
                psS[0][0:1, tc * 512:(tc + 1) * 512], AF.Copy, scale=1.0 / D))
            b.op("ac", lambda e, tc=tc: e.activation(
                rows[0:1, 1024 + tc * 512:1024 + (tc + 1) * 512],
                psS[1][0:1, tc * 512:(tc + 1) * 512], AF.Copy, scale=1.0 / D),
                inc="ac")
        b.setg("psS0", "ac")
        b.setg("psS1", "ac")
        b.wait("dv", "ac", b.now("ac"))
        for tc in range(2):
            m_h = rows[0:1, tc * 512:(tc + 1) * 512]
            e_h = rows[0:1, 1024 + tc * 512:1024 + (tc + 1) * 512]
            t_h = rows[0:1, 2048:2560]
            b.dvf(lambda e, m=m_h, t=t_h: e.tensor_mul(t, m, m))
            b.op("dv", lambda e, eh=e_h, t=t_h: e.tensor_sub(eh, eh, t),
                 inc="dv")
            b.wait("dv", "dv")
        b.wait("ac", "dv", b.now("dv"))
        var_r = rows[0:1, 1024:2048]
        b.op("ac", lambda e: e.activation(var_r, var_r, AF.Ln,
                                          bias=sca[0:1, 6:7]))
        b.op("ac", lambda e: e.activation(var_r, var_r, AF.Exp, scale=-0.5),
             inc="ac")
        b.wait("dv", "ac", b.now("ac"))
        if ffn:
            b.dvf(lambda e: e.tensor_scalar(var_r, var_r, sca[0:1, 2:3],
                                            None, ALU.mult))
        b.op("dv", lambda e: e.tensor_copy(rowsb[0:1, 0:1024],
                                           rows[0:1, 0:1024]))
        b.op("dv", lambda e: e.tensor_copy(rowsb[0:1, 1024:2048], var_r),
             inc="dv")
        b.wait("te", "dv", b.now("dv"))
        for tc in range(2):
            b.need("te", f"c{tc}")
            b.need("te", f"c{2 + tc}")
            b.op("te", lambda e, tc=tc: e.matmul(
                ps5[tc][:, :], ones_r[0:1, :],
                rowsb[0:1, tc * 512:(tc + 1) * 512], start=True, stop=True))
            b.op("te", lambda e, tc=tc: e.matmul(
                ps5[2 + tc][:, :], ones_r[0:1, :],
                rowsb[0:1, 1024 + tc * 512:1024 + (tc + 1) * 512],
                start=True, stop=True), inc="te")
        b.wait("ac", "te", b.now("te"))
        b.need("ac", "ab16")
        for tc in range(2):
            b.op("ac", lambda e, tc=tc: e.activation(
                mb_t[:, tc * 512:(tc + 1) * 512], ps5[tc][:, :], AF.Copy))
            b.op("ac", lambda e, tc=tc: e.activation(
                rb_t[:, tc * 512:(tc + 1) * 512], ps5[2 + tc][:, :], AF.Copy),
                inc="ac")
        for i in range(4):
            b.setg(f"c{i}", "ac")
        b.wait("dv", "ac", b.now("ac"))
        for i in range(NCH):
            for tc in range(2):
                s_ = scr32[:, i * TOK + tc * 512:i * TOK + (tc + 1) * 512]
                b.dvf(lambda e, s=s_, tc=tc:
                      e.tensor_sub(s, s, mb_t[:, tc * 512:(tc + 1) * 512]))
                b.op("dv", lambda e, s=s_, tc=tc:
                     e.tensor_mul(s, s, rb_t[:, tc * 512:(tc + 1) * 512]),
                     inc=("dv" if (i == NCH - 1 and tc == 1) else None))
        b.setg("ab16", "dv")

    def first_stats():
        for i in range(NCH):
            xc = xch[i % 2]
            b.need("sy", f"xch{i % 2}")
            b.dma("sy", xc[:, :], x_in.ap()[i * 128:(i + 1) * 128, :], "sy")
            b.wait("dv", "sy", b.now("sy"))
            dst = scr32[:, i * TOK:(i + 1) * TOK]
            b.op("dv", lambda e, s=xc, d=dst: e.tensor_copy(d, s[:, :]),
                 inc="dv")
            b.setg(f"xch{i % 2}", "dv")
            b.wait("ac", "dv", b.now("dv"))
            stats_chunk(i, dst)

    def xn_exchange(which):
        agi = agi1[which]
        cc = "cca1" if which == "a" else "ccb1"
        dstd = agi.ap().rearrange("(i p) c -> p i c", p=128)
        srcd = scr32.ap().rearrange("p (i c) -> p i c", c=TOK)
        b.wait("sy", "dv", b.now("dv"))
        b.dma("sy", dstd, srcd, "sy")
        b.wait("gp", "sy", b.now("sy"))
        xl = xnloc[which]
        b.op("gp", lambda e, xl=xl, agi=agi: e.dma_start(
            out=bass.AP(xl, r_xnb, [[1, D * TOK]]),
            in_=bass.AP(agi, 0, [[1, D * TOK]])), inc="gp", n=16)
        b.wait("gp", "gp", b.now("gp"))
        b.op("gp", lambda e: e.collective_compute(
            "AllGather", ALU.bypass, replica_groups=PAIRS,
            ins=[bti.ap().opt()], outs=[bto.ap().opt()]), inc=cc)
        b.wait("gp", cc, 1)
        b.wait("sy", cc, 1)
        b.need("sy", "xn_full")
        for r in (0, 1):
            dstap = bass.AP(xn_full, r * TOK,
                            [[NCH * S, 128], [S, NCH], [1, TOK]])
            srcap = bass.AP(xl, r * D * TOK,
                            [[TOK, 128], [128 * TOK, NCH], [1, TOK]])
            b.dma("sy", dstap, srcap, "sy")
        b.wait("te", "sy", b.now("sy"))

    def wblock(src_dram, j, slot, rows_n=NCH):
        dst = wpool.ap()[:, slot * 2048: slot * 2048 + rows_n * 128].rearrange(
            "p (i c) -> p i c", c=128)
        src = src_dram.ap()[:, j * 128:(j + 1) * 128].rearrange(
            "(i p) c -> p i c", p=128)
        b.need("gp", f"wslot{slot}")
        b.dma("gp", dst, src, "gp")

    def wtile(slot, i):
        return wpool[:, slot * 2048 + i * 128: slot * 2048 + (i + 1) * 128]

    def wvblock(src_dram, col0):
        dst = wpool.ap().rearrange("p (i c) -> p i c", c=256)
        src = src_dram.ap()[:, col0:col0 + 256].rearrange(
            "(i p) c -> p i c", p=128)
        b.need("gp", "wslot0")
        b.need("gp", "wslot1")
        b.dma("gp", dst, src, "gp")

    def wvtile(i):
        return wpool[:, i * 256:(i + 1) * 256]

    def attn_sub(which, xsrc, xdst, fuse_stats, last):
        is_a = which == "a"
        wlq, wlk, wrv, wout = (wq, wk, wv, wo) if is_a else (wbq, wbk, wbv, wbo)
        scale = SCALE_A if is_a else SCALE_B
        cc2 = "cca2" if is_a else "ccb2"

        xn_exchange(which)

        for g in range(2):
            blocklist = [(kind, jj) for kind in ("k", "q") for jj in range(4)]
            gp_after = {}

            def load_blk(idx):
                kind, jj = blocklist[idx]
                wsrc = {"k": wlk, "q": wlq}[kind]
                wblock(wsrc, 4 * g + jj, idx % 2)
                gp_after[idx] = b.now("gp")

            load_blk(0)
            for bi, (kind, jj) in enumerate(blocklist):
                dbuf = k4 if kind == "k" else q4
                slot = bi % 2
                if bi + 1 < len(blocklist):
                    load_blk(bi + 1)
                b.wait("te", "gp", gp_after[bi])
                for tc in range(4):
                    pj = ps5[tc % 2]
                    b.need("te", f"c{tc % 2}")
                    for i in range(NCH):
                        rh = xn_full[:, i * S + tc * 512:
                                     i * S + (tc + 1) * 512]
                        b.op("te", lambda e, p=pj, l=wtile(slot, i), r=rh,
                             i=i: e.matmul(p[:, :], l, r, start=(i == 0),
                                           stop=(i == NCH - 1)),
                             inc=("te" if i == NCH - 1 else None))
                    dsl = dbuf[:, jj * S + tc * 512:jj * S + (tc + 1) * 512]
                    if is_a:
                        b.wait("ac", "te", b.now("te"))
                        b.need("ac", "qraw")
                        b.op("ac", lambda e, p=pj: e.activation(
                            qraw[:, :], p[:, :], AF.Copy), inc="ac")
                        b.setg(f"c{tc % 2}", "ac")
                        b.wait("te", "ac", b.now("ac"))
                        prot = ps5[2 + tc % 2]
                        b.need("te", f"c{2 + tc % 2}")
                        b.op("te", lambda e, p=prot: e.matmul(
                            p[:, :], rotmb[:, :], qraw[:, :],
                            start=True, stop=True), inc="te")
                        csl = cosb[:, tc * 512:(tc + 1) * 512]
                        ssl = sinb[:, tc * 512:(tc + 1) * 512]
                        b.wait("dv", "te", b.now("te"))
                        b.op("dv", lambda e, d=dsl, c=csl:
                             e.tensor_mul(d, qraw[:, :], c))
                        b.dvf(lambda e, p=prot, s=ssl:
                              e.tensor_mul(w1t[:, :], p[:, :], s))
                        b.op("dv", lambda e, d=dsl:
                             e.tensor_add(d, d, w1t[:, :]), inc="dv")
                        b.setg("qraw", "dv")
                        b.setg(f"c{2 + tc % 2}", "dv")
                    else:
                        b.wait("ac", "te", b.now("te"))
                        b.op("ac", lambda e, p=pj, d=dsl: e.activation(
                            d, p[:, :], AF.Copy), inc="ac")
                        b.setg(f"c{tc % 2}", "ac")
                b.setg(f"wslot{slot}", "te")

            for vb in range(2):
                wvblock(wrv, g * 512 + vb * 256)
                b.wait("te", "gp", b.now("gp"))
                for kc in range(NCH):
                    pj = ps5[kc % 2]
                    b.need("te", f"c{kc % 2}")
                    for i in range(NCH):
                        lt = xn_full[:, i * S + kc * 128:i * S + (kc + 1) * 128]
                        b.op("te", lambda e, p=pj, l=lt, r=wvtile(i), i=i:
                             e.matmul(p[:, 0:256], l, r, start=(i == 0),
                                      stop=(i == NCH - 1)),
                             inc=("te" if i == NCH - 1 else None))
                    dsl = v4[:, kc * 512 + vb * 256:kc * 512 + (vb + 1) * 256]
                    b.wait("ac", "te", b.now("te"))
                    b.op("ac", lambda e, p=pj, d=dsl: e.activation(
                        d, p[:, 0:256], AF.Copy), inc="ac")
                    b.setg(f"c{kc % 2}", "ac")
                b.setg("wslot0", "te")
                b.setg("wslot1", "te")
            if g == 1:
                b.setg("xn_full", "te")

            b.wait("te", "ac", b.now("ac"))
            b.wait("te", "dv", b.now("dv"))
            for jj in range(4):
                head = 4 * g + jj
                for qc in range(4):
                    nkc = 4 * (qc + 1)
                    npair = nkc if is_a else nkc // 2
                    b.need("te", "c0")
                    b.need("te", "c1")
                    b.need("te", "c2")
                    b.need("te", "c3")

                    def emit_Spair(t):
                        sp = psS[t % 2]
                        b.need("te", f"psS{t % 2}")
                        rh = q4[:, jj * S + qc * 512:jj * S + (qc + 1) * 512]
                        if is_a:
                            lt = k4[:, jj * S + t * 128:jj * S + (t + 1) * 128]
                            b.op("te", lambda e, p=sp, l=lt, r=rh: e.matmul(
                                p[:, 0:512], l[0:64, :], r[0:64, :],
                                start=True, stop=True))
                            b.op("te", lambda e, p=sp, l=lt, r=rh: e.matmul(
                                p[:, 512:1024], l[64:128, :], r[64:128, :],
                                start=True, stop=True), inc="te")
                        else:
                            l0 = k4[:, jj * S + 2 * t * 128:
                                    jj * S + (2 * t + 1) * 128]
                            l1 = k4[:, jj * S + (2 * t + 1) * 128:
                                    jj * S + (2 * t + 2) * 128]
                            b.op("te", lambda e, p=sp, l=l0, r=rh: e.matmul(
                                p[:, 0:512], l, r, start=True, stop=True))
                            b.op("te", lambda e, p=sp, l=l1, r=rh: e.matmul(
                                p[:, 512:1024], l, r, start=True, stop=True),
                                inc="te")
                        return b.now("te")

                    s_done = {0: emit_Spair(0)}
                    for t in range(npair):
                        pt = ptile[t % 2]
                        sp = psS[t % 2]
                        if t + 1 < npair:
                            s_done[t + 1] = emit_Spair(t + 1)
                        b.wait("ac", "te", s_done[t])
                        b.need("ac", f"pt{t % 2}")
                        b.op("ac", lambda e, p=sp, t_=pt: e.activation(
                            t_[:, 0:512], p[:, 0:512], AF.Exp, scale=scale))
                        b.op("ac", lambda e, p=sp, t_=pt: e.activation(
                            t_[:, 512:1024], p[:, 512:1024], AF.Exp,
                            scale=scale), inc="ac")
                        b.setg(f"psS{t % 2}", "ac")
                        if is_a:
                            mask_jobs = [(t, (0, 512))]
                        else:
                            mask_jobs = [(2 * t, (0,)), (2 * t + 1, (512,))]
                        did_mask = False
                        for kc, tgt in mask_jobs:
                            m = kc - 4 * qc
                            if m < 0:
                                continue
                            if not did_mask:
                                b.wait("dv", "ac", b.now("ac"))
                            did_mask = True
                            for o2 in tgt:
                                if m > 0:
                                    b.op("dv", lambda e, t_=pt, m=m, o=o2:
                                         e.memset(t_[:, o:o + m * 128], 0.0))
                                b.op("dv", lambda e, t_=pt, m=m, o=o2:
                                     e.tensor_mul(
                                         t_[:, o + m * 128:o + (m + 1) * 128],
                                         t_[:, o + m * 128:o + (m + 1) * 128],
                                         maskt[:, :]), inc="dv")
                        if did_mask:
                            b.wait("te", "dv", b.now("dv"))
                        else:
                            b.wait("te", "ac", b.now("ac"))
                        if is_a:
                            vl = v4[:, t * 512 + jj * 128:
                                    t * 512 + (jj + 1) * 128]
                            b.op("te", lambda e, v=vl, t_=pt, t0=t: e.matmul(
                                ps5[0][:, :], v, t_[:, 0:512],
                                start=(t0 == 0), stop=(t0 == npair - 1)))
                            b.op("te", lambda e, v=vl, t_=pt, t0=t: e.matmul(
                                ps5[1][:, :], v, t_[:, 512:1024],
                                start=(t0 == 0), stop=(t0 == npair - 1)))
                            b.op("te", lambda e, t_=pt, t0=t: e.matmul(
                                ps5[2][:, :], ones_m[:, :], t_[:, 0:512],
                                start=(t0 == 0), stop=(t0 == npair - 1)))
                            b.op("te", lambda e, t_=pt, t0=t: e.matmul(
                                ps5[3][:, :], ones_m[:, :], t_[:, 512:1024],
                                start=(t0 == 0), stop=(t0 == npair - 1)),
                                inc="te")
                        else:
                            for hi, kc in enumerate((2 * t, 2 * t + 1)):
                                vl = v4[:, kc * 512 + jj * 128:
                                        kc * 512 + (jj + 1) * 128]
                                o = hi * 512
                                fst = kc == 0
                                lst = kc == nkc - 1
                                b.op("te", lambda e, v=vl, t_=pt, o=o, f=fst,
                                     l_=lst: e.matmul(
                                         ps5[0][:, :], v, t_[:, o:o + 512],
                                         start=f, stop=l_))
                                b.op("te", lambda e, t_=pt, o=o, f=fst, l_=lst:
                                     e.matmul(
                                         ps5[2][:, :], ones_m[:, :],
                                         t_[:, o:o + 512], start=f, stop=l_),
                                     inc="te")
                        b.setg(f"pt{t % 2}", "te")

                    b.wait("dv", "te", b.now("te"))
                    zsl = scr32[:, head * S + qc * 512:
                                head * S + (qc + 1) * 512]
                    if is_a:
                        b.dvf(lambda e: e.reciprocal(vtmp2[:, :],
                                                     ps5[2][:, :]))
                        b.dvf(lambda e: e.tensor_mul(w1t[:, :], ps5[3][:, :],
                                                     vtmp2[:, :]))
                        b.dvf(lambda e: e.tensor_mul(vtmp2[:, :], ps5[0][:, :],
                                                     w1t[:, :]))
                        b.op("dv", lambda e, z=zsl: e.scalar_tensor_tensor(
                            z, ps5[1][:, :], lamn_c[:, 0:1], vtmp2[:, :],
                            ALU.mult, ALU.add), inc="dv")
                    else:
                        b.dvf(lambda e: e.reciprocal(vtmp2[:, :],
                                                     ps5[2][:, :]))
                        b.op("dv", lambda e, z=zsl: e.tensor_mul(
                            z, ps5[0][:, :], vtmp2[:, :]), inc="dv")
                    for i in range(4):
                        b.setg(f"c{i}", "dv")

                if is_a:
                    b.need("te", "psS0")
                    b.need("te", "psS1")
                    for seg in range(2):
                        hsl = scr32[:, head * S + seg * TOK:
                                    head * S + (seg + 1) * TOK]
                        b.need("ac", "sq1")
                        b.wait("ac", "dv", b.now("dv"))
                        b.op("ac", lambda e, h=hsl: e.activation(
                            sq1[:, :], h, AF.Square), inc="ac")
                        b.wait("te", "ac", b.now("ac"))
                        for sc in range(2):
                            b.op("te", lambda e, sc=sc, seg=seg: e.matmul(
                                psS[seg][:, sc * 512:(sc + 1) * 512],
                                ones_m[:, :], sq1[:, sc * 512:(sc + 1) * 512],
                                start=True, stop=True),
                                 inc=("te" if sc == 1 else None))
                        b.setg("sq1", "te")
                    b.wait("ac", "te", b.now("te"))
                    for sc4 in range(4):
                        b.op("ac", lambda e, sc4=sc4: e.activation(
                            rows[0:1, sc4 * 512:(sc4 + 1) * 512],
                            psS[sc4 // 2][0:1, (sc4 % 2) * 512:
                                          (sc4 % 2) * 512 + 512],
                            AF.Copy, scale=1.0 / 128), inc="ac")
                    b.setg("psS0", "ac")
                    b.setg("psS1", "ac")
                    b.op("ac", lambda e: e.activation(
                        rows[0:1, 0:2048], rows[0:1, 0:2048], AF.Ln,
                        bias=sca[0:1, 7:8]))
                    b.op("ac", lambda e: e.activation(
                        rows[0:1, 0:2048], rows[0:1, 0:2048], AF.Exp,
                        scale=-0.5), inc="ac")
                    b.wait("dv", "ac", b.now("ac"))
                    b.op("dv", lambda e: e.tensor_copy(rowsb[0:1, 0:2048],
                                                       rows[0:1, 0:2048]),
                         inc="dv")
                    b.wait("te", "dv", b.now("dv"))
                    for sc4 in range(4):
                        pb = ps5[sc4 % 2]
                        b.need("te", f"c{sc4 % 2}")
                        b.op("te", lambda e, p=pb, sc4=sc4: e.matmul(
                            p[:, :], ones_r[0:1, :],
                            rowsb[0:1, sc4 * 512:(sc4 + 1) * 512],
                            start=True, stop=True), inc="te")
                        b.wait("dv", "te", b.now("te"))
                        hsl = scr32[:, head * S + sc4 * 512:
                                    head * S + (sc4 + 1) * 512]
                        b.op("dv", lambda e, h=hsl, p=pb:
                             e.tensor_mul(h, h, p[:, :]), inc="dv")
                        b.wait("dv", "dv")
                        b.setg(f"c{sc4 % 2}", "dv")
                    for q2 in range(4):
                        hsl = scr32[:, head * S + q2 * 512:
                                    head * S + (q2 + 1) * 512]
                        b.op("dv", lambda e, h=hsl: e.tensor_scalar(
                            h, h, lnw_eff[:, 0:1], lnb_eff[:, 0:1],
                            ALU.mult, ALU.add), inc="dv")

        tap(f"zbuf{which}", scr32, NCH * TOK)
        agi, shp = agi2[which], shpar[which]
        b.wait("te", "ac", b.now("ac"))
        b.wait("te", "dv", b.now("dv"))
        b.wait("ac", "te", b.now("te"))
        gp_after_wo = {}

        def load_wo(oc):
            wblock(wout, oc, oc % 2, rows_n=8)
            gp_after_wo[oc] = b.now("gp")

        load_wo(0)
        for oc in range(NCH):
            slot = oc % 2
            if oc + 1 < NCH:
                load_wo(oc + 1)
            b.wait("te", "gp", gp_after_wo[oc])
            psl = par_sl[oc % 2]
            b.need("ac", f"par{oc % 2}")
            for tc in range(4):
                pj = ps5[tc % 2]
                b.need("te", f"c{tc % 2}")
                for i in range(8):
                    rh = scr32[:, i * S + tc * 512:i * S + (tc + 1) * 512]
                    b.op("te", lambda e, p=pj, l=wtile(slot, i), r=rh, i=i:
                         e.matmul(p[:, :], l, r, start=(i == 0),
                                  stop=(i == 7)),
                         inc=("te" if i == 7 else None))
                b.wait("ac", "te", b.now("te"))
                b.op("ac", lambda e, p=pj, d=psl[:, tc * 512:(tc + 1) * 512]:
                     e.activation(d, p[:, :], AF.Copy), inc="ac")
                b.setg(f"c{tc % 2}", "ac")
            b.setg(f"wslot{slot}", "te")
            b.wait("sy", "ac", b.now("ac"))
            for hh in (0, 1):
                b.dma("sy",
                      agi.ap()[hh * D + oc * 128:hh * D + (oc + 1) * 128, :],
                      psl[:, hh * TOK:(hh + 1) * TOK], "sy")
            b.setg(f"par{oc % 2}", "sy")
        b.wait("gp", "sy", b.now("sy"))
        if "par" in TAPS and which == "a":
            tpa = nc.dram_tensor("tap_agi2", [2 * D, TOK], BF16,
                                 kind="ExternalOutput")
            tap_t["agi2"] = tpa
            b.dma("gp", tpa.ap(), agi.ap(), "gp")
        for hh in (0, 1):
            reg = r_xnb if hh == 0 else r_parb
            b.op("gp", lambda e, shp=shp, agi=agi, reg=reg, hh=hh:
                 e.dma_start(
                     out=bass.AP(shp, reg, [[1, D * TOK]]),
                     in_=bass.AP(agi, hh * D * TOK, [[1, D * TOK]])),
                 inc="gp", n=16)
        b.wait("gp", "gp", b.now("gp"))
        b.op("gp", lambda e: e.collective_compute(
            "AllGather", ALU.bypass, replica_groups=PAIRS,
            ins=[bti.ap().opt()], outs=[bto.ap().opt()]), inc=cc2)
        b.wait("gp", cc2, 1)
        stream_update(which, xsrc, xdst, fuse_stats, last, shp)
        if "par" in TAPS and which == "a":
            tpp = nc.dram_tensor("tap_parloc", [2 * D, TOK], BF16,
                                 kind="ExternalOutput")
            tap_t["parloc"] = tpp
            b.dma("gp", tpp.ap(), parloc[which].ap(), "gp")

    def stream_update(which, xsrc, xdst, fuse_stats, last, shp):
        pl = parloc[which]
        b.op("gp", lambda e, pl=pl, shp=shp: e.dma_start(
            out=bass.AP(pl, 0, [[1, 2 * D * TOK]]),
            in_=bass.AP(shp, r_tmp, [[1, 2 * D * TOK]])), inc="gp", n=16)
        b.wait("gp", "gp", b.now("gp"))
        gp_po = {}
        sy_xc = {}

        def load_chunk(i):
            pob = po_sl[i % 2]
            b.need("gp", f"po{i % 2}")
            srcap = bass.AP(pl, i * 128 * TOK,
                            [[TOK, 128], [D * TOK, 2], [1, TOK]])
            dstap = pob.rearrange("p (r c) -> p r c", c=TOK)
            b.op("gp", lambda e, d=dstap, s=srcap:
                 e.dma_start(out=d, in_=s), inc="gp", n=16)
            gp_po[i] = b.now("gp")
            xc = xch[i % 2]
            b.need("sy", f"xch{i % 2}")
            b.dma("sy", xc[:, :], xsrc.ap()[i * 128:(i + 1) * 128, :], "sy")
            sy_xc[i] = b.now("sy")

        for i in range(NCH):
            load_chunk(i)
            pob = po_sl[i % 2]
            xc = xch[i % 2]
            b.wait("dv", "gp", gp_po[i])
            b.wait("dv", "sy", sy_xc[i])
            for tc in range(2):
                sl = slice(tc * 512, (tc + 1) * 512)
                sl2 = slice(1024 + tc * 512, 1024 + (tc + 1) * 512)
                b.dvf(lambda e, p=pob, sl=sl, sl2=sl2:
                      e.tensor_add(w1t[:, :], p[:, sl], p[:, sl2]))
                b.op("dv", lambda e, x=xc, sl=sl:
                     e.tensor_add(x[:, sl], x[:, sl], w1t[:, :]), inc="dv")
                b.wait("dv", "dv")
            b.setg(f"po{i % 2}", "dv")
            if fuse_stats:
                dstn = scr32[:, i * TOK:(i + 1) * TOK]
                b.op("dv", lambda e, s=xc, d=dstn: e.tensor_copy(d, s[:, :]),
                     inc="dv")
                b.wait("ac", "dv", b.now("dv"))
                stats_chunk(i, dstn)
            b.wait("sy", "dv", b.now("dv"))
            b.dma("sy", xdst.ap()[i * 128:(i + 1) * 128, :], xc[:, :], "sy")
            b.setg(f"xch{i % 2}", "sy")

    def ffn_sub(xsrc, xdst, fuse_stats, last):
        b.wait("dv", "ac", b.now("ac"))
        for i in range(NCH):
            for tc in range(2):
                s_ = scr32[:, i * TOK + tc * 512:i * TOK + (tc + 1) * 512]
                b.dvf(lambda e, s=s_: e.tensor_scalar(
                    f8t[:, :], s, 240.0, -240.0, ALU.min, ALU.max))
                b.op("dv", lambda e, s=s_: e.tensor_copy(s, f8t[:, :]),
                     inc="dv")
                b.wait("dv", "dv")
        sy_after_w8 = {}

        def load_w8(oc):
            slot_ = oc % 2
            dst8 = w8p.ap()[:, slot_ * 2048:(slot_ + 1) * 2048].rearrange(
                "p (i c) -> p i c", c=128)
            src8 = w8t.ap()[:, oc * 128:(oc + 1) * 128].rearrange(
                "(i p) c -> p i c", p=128)
            b.need("sy", f"w8s{slot_}")
            b.dma("sy", dst8, src8, "sy")
            sy_after_w8[oc] = b.now("sy")

        load_w8(0)
        for oc in range(NCH):
            slot = oc % 2
            if oc + 1 < NCH:
                load_w8(oc + 1)
            b.wait("dv", "sy", sy_after_w8[oc])
            b.need("dv", f"wslot{slot}")
            b.op("dv", lambda e, s=slot: e.tensor_copy(
                wpool[:, s * 2048:(s + 1) * 2048],
                w8p[:, s * 2048:(s + 1) * 2048]), inc="dv")
            b.setg(f"w8s{slot}", "dv")
            b.wait("te", "dv", b.now("dv"))
            xc = xch[oc % 2]
            b.need("sy", f"xch{oc % 2}")
            b.dma("sy", xc[:, :], xsrc.ap()[oc * 128:(oc + 1) * 128, :], "sy")
            for tc in range(2):
                pj = ps5[tc % 2]
                b.need("te", f"c{tc % 2}")
                for i in range(NCH):
                    rh = scr32[:, i * TOK + tc * 512:i * TOK + (tc + 1) * 512]
                    b.op("te", lambda e, p=pj, l=wtile(slot, i), r=rh, i=i:
                         e.matmul(p[:, :], l, r, start=(i == 0),
                                  stop=(i == NCH - 1)),
                         inc=("te" if i == NCH - 1 else None))
            b.setg(f"wslot{slot}", "te")
            b.wait("dv", "te", b.now("te"))
            b.wait("dv", "sy", b.now("sy"))
            for tc in range(2):
                pj = ps5[tc % 2]
                sl = slice(tc * 512, (tc + 1) * 512)
                b.op("dv", lambda e, p=pj, x=xc, sl=sl:
                     e.scalar_tensor_tensor(x[:, sl], p[:, :],
                                            alpha_c[:, 0:1], x[:, sl],
                                            ALU.mult, ALU.add), inc="dv")
                b.setg(f"c{tc % 2}", "dv")
            b.wait("sy", "dv", b.now("dv"))
            b.dma("sy", xdst.ap()[oc * 128:(oc + 1) * 128, :], xc[:, :], "sy")
            b.setg(f"xch{oc % 2}", "sy")
        if fuse_stats:
            b.wait("sy", "sy", b.now("sy"))
            for i in range(NCH):
                xc = xch[i % 2]
                b.need("sy", f"xch{i % 2}")
                b.dma("sy", xc[:, :], xdst.ap()[i * 128:(i + 1) * 128, :],
                      "sy")
                b.wait("dv", "sy", b.now("sy"))
                dstn = scr32[:, i * TOK:(i + 1) * TOK]
                b.op("dv", lambda e, s=xc, d=dstn: e.tensor_copy(d, s[:, :]),
                     inc="dv")
                b.setg(f"xch{i % 2}", "dv")
                b.wait("ac", "dv", b.now("dv"))
                stats_chunk(i, dstn)

    first_stats()
    ln_finish(ffn=False)
    tap("xnw1", scr32, NCH * TOK)

    attn_sub("a", x_in, xbufA, fuse_stats=(n_sub >= 2), last=(n_sub == 1))
    if n_sub >= 2:
        ln_finish(ffn=True)
        ffn_sub(xbufA, xbufB, fuse_stats=(n_sub >= 3), last=(n_sub == 2))
    if n_sub >= 3:
        ln_finish(ffn=False)
        attn_sub("b", xbufB, xbufA, fuse_stats=(n_sub >= 4), last=(n_sub == 3))
    if n_sub >= 4:
        ln_finish(ffn=True)
        ffn_sub(xbufA, out_fm, fuse_stats=False, last=True)

    fin = {1: xbufA, 2: xbufB, 3: xbufA}.get(n_sub)
    if fin is not None and n_sub < 4:
        b.wait("gp", "sy", b.now("sy"))
        b.dma("gp", out_fm.ap(), fin.ap(), "gp")

    b.wait("sy", "sy", b.now("sy"))
    b.wait("sy", "gp", b.now("gp"))

    with nc.Block() as block:
        for eng_name, attr in (("te", "tensor"), ("dv", "vector"),
                               ("ac", "scalar"), ("gp", "gpsimd"),
                               ("sy", "sync")):
            ops = b.q[eng_name]

            def runner(e, ops=ops):
                for f in ops:
                    f(e)
            getattr(block, attr)(runner)

    b.stack.close()
    return nc


_NC_CACHE = {}


def _get_nc():
    key = N_SUB
    if key not in _NC_CACHE:
        _NC_CACHE[key] = build_nc(key)
    return _NC_CACHE[key]


def _rope_tables():
    dd = 64
    inv = (1.0 / np.float32(10000.0) **
           (np.arange(0, dd, 2, dtype=np.float32) / np.float32(dd)))
    t = np.arange(S, dtype=np.float32)
    freqs = (t[:, None] * inv[None, :]).astype(np.float32)
    freqs = np.repeat(freqs, 2, axis=1)
    cos = np.cos(freqs).astype(np.float32)
    sin = np.sin(freqs).astype(np.float32)
    cosf = np.concatenate([cos.T, cos.T], axis=0)
    sinf = np.concatenate([sin.T, sin.T], axis=0)
    return (np.ascontiguousarray(cosf).astype(ml_dtypes.bfloat16),
            np.ascontiguousarray(sinf).astype(ml_dtypes.bfloat16))


def _host_prep(inputs):
    x = np.asarray(inputs["L0_attn_a_x"])
    Wq = np.asarray(inputs["L0_attn_a_Wq"])
    Wk = np.asarray(inputs["L0_attn_a_Wk"])
    Wv = np.asarray(inputs["L0_attn_a_Wv"])
    Wo = np.asarray(inputs["L0_attn_a_Wo"])
    WK = np.asarray(inputs["L0_attn_b_W_K"])
    WQ = np.asarray(inputs["L0_attn_b_W_Q"])
    WV = np.asarray(inputs["L0_attn_b_W_V"])
    WO = np.asarray(inputs["L0_attn_b_W_O"])
    w8 = np.asarray(inputs["L0_ffn_weight"])
    wsc = np.float32(inputs["L0_ffn_weight_scale"])
    isc = np.float32(inputs["L0_ffn_input_scale"])
    lam = np.concatenate([np.asarray(inputs["L0_attn_a_lambda_q1"]),
                          np.asarray(inputs["L0_attn_a_lambda_k1"]),
                          np.asarray(inputs["L0_attn_a_lambda_q2"]),
                          np.asarray(inputs["L0_attn_a_lambda_k2"])]
                         ).astype(np.float32).reshape(1, 256)
    lnw = np.asarray(inputs["L0_attn_a_ln_w"]).astype(np.float32).reshape(128, 1)
    lnb = np.asarray(inputs["L0_attn_a_ln_b"]).astype(np.float32).reshape(128, 1)

    cosb, sinb = _rope_tables()
    rot = np.zeros((128, 128), dtype=np.float32)
    for p in range(0, 128, 2):
        rot[p + 1, p] = -1.0
        rot[p, p + 1] = 1.0
    rot = np.ascontiguousarray(rot).astype(ml_dtypes.bfloat16)
    tri = np.ascontiguousarray(
        np.triu(np.ones((128, 128), dtype=np.float32))).astype(
        ml_dtypes.bfloat16)
    w8t = np.ascontiguousarray(w8.T).view(ml_dtypes.float8_e4m3)

    in_maps = []
    for c in range(8):
        bb, h = c >> 1, c & 1
        tok = slice(1024 * h, 1024 * h + 1024)
        qk = slice(1024 * h, 1024 * h + 1024)
        heads = slice(8 * h, 8 * h + 8)
        x_fm = np.ascontiguousarray(x[bb, tok, :].T)
        wbk_c = np.ascontiguousarray(
            WK[heads].transpose(2, 0, 1).reshape(D, 1024))
        wbq_c = np.ascontiguousarray(
            WQ[heads].transpose(2, 0, 1).reshape(D, 1024))
        wbv_c = np.ascontiguousarray(
            WV[heads].transpose(2, 0, 1).reshape(D, 1024))
        wbo_c = np.ascontiguousarray(WO[:, qk].T)
        offs = np.array([[h * D * TOK,
                          2 * D * TOK + h * D * TOK,
                          h * 2 * D * TOK, 0]],
                        dtype=np.int32)
        in_maps.append({
            "x_in": x_fm,
            "wq": np.ascontiguousarray(Wq[:, qk]),
            "wk": np.ascontiguousarray(Wk[:, qk]),
            "wv": np.ascontiguousarray(Wv[:, qk]),
            "wo": np.ascontiguousarray(Wo[qk, :]),
            "wbk": wbk_c, "wbq": wbq_c, "wbv": wbv_c, "wbo": wbo_c,
            "w8t": w8t,
            "cos_in": cosb, "sin_in": sinb, "rot_in": rot,
            "tri_in": tri,
            "lam_in": lam, "lnw_in": lnw, "lnb_in": lnb,
            "wsc_in": wsc.reshape(1, 1),
            "isc_in": isc.reshape(1, 1),
            "offs_in": offs,
        })
    return in_maps


def kernel(**inputs):
    in_maps = _host_prep(inputs)
    nc = _get_nc()
    res = run_bass_kernel_spmd(nc, in_maps, core_ids=list(range(8)))
    out = np.empty((4, S, D), dtype=np.float32)
    for c in range(8):
        bb, h = c >> 1, c & 1
        out[bb, 1024 * h:1024 * h + 1024, :] = res.results[c]["out_fm"].T
    if TAPS:
        kernel.tap_results = [
            {k: v for k, v in r.items() if k.startswith("tap_")}
            for r in res.results]
    return out
